# version 02
# baseline (speedup 1.0000x reference)
import numpy as np
import ml_dtypes
from contextlib import ExitStack

import concourse.bass as bass
import concourse.mybir as mybir
from concourse.bass_utils import run_bass_kernel_spmd

F32 = mybir.dt.float32
BF16 = mybir.dt.bfloat16
I32 = mybir.dt.int32
AF = mybir.ActivationFunctionType
ALU = mybir.AluOpType
AX = mybir.AxisListType
BF = ml_dtypes.bfloat16

TOK = 1024
D = 4096
CH = D // 128
TT = TOK // 128
NH, DH = 32, 128
NCORE = 8

COMPUTE = ("scalar", "vector", "tensor", "gpsimd")
DMAQ = ("sync", "scalar", "gpsimd")
ALL_ENG = ("sync", "scalar", "vector", "tensor", "gpsimd")


class Prog:

    def __init__(self):
        self.ops = {e: [] for e in ALL_ENG}
        self.ccnt = {e: 0 for e in COMPUTE}
        self.dcnt = {}
        self.water = {e: {} for e in ALL_ENG}
        self.floor = []

    def set_floor(self):
        self.floor = self.barrier_token()

    def _waits(self, e, deps):
        deps = list(deps) + self.floor
        best = {}
        for dep in deps:
            if dep is None:
                continue
            kind, name, n = dep
            val = n if kind == "c" else 16 * n
            key = (kind, name)
            if val > best.get(key, -1):
                best[key] = val
        waits = []
        for (kind, name), val in best.items():
            if self.water[e].get((kind, name), -1) >= val:
                continue
            self.water[e][(kind, name)] = val
            waits.append((kind, name, val))
        return waits

    def c(self, e, emit, deps=()):
        assert e in COMPUTE
        waits = self._waits(e, deps)
        self.ccnt[e] += 1
        n = self.ccnt[e]

        def run(em, env):
            for kind, name, val in waits:
                em.wait_ge(env.csem[name] if kind == "c" else env.dsem[name], val)
            emit(em).then_inc(env.csem[e], 1)

        self.ops[e].append(run)
        return ("c", e, n)

    def d(self, q, out, in_, deps=(), transpose=False, sem=None, **kw):
        assert q in DMAQ
        sem = sem or ("q_" + q)
        waits = self._waits(q, deps)
        self.dcnt[sem] = self.dcnt.get(sem, 0) + 1
        n = self.dcnt[sem]

        def run(em, env):
            for kind, name, val in waits:
                em.wait_ge(env.csem[name] if kind == "c" else env.dsem[name], val)
            o = out() if callable(out) else out
            i = in_() if callable(in_) else in_
            em.dma_start(out=o, in_=i, transpose=transpose, **kw).then_inc(env.dsem[sem], 16)

        self.ops[q].append(run)
        return ("d", sem, n)

    def wait(self, e, deps):
        waits = self._waits(e, deps)
        if not waits:
            return

        def run(em, env):
            for kind, name, val in waits:
                em.wait_ge(env.csem[name] if kind == "c" else env.dsem[name], val)

        self.ops[e].append(run)

    def barrier_token(self):
        toks = [("c", e, self.ccnt[e]) for e in COMPUTE if self.ccnt[e] > 0]
        toks += [("d", s, n) for s, n in self.dcnt.items() if n > 0]
        return toks


class Env:
    def __init__(self, csem, dsem):
        self.csem = csem
        self.dsem = dsem


def build_block(nc, prog, es):
    csem = {e: es.enter_context(nc.semaphore("cs_" + e)) for e in COMPUTE}
    dsem = {s: es.enter_context(nc.semaphore("ds_" + s)) for s in prog.dcnt}
    env = Env(csem, dsem)
    block = es.enter_context(nc.Block())

    def section(name):
        def body(em):
            for run in prog.ops[name]:
                run(em, env)
        return body

    block.sync(section("sync"))
    block.scalar(section("scalar"))
    block.vector(section("vector"))
    block.tensor(section("tensor"))
    block.gpsimd(section("gpsimd"))


class NS(dict):
    __getattr__ = dict.__getitem__
    __setattr__ = dict.__setitem__



def prep_cast_bf16(b, src_f32, dst_bf, tag):
    p = b.prog
    toks = []
    H = D // 2
    for k in range(CH):
        for h in range(2):
            buf = b.wchunk_f32
            obuf = b.wchunk_bf
            prev = b.wprev.get(("wcast",), ())
            ld = p.d("sync", buf[:, 0:H], src_f32[128 * k:128 * (k + 1), H * h:H * (h + 1)], deps=prev, sem="wp_ld")
            ct = p.c("scalar", lambda em, buf=buf, obuf=obuf, h=h: em.activation(
                obuf[:, H * h:H * (h + 1)], buf[:, 0:H], AF.Copy, bias=0.0, scale=1.0), deps=[ld])
            st = p.d("gpsimd", dst_bf[128 * k:128 * (k + 1), H * h:H * (h + 1)],
                     obuf[:, H * h:H * (h + 1)], deps=[ct], sem="wp_st")
            b.wprev[("wcast",)] = [ct, st]
            toks.append(st)
    return toks


def prep_wdq(b):
    p = b.prog
    toks = []
    for k in range(CH):
        wd, lo = b.wd_i32, b.wd_lo
        sc, m8 = b.wd_sc, b.wd_m8
        out = b.wchunk_bf
        prev = list(b.wprev.get(("wdq",), ())) + list(b.wprev.get(("wcast",), ()))
        ld = p.d("sync", wd[:], b.wdata[128 * k:128 * (k + 1), :], deps=prev, sem="wdq_ld")
        ls = p.d("sync", sc[:], b.scales[128 * k:128 * (k + 1), :], deps=prev, sem="wdq_ld")
        tl = p.c("vector", lambda em, lo=lo, wd=wd: em.tensor_single_scalar(lo, wd, 15, ALU.bitwise_and), deps=[ld, ls])
        th = p.c("vector", lambda em, wd=wd: em.tensor_single_scalar(wd, wd, 4, ALU.logical_shift_right), deps=[tl])
        tm = p.c("vector", lambda em, m8=m8, sc=sc: em.tensor_scalar_mul(m8, sc, -8.0), deps=[ld, ls])
        deps0 = [tl, th, tm]
        gt = []
        for g in range(32):
            for off, srct in ((0, lo), (1, wd)):
                def mk(g=g, off=off, srct=srct, sc=sc, m8=m8, out=out):
                    def f(em):
                        return em.tensor_scalar(
                            out[:, 128 * g + off: 128 * (g + 1): 2],
                            srct[:, 64 * g: 64 * (g + 1)],
                            sc[:, g:g + 1], m8[:, g:g + 1], ALU.mult, ALU.add)
                    return f
                gt.append(p.c("vector", mk(), deps=deps0))
                deps0 = ()
        st = p.d("gpsimd", b.wdq_bf[128 * k:128 * (k + 1), :], out[:, 0:D], deps=gt, sem="wdq_st")
        b.wprev[("wdq",)] = [st] + gt[-1:]
        b.wprev[("wcast",)] = [st]
        toks.append(st)
    return toks



def ln_stats(b, xsrc):
    p = b.prog
    done = []
    for t in range(TT):
        xc = b.xchunk
        prev = b.lnprev.get(0, ())
        ld = p.d("sync", xc[:], xsrc[128 * t:128 * (t + 1), :], deps=prev, sem="x_ld")
        s1 = p.c("vector", lambda em, t=t, xc=xc: em.reduce_sum(
            b.ssum[:, t:t + 1], xc[:], axis=AX.X), deps=[ld])
        s2 = p.c("scalar", lambda em, t=t, xc=xc: em.activation(
            b.lnchunk[:, 0:D], xc[:], AF.Square, bias=0.0, scale=1.0,
            accum_out=b.ssq[:, t:t + 1]), deps=[ld])
        b.lnprev[0] = [s1, s2]
        done += [s1, s2]
    t0 = p.c("vector", lambda em: em.tensor_scalar_mul(b.mean[:], b.ssum[:, 0:TT], 1.0 / D), deps=done)
    t1 = p.c("vector", lambda em: em.tensor_scalar_mul(b.var[:], b.ssq[:, 0:TT], 1.0 / D), deps=done)
    t2 = p.c("vector", lambda em: em.tensor_tensor(b.msq[:], b.mean[:], b.mean[:], ALU.mult), deps=[t0])
    t3 = p.c("vector", lambda em: em.tensor_sub(b.var[:], b.var[:], b.msq[:]), deps=[t1, t2])
    t3b = p.c("vector", lambda em: em.tensor_scalar_add(b.var[:], b.var[:], 1e-5), deps=[t3])
    t4 = p.c("scalar", lambda em: em.activation(b.std[:], b.var[:], AF.Sqrt, bias=0.0, scale=1.0), deps=[t3b])
    t5 = p.c("vector", lambda em: em.reciprocal(b.rstd[:], b.std[:]), deps=[t4])
    t6 = p.c("vector", lambda em: em.tensor_tensor(b.nmr[:], b.mean[:], b.rstd[:], ALU.mult), deps=[t5])
    t7 = p.c("vector", lambda em: em.tensor_scalar_mul(b.nmr[:], b.nmr[:], -1.0), deps=[t6])
    return [t5, t7]


def ln_normalize_to_fm(b, xsrc, fm_buf, stat_toks):
    p = b.prog
    wr = []
    for t in range(TT):
        xc = b.xchunk
        lc = b.lnchunk
        prev = b.lnprev.get(0, ())
        ld = p.d("sync", xc[:], xsrc[128 * t:128 * (t + 1), :], deps=prev, sem="x_ld")
        ct = p.c("scalar", lambda em, t=t, xc=xc, lc=lc: em.activation(
            lc[:, 0:D], xc[:], AF.Identity, bias=b.nmr[:, t:t + 1], scale=b.rstd[:, t:t + 1]),
            deps=[ld] + stat_toks)
        st = p.d("gpsimd", b.ln_dram[128 * t:128 * (t + 1), :], lc[:, 0:D], deps=[ct], sem="ln_st")
        b.lnprev[0] = [ct, st]
        wr.append(st)
    tr = []
    for c in range(CH):
        tr.append(p.d("scalar", fm_buf[:, TOK * c:TOK * (c + 1)],
                      b.ln_dram[:, 128 * c:128 * (c + 1)], deps=wr, transpose=True, sem="fm_tr"))
    return tr



def t2_matmul(b, fm_buf, fm_ready, moving_loader, epilogue, n_out=D, og_size=512,
              bias_row=None):
    p = b.prog
    n_og = n_out // og_size
    mv_prev = b.mv_prev
    for og in range(n_og):
        for c in range(CH):
            i = (og * CH + c) % 2
            ld = moving_loader(og, c, b.mvchunk[i][:, 0:og_size], list(mv_prev[i]))
            mm_toks = []
            for t in range(TT):
                deps = [ld] + fm_ready
                if c == 0:
                    deps = deps + b.bank_guard[t]
                def mk(og=og, c=c, t=t, i=i):
                    def f(em):
                        return em.matmul(
                            b.psum[t][:, 0:og_size],
                            fm_buf[:, TOK * c + 128 * t: TOK * c + 128 * (t + 1)],
                            b.mvchunk[i][:, 0:og_size],
                            start=(c == 0), stop=(c == CH - 1 and bias_row is None))
                    return f
                mm_toks.append(p.c("tensor", mk(), deps=deps))
            mv_prev[i] = mm_toks
            if c == CH - 1:
                if bias_row is not None:
                    brow, btoks = bias_row
                    bias_mm = []
                    for t in range(TT):
                        def mkb(og=og, t=t):
                            def f(em):
                                return em.matmul(
                                    b.psum[t][:, 0:og_size],
                                    b.onesbf[0:1, 128 * 0:128 * 1],
                                    brow[0:1, og_size * og:og_size * (og + 1)],
                                    start=False, stop=True)
                            return f
                        bias_mm.append(p.c("tensor", mkb(), deps=[mm_toks[t]] + btoks))
                    mm_toks = bias_mm
                for t in range(TT):
                    b.bank_guard[t] = list(epilogue(t, og, b.psum[t][:, 0:og_size], mm_toks[t]))
    out = []
    for t in range(TT):
        out += b.bank_guard[t]
    return out



def load_bias_row(b, bias_row_dram, deps):
    p = b.prog
    on = p.c("vector", lambda em: em.memset(b.onesbf[0:1, 0:128], 1.0), deps=())
    toks = [on]
    prev = ()
    for g in range(D // 512):
        ld = p.d("sync", b.browf[0:1, 0:512], bias_row_dram[None, 512 * g:512 * (g + 1)],
                 deps=list(deps) + list(prev), sem="bias_ld")
        ct = p.c("scalar", lambda em, g=g: em.activation(
            b.browbf[0:1, 512 * g:512 * (g + 1)], b.browf[0:1, 0:512], AF.Copy,
            bias=0.0, scale=1.0), deps=[ld])
        prev = [ct]
        toks.append(ct)
    return toks



def ffn0_sublayer(b, xsrc, xdst, wprep_toks):
    p = b.prog
    st = ln_stats(b, xsrc)
    fm_ready = ln_normalize_to_fm(b, xsrc, b.fmA, st)

    def mv_had(og, c, buf, prev):
        i = (og * CH + c) % 2
        return p.d("sync", buf, b.had_bf[128 * c:128 * (c + 1), 512 * og:512 * (og + 1)],
                   deps=prev + wprep_toks, sem=f"mv{i}")

    def ep_mm1(t, og, ps, mm_tok):
        ct = p.c("scalar", lambda em, t=t, og=og, ps=ps: em.activation(
            b.tmB[:, D * t + 512 * og: D * t + 512 * (og + 1)], ps, AF.Copy, bias=0.0, scale=1.0),
            deps=[mm_tok])
        return [ct]

    mm1_done = t2_matmul(b, b.fmA, fm_ready, mv_had, ep_mm1)

    xr_wr = []
    for t in range(TT):
        xr_wr.append(p.d("gpsimd", b.xr_dram[128 * t:128 * (t + 1), :],
                         b.tmB[:, D * t:D * (t + 1)], deps=mm1_done, sem="xr_st"))
    xr_fm = []
    for c in range(CH):
        xr_fm.append(p.d("scalar", b.fmA[:, TOK * c:TOK * (c + 1)],
                         b.xr_dram[:, 128 * c:128 * (c + 1)], deps=xr_wr, transpose=True, sem="fm_tr"))

    bb = load_bias_row(b, b.ffn0_bias, wprep_toks)

    def mv_wdq(og, c, buf, prev):
        i = (og * CH + c) % 2
        return p.d("sync", buf, b.wdq_bf[512 * og:512 * (og + 1), 128 * c:128 * (c + 1)],
                   deps=prev + wprep_toks, transpose=True, sem=f"mv{i}")

    ep_state = {}

    def ep_mm2(t, og, ps, mm_tok):
        i = t % 2
        prev = ep_state.get(i, ())
        ld = p.d("gpsimd", b.epi_x[i][:, 0:512],
                 xsrc[128 * t:128 * (t + 1), 512 * og:512 * (og + 1)], deps=prev, sem=f"epix{i}")
        a1 = p.c("vector", lambda em, i=i, ps=ps: em.scalar_tensor_tensor(
            b.epi_y[:, 0:512], ps, 0.5, b.epi_x[i][:, 0:512], ALU.mult, ALU.add),
            deps=[ld, mm_tok] + list(ep_state.get("y", ())))
        stq = p.d("gpsimd", xdst[128 * t:128 * (t + 1), 512 * og:512 * (og + 1)],
                  b.epi_y[:, 0:512], deps=[a1], sem="epi_st")
        ep_state[i] = [ld, a1, stq]
        ep_state["y"] = [stq]
        return [a1, stq]

    done = t2_matmul(b, b.fmA, xr_fm, mv_wdq, ep_mm2, bias_row=(b.browbf, bb))
    return done



def alloc_buffers(b, nc, es):
    sb = lambda name, shape, dt: es.enter_context(nc.sbuf_tensor(name, shape, dt)).ap()
    b.fmA = sb("fmA", [128, CH * TOK], BF16)
    b.tmB = sb("tmB", [128, CH * TOK], BF16)
    b.xchunk = sb("xchunk", [128, D], F32)
    b.lnchunk = sb("lnchunk", [128, D], BF16)
    b.mvchunk = [sb(f"mv{i}", [128, 512], BF16) for i in range(2)]
    b.epi_x = [sb(f"epix{i}", [128, 512], F32) for i in range(2)]
    b.epi_y = sb("epiy", [128, 512], F32)
    b.browf = sb("browf", [1, 512], F32)
    b.browbf = sb("browbf", [1, D], BF16)
    b.onesbf = sb("onesbf", [1, 128], BF16)
    for n in ("ssum", "ssq", "mean", "var", "msq", "std", "rstd", "nmr"):
        b[n] = sb(n, [128, TT], F32)
    b.wchunk_f32 = sb("wcf", [128, D // 2], F32)
    b.wchunk_bf = sb("wcb", [128, D], BF16)
    b.wd_i32 = sb("wdi", [128, 2048], I32)
    b.wd_lo = sb("wdlo", [128, 2048], I32)
    b.wd_sc = sb("wdsc", [128, 32], F32)
    b.wd_m8 = sb("wdm8", [128, 32], F32)
    b.psum = [es.enter_context(nc.psum_tensor(f"pb{i}", [128, 512], F32)).ap() for i in range(8)]
    b.bank_guard = {i: [] for i in range(8)}


def new_builder(nc):
    b = NS()
    b.prog = Prog()
    b.wprev = {}
    b.lnprev = {}
    b.mv_prev = [(), ()]
    b.nc = nc
    return b



def build_launchA():
    nc = bass.Bass("TRN2", target_bir_lowering=False, debug=False)
    es = ExitStack()
    b = new_builder(nc)

    b.xin = nc.dram_tensor("xin", [TOK, D], F32, kind="ExternalInput").ap()
    b.had = nc.dram_tensor("had", [D, D], F32, kind="ExternalInput").ap()
    b.wdata = nc.dram_tensor("wdata", [D, 2048], I32, kind="ExternalInput").ap()
    b.scales = nc.dram_tensor("scales", [D, 32], F32, kind="ExternalInput").ap()
    b.ffn0_bias = nc.dram_tensor("fbias", [D], F32, kind="ExternalInput").ap()
    b.xout = nc.dram_tensor("xout", [TOK, D], F32, kind="ExternalOutput").ap()

    b.had_bf = nc.dram_tensor("had_bf", [D, D], BF16, kind="Internal").ap()
    b.wdq_bf = nc.dram_tensor("wdq_bf", [D, D], BF16, kind="Internal").ap()
    b.ln_dram = nc.dram_tensor("ln_dram", [TOK, D], BF16, kind="Internal").ap()
    b.xr_dram = nc.dram_tensor("xr_dram", [TOK, D], BF16, kind="Internal").ap()

    alloc_buffers(b, nc, es)

    wt = prep_cast_bf16(b, b.had, b.had_bf, "had") + prep_wdq(b)
    ffn0_sublayer(b, b.xin, b.xout, wt)

    b.prog.wait("gpsimd", b.prog.barrier_token())
    build_block(nc, b.prog, es)
    es.close()
    return nc


def run_launchA(x_slices, had, wdata, scales, fbias):
    nc = build_launchA()
    in_maps = [{"xin": x_slices[i], "had": had, "wdata": wdata,
                "scales": scales, "fbias": fbias} for i in range(len(x_slices))]
    res = run_bass_kernel_spmd(nc, in_maps, core_ids=list(range(len(x_slices))))
    return [r["xout"] for r in res.results]





TK = 2048
NKB = TK // 128
INV_SQRT_DH = 0.08838834764831845
C_SHIFT = 40.0
BANDW = 3456

def carve(b, name):
    e0, e1, dt = b.carves[name]
    ap = b.tmB[:, e0:e1]
    return ap.bitcast(F32) if dt == "f32" else ap

def setup_carves(b):
    b.carves = {
        "pTa":   (0, 8192, "bf"),
        "pT":    (8192, 24576, "bf"),
        "kh":    (24576, 26624, "bf"),
        "vh":    (26624, 28688, "bf"),
        "aband": (28688, 32144, "bf"),
        "btmp0": (0, 3456, "bf"),
        "btmp1": (3456, 6912, "bf"),
        "bones": (8192, 11648, "bf"),
        "kstg":  (0, 2048, "bf"),
        "cosb":  (24576, 25600, "bf"),
        "sinb":  (25600, 26624, "bf"),
        "vst0":  (26624, 27140, "bf"),
        "vst1":  (27140, 27656, "bf"),
        "ropeA": (27688, 28200, "f32"),
        "ropeB": (28200, 28712, "f32"),
        "qstg":  (28712, 29224, "bf"),
    }

def broadcast_scalar(b, src_dram11, dst_f32col, bank, deps):
    p = b.prog
    ld = p.d("sync", b.sc11[0:1, 0:1], src_dram11[0:1, 0:1],
             deps=list(deps) + list(b.get("sc11_guard", ())), sem="sc11")
    on = p.c("vector", lambda em: em.memset(b.onesf1[0:1, 0:128], 1.0), deps=())
    mm = p.c("tensor", lambda em, bank=bank: em.matmul(
        b.psum[bank][:, 0:1], b.onesf1[0:1, 0:128], b.sc11[0:1, 0:1],
        start=True, stop=True), deps=[ld, on] + b.bank_guard[bank])
    cp = p.c("vector", lambda em, bank=bank: em.tensor_copy(dst_f32col, b.psum[bank][:, 0:1]), deps=[mm])
    b.bank_guard[bank] = [cp]
    b.sc11_guard = [mm]
    return [cp]

def build_rope_tables(b, deps):
    p = b.prog
    cosb, sinb = carve(b, "cosb"), carve(b, "sinb")
    tA, tB = carve(b, "ropeA"), carve(b, "ropeB")
    th = tA[:, 0:128]
    ang = tA[:, 128:256]
    u = tB[:, 0:128]
    red = tB[:, 128:256]
    gi = b.gint
    pv = broadcast_scalar(b, b.poff, b.poffv[:, 0:1], 7, deps)
    j0 = p.c("gpsimd", lambda em: em.iota(th, pattern=[[1, 128]], base=0, channel_multiplier=0,
                                          allow_small_or_imprecise_dtypes=True), deps=())
    j1 = p.c("vector", lambda em: em.tensor_single_scalar(ang, th, 64.0, ALU.is_ge), deps=[j0])
    j2 = p.c("vector", lambda em: em.scalar_tensor_tensor(th, ang, -64.0, th, ALU.mult, ALU.add), deps=[j1])
    j3 = p.c("scalar", lambda em: em.activation(th, th, AF.Exp, bias=0.0,
                                                scale=-2.0 * float(np.log(10000.0)) / 128.0), deps=[j2])
    prev = [j3]
    for t in range(TT):
        ps = p.c("gpsimd", lambda em, t=t: em.iota(b.posv[:, 0:1], pattern=[[0, 1]], base=128 * t,
                                                   channel_multiplier=1,
                                                   allow_small_or_imprecise_dtypes=True), deps=prev)
        pa = p.c("vector", lambda em: em.tensor_add(b.posv[:, 1:2], b.posv[:, 0:1], b.poffv[:, 0:1]),
                 deps=[ps] + pv)
        for trig in range(2):
            if trig == 0:
                a0 = p.c("vector", lambda em: em.tensor_scalar_mul(ang, th, b.posv[:, 1:2]),
                         deps=[pa, j3] + prev)
            else:
                a0 = p.c("vector", lambda em: em.tensor_scalar(ang, th, b.posv[:, 1:2],
                         float(np.pi / 2), ALU.mult, ALU.add), deps=[pa, j3] + prev)
            u1 = p.c("vector", lambda em: em.tensor_scalar(u, ang, 1.0 / (2 * np.pi), 0.5,
                                                           ALU.mult, ALU.add), deps=[a0])
            i1 = p.c("vector", lambda em: em.tensor_copy(b.kint[:, 0:128], u), deps=[u1])
            g1 = p.c("vector", lambda em: em.tensor_tensor(red, b.kint[:, 0:128], u, ALU.is_gt), deps=[i1])
            k1 = p.c("vector", lambda em: em.tensor_sub(u, b.kint[:, 0:128], red), deps=[g1])
            r1 = p.c("vector", lambda em: em.scalar_tensor_tensor(red, u, -6.28125, ang,
                                                                  ALU.mult, ALU.add), deps=[k1])
            r2 = p.c("vector", lambda em: em.scalar_tensor_tensor(red, u, -0.0019353071795864769, red,
                                                                  ALU.mult, ALU.add), deps=[r1])
            dst = (cosb if trig else sinb)[:, 128 * t:128 * (t + 1)]
            s1 = p.c("scalar", lambda em, dst=dst: em.activation(dst, red, AF.Sin, bias=0.0, scale=1.0),
                     deps=[r2])
            prev = [s1]
    return [prev[0]]

def build_band(b, deps):
    p = b.prog
    band = carve(b, "btmp0")
    tmp = carve(b, "btmp1")
    onesw = carve(b, "bones")
    aband = carve(b, "aband")
    g0 = broadcast_scalar(b, b.g0in, b.gvec[:, 0:1], 6, deps)
    g1 = broadcast_scalar(b, b.g1in, b.gvec[:, 1:2], 6, deps)
    on = p.c("vector", lambda em: em.memset(onesw, 1.0), deps=())
    a0 = p.c("gpsimd", lambda em: em.affine_select(band, onesw, pattern=[[1, BANDW]],
             compare_op=ALU.is_ge, fill=0.0, channel_multiplier=-1, base=-1920), deps=[on])
    m0 = p.c("vector", lambda em: em.tensor_scalar_mul(band, band, b.gvec[:, 0:1]), deps=[a0] + g0)
    a1 = p.c("gpsimd", lambda em: em.affine_select(tmp, onesw, pattern=[[1, BANDW]],
             compare_op=ALU.is_ge, fill=0.0, channel_multiplier=-1, base=-896), deps=[on, m0])
    m1 = p.c("vector", lambda em: em.tensor_scalar_mul(tmp, tmp, b.gvec[:, 1:2]), deps=[a1] + g1)
    m2 = p.c("vector", lambda em: em.tensor_add(band, band, tmp), deps=[m1])
    m3 = p.c("vector", lambda em: em.tensor_scalar(aband, band, 1e30, -1e30, ALU.mult, ALU.add),
             deps=[m2])
    return [m3]

def band_slice(b, t2b, t1b, width):
    uu0 = 128 * (t1b - t2b) + 1920
    return carve(b, "aband")[:, uu0:uu0 + width]


def ep_rope(b, dst_dram, wname):
    p = b.prog
    cosb, sinb = carve(b, "cosb"), carve(b, "sinb")
    tA, tB = carve(b, "ropeA"), carve(b, "ropeB")
    qstg = carve(b, "qstg")
    if "rope_state" not in b:
        b.rope_state = {}
    state = b.rope_state

    def ep(t, og, ps, mm_tok):
        toks = []
        prev = state.get("dma", ())
        cos_t = cosb[:, 128 * t:128 * (t + 1)]
        sin_t = sinb[:, 128 * t:128 * (t + 1)]
        last = ()
        for hh in range(4):
            ph = ps[:, 128 * hh:128 * (hh + 1)]
            tAh, tBh = tA[:, 0:128], tA[:, 128:256]
            c1 = p.c("vector", lambda em, ph=ph, tAh=tAh, cos_t=cos_t: em.tensor_tensor(
                tAh, ph, cos_t, ALU.mult), deps=[mm_tok] + list(prev) + list(last))
            c2 = p.c("vector", lambda em, ph=ph, tBh=tBh, sin_t=sin_t: em.scalar_tensor_tensor(
                tBh[:, 0:64], ph[:, 64:128], -1.0, sin_t[:, 0:64], ALU.mult, ALU.mult), deps=[mm_tok, c1])
            c3 = p.c("vector", lambda em, ph=ph, tBh=tBh, sin_t=sin_t: em.tensor_tensor(
                tBh[:, 64:128], ph[:, 0:64], sin_t[:, 64:128], ALU.mult), deps=[mm_tok, c2])
            c4 = p.c("vector", lambda em, hh=hh, tAh=tAh, tBh=tBh: em.tensor_add(
                qstg[:, 128 * hh:128 * (hh + 1)], tAh, tBh), deps=[c3])
            last = [c4]
            toks.append(c4)
        dm = p.d("gpsimd", dst_dram[128 * t:128 * (t + 1), 512 * og:512 * (og + 1)],
                 qstg[:, 0:512], deps=toks, sem="rope_st")
        state["dma"] = [dm]
        return [toks[0], dm]
    return ep

def ep_vaug(b, dst_dram):
    p = b.prog
    state = {"i": 0}

    def ep(t, og, ps, mm_tok):
        i = state["i"] % 2
        state["i"] += 1
        vst = carve(b, f"vst{i}")
        prev = state.get(("dma", i), ())
        toks = []
        for hh in range(4):
            ct = p.c("scalar", lambda em, hh=hh, ps=ps, vst=vst: em.activation(
                vst[:, 129 * hh:129 * hh + 128], ps[:, 128 * hh:128 * (hh + 1)],
                AF.Copy, bias=0.0, scale=1.0), deps=[mm_tok] + list(prev))
            toks.append(ct)
        dm = p.d("gpsimd", dst_dram[128 * t:128 * (t + 1), 516 * og:516 * (og + 1)],
                 vst[:, 0:516], deps=toks, sem=f"vaug_st{i}")
        state[("dma", i)] = [dm]
        return [toks[0], dm]
    return ep

def attn0_launchB(b, xsrc, wq_bf, wk_bf, wv_bf, wprep):
    p = b.prog
    st = ln_stats(b, xsrc)
    fm_ready = ln_normalize_to_fm(b, xsrc, b.fmA, st)
    rt = build_rope_tables(b, ())
    vst0, vst1 = carve(b, "vst0"), carve(b, "vst1")
    on0 = p.c("vector", lambda em, vst0=vst0: em.memset(vst0[:, 0:516], 1.0), deps=())
    on1 = p.c("vector", lambda em, vst1=vst1: em.memset(vst1[:, 0:516], 1.0), deps=())

    def mv(wbf):
        def f(og, c, buf, prev):
            i = (og * CH + c) % 2
            return p.d("sync", buf, wbf[512 * og:512 * (og + 1), 128 * c:128 * (c + 1)],
                       deps=prev + wprep, transpose=True, sem=f"mv{i}")
        return f

    p.wait("vector", rt)
    t2_matmul(b, b.fmA, fm_ready, mv(wq_bf), ep_rope(b, b.q_dram, "q"))
    t2_matmul(b, b.fmA, fm_ready, mv(wk_bf), ep_rope(b, b.k_dram, "k"))
    t2_matmul(b, b.fmA, fm_ready, mv(wv_bf), ep_vaug(b, b.vaug_dram))
    p.wait("gpsimd", [on0, on1])
    return


def attn0_launchC(b, xsrc, xdst, wo_bf, wprep):
    p = b.prog
    bd = build_band(b, ())
    pTa = carve(b, "pTa")
    pT = carve(b, "pT")
    aband = carve(b, "aband")
    qfm = []
    for c in range(CH):
        qfm.append(p.d("scalar", b.fmA[:, TOK * c:TOK * (c + 1)],
                       b.q_dram[:, 128 * c:128 * (c + 1)], deps=(), transpose=True, sem="fm_tr"))
    kstg = carve(b, "kstg")
    kfm_wr = []
    ks_prev = list(bd)
    for c in range(CH):
        t0 = p.d("scalar", kstg[:, 0:1024], b.kt0_dram[:, 128 * c:128 * (c + 1)],
                 deps=ks_prev, transpose=True, sem="kstg")
        t1 = p.d("scalar", kstg[:, 1024:2048], b.kt1_dram[:, 128 * c:128 * (c + 1)],
                 deps=ks_prev, transpose=True, sem="kstg")
        wr = p.d("gpsimd", b.kfm_dram[128 * c:128 * (c + 1), :], kstg[:, 0:2048],
                 deps=[t0, t1], sem="kstg_wr")
        ks_prev = [wr]
        kfm_wr.append(wr)
    neg40 = b.cm40
    n40 = p.c("vector", lambda em: em.memset(neg40[:, 0:1], -C_SHIFT), deps=())
    kh = carve(b, "kh")
    vh = carve(b, "vh")
    kh_guard = []
    vh_guard = []
    pta_guard = {}
    pt_guard = [list(bd)]
    ao_state = {}
    ao_cnt = [0]
    for h in range(NH):
        kl = p.d("sync", kh[:, 0:2048], b.kfm_dram[128 * h:128 * (h + 1), :],
                 deps=kh_guard + kfm_wr, sem="kh")
        v0 = p.d("sync", vh.rearrange("p (a c) -> p a c", c=129)[:, 0:8, :],
                 b.vaug0_dram.rearrange("(a p) c -> p a c", p=128)[:, :, 129 * h:129 * (h + 1)],
                 deps=vh_guard, sem="vh")
        v1 = p.d("sync", vh.rearrange("p (a c) -> p a c", c=129)[:, 8:16, :],
                 b.vaug1_dram.rearrange("(a p) c -> p a c", p=128)[:, :, 129 * h:129 * (h + 1)],
                 deps=vh_guard, sem="vh")
        kh_guard = []
        vh_guard = []
        cp_toks = []
        for half in range(2):
            ex_toks = []
            for t2q in range(8):
                t2b = 8 * half + t2q
                for t1h in range(2):
                    bank = 4 + (t2q * 2 + t1h) % 4
                    mm = p.c("tensor", lambda em, h=h, t2b=t2b, t1h=t1h, bank=bank: em.matmul(
                        b.psum[bank][:, 0:512], kh[:, 128 * t2b:128 * (t2b + 1)],
                        b.fmA[:, TOK * h + 512 * t1h: TOK * h + 512 * (t1h + 1)],
                        start=True, stop=True),
                        deps=[kl] + qfm + b.bank_guard[bank])
                    md = p.c("vector", lambda em, bank=bank, bs=band_slice(b, t2b, 4 * t1h, 512):
                             em.tensor_add(b.psum[bank][:, 0:512], b.psum[bank][:, 0:512], bs),
                             deps=[mm] + bd)
                    ex = p.c("scalar", lambda em, t2q=t2q, t1h=t1h, bank=bank: em.activation(
                        pTa[:, 1024 * t2q + 512 * t1h: 1024 * t2q + 512 * (t1h + 1)],
                        b.psum[bank][:, 0:512], AF.Exp, bias=neg40[:, 0:1], scale=INV_SQRT_DH),
                        deps=[md, n40] + pta_guard.get((t2q, t1h), []))
                    b.bank_guard[bank] = [ex]
                    ex_toks.append(ex)
            cp = p.d("gpsimd", pT[:, 8192 * half:8192 * (half + 1)], pTa[:, 0:8192],
                     deps=ex_toks + pt_guard[0], sem="ptcp")
            for t2q in range(8):
                for t1h in range(2):
                    pta_guard[(t2q, t1h)] = [cp]
            cp_toks.append(cp)
        av_mms = []
        for pas in range(2):
            for t1bq in range(4):
                t1b = 4 * pas + t1bq
                bank = t1bq
                mms = []
                for t2b in range(NKB):
                    deps = [cp_toks[t2b // 8], v0, v1]
                    if t2b == 0:
                        deps = deps + b.bank_guard[bank]
                    mm = p.c("tensor", lambda em, t1b=t1b, t2b=t2b, bank=bank: em.matmul(
                        b.psum[bank][:, 0:129],
                        pT[:, 1024 * t2b + 128 * t1b: 1024 * t2b + 128 * (t1b + 1)],
                        vh[:, 129 * t2b:129 * (t2b + 1)],
                        start=(t2b == 0), stop=(t2b == NKB - 1)), deps=deps)
                    mms.append(mm)
                av_mms.append(mms[-1])
                i = ao_cnt[0] % 2
                ao_cnt[0] += 1
                prev = ao_state.get(i, ())
                rc0 = p.c("scalar", lambda em, bank=bank: em.activation(
                    b.rcpl[:, 1:2], b.psum[bank][:, 128:129], AF.Ln, bias=0.0, scale=1.0),
                    deps=[mms[-1]] + list(prev) + list(ao_state.get("rc", ())))
                rc = p.c("scalar", lambda em: em.activation(
                    b.rcpl[:, 0:1], b.rcpl[:, 1:2], AF.Exp, bias=0.0, scale=-1.0),
                    deps=[rc0] + list(ao_state.get("sc", ())))
                sc = p.c("vector", lambda em, bank=bank, i=i: em.tensor_scalar_mul(
                    b.aostg[i][:, 0:128], b.psum[bank][:, 0:128], b.rcpl[:, 0:1]), deps=[rc])
                dm = p.d("gpsimd", b.ao_dram[128 * t1b:128 * (t1b + 1), 128 * h:128 * (h + 1)],
                         b.aostg[i][:, 0:128], deps=[sc], sem=f"ao_st{i}")
                b.bank_guard[bank] = [sc]
                ao_state[i] = [dm]
                ao_state["sc"] = [sc]
                ao_state["rc"] = [rc]
                kh_guard.append(mms[-1])
                vh_guard.append(mms[-1])
        pt_guard[0] = av_mms
    if b.get("SKIP_WO"):
        return
    ao_done = p.barrier_token()
    aofm = []
    for c in range(CH):
        aofm.append(p.d("scalar", b.fmA[:, TOK * c:TOK * (c + 1)],
                        b.ao_dram[:, 128 * c:128 * (c + 1)], deps=ao_done, transpose=True, sem="fm_tr"))

    def mv_wo(og, c, buf, prev):
        i = (og * CH + c) % 2
        return p.d("sync", buf, wo_bf[512 * og:512 * (og + 1), 128 * c:128 * (c + 1)],
                   deps=prev + wprep, transpose=True, sem=f"mv{i}")

    ep_state = {}

    def ep_wo(t, og, ps, mm_tok):
        i = t % 2
        prev = ep_state.get(i, ())
        ld = p.d("gpsimd", b.epi_x[i][:, 0:512],
                 xsrc[128 * t:128 * (t + 1), 512 * og:512 * (og + 1)], deps=prev, sem=f"epix{i}")
        a1 = p.c("vector", lambda em, i=i, ps=ps: em.tensor_add(
            b.epi_y[:, 0:512], ps, b.epi_x[i][:, 0:512]),
            deps=[ld, mm_tok] + list(ep_state.get("y", ())))
        stq = p.d("gpsimd", xdst[128 * t:128 * (t + 1), 512 * og:512 * (og + 1)],
                  b.epi_y[:, 0:512], deps=[a1], sem="epi_st")
        ep_state[i] = [ld, a1, stq]
        ep_state["y"] = [stq]
        return [a1, stq]

    t2_matmul(b, b.fmA, aofm, mv_wo, ep_wo)
    return


def alloc_attn_buffers(b, nc, es):
    sb = lambda name, shape, dt: es.enter_context(nc.sbuf_tensor(name, shape, dt)).ap()
    b.sc11 = sb("sc11", [1, 1], F32)
    b.onesf1 = sb("onesf1", [1, 128], F32)
    b.poffv = sb("poffv", [128, 1], F32)
    b.posv = sb("posv", [128, 2], F32)
    b.gvec = sb("gvec", [128, 2], F32)
    b.kint = sb("kint", [128, 128], I32)
    b.gint = None
    b.cm40 = sb("cm40", [128, 1], F32)
    b.rcpl = sb("rcpl", [128, 2], F32)
    b.aostg = [sb(f"aostg{i}", [128, 128], BF16) for i in range(2)]


def build_launchB():
    nc = bass.Bass("TRN2", target_bir_lowering=False, debug=False)
    es = ExitStack()
    b = new_builder(nc)
    b.xin = nc.dram_tensor("xin", [TOK, D], F32, kind="ExternalInput").ap()
    b.wq = nc.dram_tensor("wq", [D, D], F32, kind="ExternalInput").ap()
    b.wk = nc.dram_tensor("wk", [D, D], F32, kind="ExternalInput").ap()
    b.wv = nc.dram_tensor("wv", [D, D], F32, kind="ExternalInput").ap()
    b.poff = nc.dram_tensor("poff", [1, 1], F32, kind="ExternalInput").ap()
    b.q_dram = nc.dram_tensor("q_out", [TOK, D], BF16, kind="ExternalOutput").ap()
    b.k_dram = nc.dram_tensor("k_out", [TOK, D], BF16, kind="ExternalOutput").ap()
    b.vaug_dram = nc.dram_tensor("v_out", [TOK, 4128], BF16, kind="ExternalOutput").ap()
    b.wq_bf = nc.dram_tensor("wq_bf", [D, D], BF16, kind="Internal").ap()
    b.wk_bf = nc.dram_tensor("wk_bf", [D, D], BF16, kind="Internal").ap()
    b.wv_bf = nc.dram_tensor("wv_bf", [D, D], BF16, kind="Internal").ap()
    b.ln_dram = nc.dram_tensor("ln_dram", [TOK, D], BF16, kind="Internal").ap()
    alloc_buffers(b, nc, es)
    alloc_attn_buffers(b, nc, es)
    setup_carves(b)
    wt = (prep_cast_bf16(b, b.wq, b.wq_bf, "wq") + prep_cast_bf16(b, b.wk, b.wk_bf, "wk")
          + prep_cast_bf16(b, b.wv, b.wv_bf, "wv"))
    attn0_launchB(b, b.xin, b.wq_bf, b.wk_bf, b.wv_bf, wt)
    b.prog.wait("gpsimd", b.prog.barrier_token())
    build_block(nc, b.prog, es)
    es.close()
    return nc


def build_launchC():
    nc = bass.Bass("TRN2", target_bir_lowering=False, debug=False)
    es = ExitStack()
    b = new_builder(nc)
    b.xin = nc.dram_tensor("xin", [TOK, D], F32, kind="ExternalInput").ap()
    b.wo = nc.dram_tensor("wo", [D, D], F32, kind="ExternalInput").ap()
    b.q_dram = nc.dram_tensor("q_in", [TOK, D], BF16, kind="ExternalInput").ap()
    b.kt0_dram = nc.dram_tensor("kt0", [TOK, D], BF16, kind="ExternalInput").ap()
    b.kt1_dram = nc.dram_tensor("kt1", [TOK, D], BF16, kind="ExternalInput").ap()
    b.vaug0_dram = nc.dram_tensor("vaug0", [TOK, 4128], BF16, kind="ExternalInput").ap()
    b.vaug1_dram = nc.dram_tensor("vaug1", [TOK, 4128], BF16, kind="ExternalInput").ap()
    b.g0in = nc.dram_tensor("g0in", [1, 1], F32, kind="ExternalInput").ap()
    b.g1in = nc.dram_tensor("g1in", [1, 1], F32, kind="ExternalInput").ap()
    b.xout = nc.dram_tensor("xout", [TOK, D], F32, kind="ExternalOutput").ap()
    b.wo_bf = nc.dram_tensor("wo_bf", [D, D], BF16, kind="Internal").ap()
    b.ao_dram = nc.dram_tensor("ao_dram", [TOK, D], BF16, kind="ExternalOutput").ap()
    b.kfm_dram = nc.dram_tensor("kfm_dram", [D, TK], BF16, kind="Internal").ap()
    alloc_buffers(b, nc, es)
    alloc_attn_buffers(b, nc, es)
    setup_carves(b)
    import os
    b.HEAD_BARRIER = bool(int(os.environ.get("KERN_HEAD_BARRIER", "0")))
    b.STAGE_BARRIER = bool(int(os.environ.get("KERN_STAGE_BARRIER", "0")))
    b.AV_ALLGATES = bool(int(os.environ.get("KERN_AV_ALLGATES", "0")))
    b.SKIP_WO = bool(int(os.environ.get("KERN_SKIP_WO", "0")))
    b.PT_FULLGUARD = bool(int(os.environ.get("KERN_PT_FULLGUARD", "0")))
    b.KH_SERIAL = bool(int(os.environ.get("KERN_KH_SERIAL", "0")))
    b.GATE_POOL = bool(int(os.environ.get("KERN_GATE_POOL", "0")))
    b.SKIP_PREP = b.SKIP_WO
    wt = [] if b.SKIP_PREP else prep_cast_bf16(b, b.wo, b.wo_bf, "wo")
    attn0_launchC(b, b.xin, b.xout, b.wo_bf, wt)
    b.prog.wait("gpsimd", b.prog.barrier_token())
    build_block(nc, b.prog, es)
    es.close()
    return nc


def run_attn0(x_slices, wq, wk, wv, wo):
    n = len(x_slices)
    ncB = build_launchB()
    in_mapsB = []
    for i in range(n):
        poff = np.array([[1024.0 * (i % 2)]], dtype=np.float32)
        in_mapsB.append({"xin": x_slices[i], "wq": wq, "wk": wk, "wv": wv, "poff": poff})
    resB = run_bass_kernel_spmd(ncB, in_mapsB, core_ids=list(range(n)))
    ncC = build_launchC()
    in_mapsC = []
    for i in range(n):
        pair = i // 2
        e, o = 2 * pair, 2 * pair + 1
        g0 = np.array([[1.0 if i % 2 == 0 else 0.0]], dtype=np.float32)
        g1 = np.array([[0.0 if i % 2 == 0 else 1.0]], dtype=np.float32)
        in_mapsC.append({
            "xin": x_slices[i], "wo": wo,
            "q_in": resB.results[i]["q_out"],
            "kt0": resB.results[e]["k_out"], "kt1": resB.results[o]["k_out"],
            "vaug0": resB.results[e]["v_out"], "vaug1": resB.results[o]["v_out"],
            "g0in": g0, "g1in": g1})
    resC = run_bass_kernel_spmd(ncC, in_mapsC, core_ids=list(range(n)))
    return [r["xout"] for r in resC.results]





def e4m3_emulate(b, p, y_f32, out_bf, bufs, deps):
    iview = y_f32.bitcast(I32)
    e, fA, fB = bufs["i0"], bufs["f0"], bufs["f1"]
    t1 = p.c("vector", lambda em: em.tensor_single_scalar(e, iview, 0x7F800000, ALU.bitwise_and), deps=deps)
    t2 = p.c("vector", lambda em: em.tensor_scalar_max(e, e, 1015021568), deps=[t1])
    t3 = p.c("vector", lambda em: em.tensor_scalar_add(e, e, -25165824), deps=[t2])
    step = e.bitcast(F32)
    t4 = p.c("vector", lambda em: em.reciprocal(fA, step), deps=[t3])
    t5 = p.c("vector", lambda em: em.tensor_mul(fB, y_f32, fA), deps=[t4])
    t6 = p.c("vector", lambda em: em.tensor_scalar_add(fB, fB, 0.5), deps=[t5])
    I = fA.bitcast(I32)
    t7 = p.c("vector", lambda em: em.tensor_copy(I, fB), deps=[t6])
    g = y_f32
    t8 = p.c("vector", lambda em: em.tensor_tensor(g, I, fB, ALU.is_gt), deps=[t7])
    t9 = p.c("vector", lambda em: em.tensor_sub(fB, I, g), deps=[t8])
    t10 = p.c("vector", lambda em: em.tensor_mul(out_bf, fB, step), deps=[t9])
    return [t10]


def cross_partition_max(b, p, col_ap, out11, deps, tag):
    t = p.d("gpsimd", b.xrow[0:1, 0:128], col_ap,
            deps=list(deps) + list(b.get("xrow_prev", ())), sem="xpm_" + tag,
            allow_slow_non_contiguous=True)
    r = p.c("vector", lambda em: em.reduce_max(out11, b.xrow[0:1, 0:128], axis=AX.X), deps=[t])
    b.xrow_prev = [r]
    return [r]


def prep_w8(b, wsrc_f32, dst_bf):
    p = b.prog
    H = D // 2
    mx = b.wd_m8[:, 0:1]
    z = p.c("vector", lambda em: em.memset(mx, 0.0), deps=())
    run = [z]
    for k in range(CH):
        for h in range(2):
            buf = b.wchunk_f32
            prev = b.wprev.get(("wcast",), ())
            ld = p.d("sync", buf[:, 0:H], wsrc_f32[128 * k:128 * (k + 1), H * h:H * (h + 1)],
                     deps=list(prev) + run, sem="wp_ld")
            r1 = p.c("vector", lambda em, buf=buf: em.reduce_max(
                b.wd_sc[:, 0:1], buf[:, 0:H], axis=AX.X, apply_absolute_value=True), deps=[ld])
            r2 = p.c("vector", lambda em: em.tensor_max(mx, mx, b.wd_sc[:, 0:1]), deps=[r1])
            run = [r2]
            b.wprev[("wcast",)] = [r1]
    cm = cross_partition_max(b, p, mx, b.s11[0:1, 0:1], run, "w8")
    c1 = p.c("vector", lambda em: em.tensor_scalar_max(b.s11[0:1, 0:1], b.s11[0:1, 0:1], 1e-12), deps=cm)
    c2 = p.c("vector", lambda em: em.reciprocal(b.s11[0:1, 1:2], b.s11[0:1, 0:1]), deps=[c1])
    c3 = p.c("vector", lambda em: em.tensor_scalar_mul(b.s11[0:1, 1:2], b.s11[0:1, 1:2], 448.0), deps=[c2])
    toks = []
    bufs = {"i0": b.wd_i32[:, 0:H], "f0": b.wd_lo.bitcast(F32)[:, 0:H],
            "f1": b.wchunk_bf.bitcast(F32)[:, 0:H]}
    yh = b.lnchunk.bitcast(F32)[:, 0:H]
    for k in range(CH):
        for h in range(2):
            prev = b.wprev.get(("wcast",), ())
            ld = p.d("sync", b.wchunk_f32[:, 0:H], wsrc_f32[128 * k:128 * (k + 1), H * h:H * (h + 1)],
                     deps=list(prev) + [c3] + toks[-1:], sem="wp_ld")
            if k == 0 and h == 0:
                bc = broadcast_scalar_sb(b, b.s11[0:1, 1:2], b.s448w[:, 0:1], 7, [c3])
                b.wprev[("s448w",)] = bc
            y1 = p.c("vector", lambda em: em.tensor_scalar_mul(
                yh, b.wchunk_f32[:, 0:H], b.s448w[:, 0:1]), deps=[ld] + b.wprev[("s448w",)])
            qt = e4m3_emulate(b, p, yh, b.scrapH, bufs, [y1])
            st = p.d("gpsimd", dst_bf[128 * k:128 * (k + 1), H * h:H * (h + 1)],
                     b.scrapH, deps=qt, sem="wp_st")
            b.wprev[("wcast",)] = [st] + qt
            b.lnprev[0] = list(qt) + [st]
            toks.append(st)
    return toks


def broadcast_scalar_sb(b, src11_sb, dst_col, bank, deps):
    p = b.prog
    on = b.get("onesf1_tok")
    if on is None:
        on = [p.c("vector", lambda em: em.memset(b.onesf1[0:1, 0:128], 1.0), deps=())]
        b.onesf1_tok = on
    mm = p.c("tensor", lambda em, bank=bank: em.matmul(
        b.psum[bank][:, 0:1], b.onesf1[0:1, 0:128], src11_sb,
        start=True, stop=True), deps=list(deps) + on + b.bank_guard[bank])
    cp = p.c("vector", lambda em, bank=bank: em.tensor_copy(dst_col, b.psum[bank][:, 0:1]), deps=[mm])
    b.bank_guard[bank] = [cp]
    return [cp]


def ln_absmax(b, xsrc, stat_toks):
    p = b.prog
    H = D // 2
    mx = b.wd_m8[:, 1:2]
    yh = b.lnchunk.bitcast(F32)[:, 0:H]
    z = p.c("vector", lambda em: em.memset(mx, 0.0), deps=())
    run = [z]
    for t in range(TT):
        for h in range(2):
            xc = b.xchunk
            prev = b.lnprev.get(0, ())
            ld = p.d("sync", xc[:, 0:H], xsrc[128 * t:128 * (t + 1), H * h:H * (h + 1)],
                     deps=prev, sem="x_ld")
            ct = p.c("scalar", lambda em, t=t: em.activation(
                yh, xc[:, 0:H], AF.Identity, bias=b.nmr[:, t:t + 1], scale=b.rstd[:, t:t + 1]),
                deps=[ld] + stat_toks + run)
            r1 = p.c("vector", lambda em: em.reduce_max(
                b.wd_sc[:, 1:2], yh, axis=AX.X, apply_absolute_value=True), deps=[ct])
            r2 = p.c("vector", lambda em: em.tensor_max(mx, mx, b.wd_sc[:, 1:2]), deps=[r1])
            run = [r2]
            b.lnprev[0] = [ct, r1]
    return cross_partition_max(b, p, mx, b.s11[0:1, 2:3], run, "lnx")


def l1ffn_main(b, xsrc, xdst, amax8_in, w8_toks, st=None, amax_w=8, amax_deps=()):
    p = b.prog
    if st is None:
        st = ln_stats(b, xsrc)
    la = p.d("sync", b.xrow[0:1, 0:amax_w], amax8_in[0:1, 0:amax_w],
             deps=list(amax_deps) + list(b.get("xrow_prev", ())), sem="amax8")
    a1 = p.c("vector", lambda em, amax_w=amax_w: em.reduce_max(
        b.s11[0:1, 2:3], b.xrow[0:1, 0:amax_w], axis=AX.X), deps=[la])
    b.xrow_prev = [a1]
    a2 = p.c("vector", lambda em: em.tensor_scalar_max(b.s11[0:1, 2:3], b.s11[0:1, 2:3], 1e-12), deps=[a1])
    a3 = p.c("vector", lambda em: em.reciprocal(b.s11[0:1, 3:4], b.s11[0:1, 2:3]), deps=[a2])
    a4 = p.c("vector", lambda em: em.tensor_scalar_mul(b.s11[0:1, 3:4], b.s11[0:1, 3:4], 448.0), deps=[a3])
    bcx = broadcast_scalar_sb(b, b.s11[0:1, 3:4], b.s448x[:, 0:1], 7, [a4])
    s1 = p.c("vector", lambda em: em.tensor_scalar_mul(b.rs2[:, 0:TT], b.rstd[:, 0:TT], b.s448x[:, 0:1]),
             deps=st + bcx)
    s2 = p.c("vector", lambda em: em.tensor_scalar_mul(b.nb2[:, 0:TT], b.nmr[:, 0:TT], b.s448x[:, 0:1]),
             deps=st + bcx)
    H = D // 2
    bufs = {"i0": b.wd_i32[:, 0:H], "f0": b.wd_lo.bitcast(F32)[:, 0:H],
            "f1": b.wchunk_bf.bitcast(F32)[:, 0:H]}
    yh = b.lnchunk.bitcast(F32)[:, 0:H]
    wr = []
    for t in range(TT):
        for h in range(2):
            xc = b.xchunk
            prev = b.lnprev.get(0, ())
            ld = p.d("sync", xc[:, 0:H], xsrc[128 * t:128 * (t + 1), H * h:H * (h + 1)],
                     deps=prev, sem="x_ld")
            y1 = p.c("scalar", lambda em, t=t: em.activation(
                yh, xc[:, 0:H], AF.Identity, bias=b.nb2[:, t:t + 1], scale=b.rs2[:, t:t + 1]),
                deps=[ld, s1, s2])
            qt = e4m3_emulate(b, p, yh, b.scrapH, bufs, [y1])
            stq = p.d("gpsimd", b.ln_dram[128 * t:128 * (t + 1), H * h:H * (h + 1)],
                      b.scrapH, deps=qt, sem="ln_st")
            b.lnprev[0] = [y1] + qt + [stq]
            wr.append(stq)
    fm = []
    for c in range(CH):
        fm.append(p.d("scalar", b.fmA[:, TOK * c:TOK * (c + 1)],
                      b.ln_dram[:, 128 * c:128 * (c + 1)], deps=wr, transpose=True, sem="fm_tr"))
    cw1 = p.c("vector", lambda em: em.tensor_mul(b.s11[0:1, 4:5], b.s11[0:1, 3:4], b.s11[0:1, 1:2]),
              deps=[a4] + w8_toks)
    sv1 = p.c("vector", lambda em: em.reciprocal(b.s11[0:1, 5:6], b.s11[0:1, 4:5]), deps=[cw1])
    sv2 = p.c("vector", lambda em: em.tensor_scalar_mul(b.s11[0:1, 5:6], b.s11[0:1, 5:6], 0.5), deps=[sv1])
    hs = broadcast_scalar_sb(b, b.s11[0:1, 5:6], b.halfS[:, 0:1], 6, [sv2])
    bb = load_bias_row_scaled(b, b.ffn1_bias, [cw1])
    def mv_w8(og, c, buf, prev):
        i = (og * CH + c) % 2
        return p.d("sync", buf, b.w8_bf[512 * og:512 * (og + 1), 128 * c:128 * (c + 1)],
                   deps=prev + w8_toks, transpose=True, sem=f"mv{i}")
    ep_state = {}
    def ep(t, og, ps, mm_tok):
        i = t % 2
        prev = ep_state.get(i, ())
        ld = p.d("gpsimd", b.epi_x[i][:, 0:512],
                 xsrc[128 * t:128 * (t + 1), 512 * og:512 * (og + 1)], deps=prev, sem=f"epix{i}")
        a1_ = p.c("vector", lambda em, i=i, ps=ps: em.scalar_tensor_tensor(
            b.epi_y[:, 0:512], ps, b.halfS[:, 0:1], b.epi_x[i][:, 0:512], ALU.mult, ALU.add),
            deps=[ld, mm_tok] + hs + list(ep_state.get("y", ())))
        stq = p.d("gpsimd", xdst[128 * t:128 * (t + 1), 512 * og:512 * (og + 1)],
                  b.epi_y[:, 0:512], deps=[a1_], sem="epi_st")
        ep_state[i] = [ld, a1_, stq]
        ep_state["y"] = [stq]
        return [a1_, stq]
    t2_matmul(b, b.fmA, fm, mv_w8, ep, bias_row=(b.browbf, bb))
    return


def load_bias_row_scaled(b, bias_row_dram, scale_tok):
    p = b.prog
    scale_tok = list(scale_tok) + list(b.get("zero11_tok", ()))
    on = b.get("onesbf_tok")
    if on is None:
        on = [p.c("vector", lambda em: em.memset(b.onesbf[0:1, 0:128], 1.0), deps=())]
        b.onesbf_tok = on
    toks = list(on)
    prev = ()
    for g in range(D // 512):
        ld = p.d("sync", b.browf[0:1, 0:512], bias_row_dram[None, 512 * g:512 * (g + 1)],
                 deps=list(prev), sem="bias_ld")
        ct = p.c("scalar", lambda em, g=g: em.activation(
            b.browbf[0:1, 512 * g:512 * (g + 1)], b.browf[0:1, 0:512], AF.Identity,
            bias=b.zero11[0:1, 0:1], scale=b.s11[0:1, 4:5]), deps=[ld] + list(scale_tok))
        prev = [ct]
        toks.append(ct)
    return toks


def alloc_l1_buffers(b, nc, es):
    sb = lambda name, shape, dt: es.enter_context(nc.sbuf_tensor(name, shape, dt)).ap()
    if "onesf1" not in b:
        b.onesf1 = sb("onesf1", [1, 128], F32)
    b.s11 = sb("s11", [1, 8], F32)
    b.xrow = sb("xrow", [1, 128], F32)
    b.s448w = sb("s448w", [128, 1], F32)
    b.s448x = sb("s448x", [128, 1], F32)
    b.halfS = sb("halfS", [128, 1], F32)
    b.rs2 = sb("rs2", [128, TT], F32)
    b.nb2 = sb("nb2", [128, TT], F32)
    b.zero11 = sb("zero11", [1, 1], F32)
    b.scrapH = b.tmB[:, 0:D // 2]


def build_launchD1():
    nc = bass.Bass("TRN2", target_bir_lowering=False, debug=False)
    es = ExitStack()
    b = new_builder(nc)
    b.xin = nc.dram_tensor("xin", [TOK, D], F32, kind="ExternalInput").ap()
    b.amax_out = nc.dram_tensor("amax_out", [1, 1], F32, kind="ExternalOutput").ap()
    alloc_buffers(b, nc, es)
    alloc_l1_buffers(b, nc, es)
    st = ln_stats(b, b.xin)
    am = ln_absmax(b, b.xin, st)
    p = b.prog
    p.d("gpsimd", b.amax_out[0:1, 0:1], b.s11[0:1, 2:3], deps=am, sem="amx_st")
    p.wait("gpsimd", p.barrier_token())
    build_block(nc, b.prog, es)
    es.close()
    return nc


def build_launchD2():
    nc = bass.Bass("TRN2", target_bir_lowering=False, debug=False)
    es = ExitStack()
    b = new_builder(nc)
    b.xin = nc.dram_tensor("xin", [TOK, D], F32, kind="ExternalInput").ap()
    b.wffn = nc.dram_tensor("wffn", [D, D], F32, kind="ExternalInput").ap()
    b.ffn1_bias = nc.dram_tensor("fbias1", [D], F32, kind="ExternalInput").ap()
    b.amax8 = nc.dram_tensor("amax8", [1, 8], F32, kind="ExternalInput").ap()
    b.xout = nc.dram_tensor("xout", [TOK, D], F32, kind="ExternalOutput").ap()
    b.w8_bf = nc.dram_tensor("w8_bf", [D, D], BF16, kind="Internal").ap()
    b.ln_dram = nc.dram_tensor("ln_dram", [TOK, D], BF16, kind="Internal").ap()
    alloc_buffers(b, nc, es)
    alloc_l1_buffers(b, nc, es)
    p = b.prog
    z = p.c("vector", lambda em: em.memset(b.zero11[0:1, 0:1], 0.0), deps=())
    b.zero11_tok = [z]
    wt = prep_w8(b, b.wffn, b.w8_bf)
    l1ffn_main(b, b.xin, b.xout, b.amax8, wt + [z])
    p.wait("gpsimd", p.barrier_token())
    build_block(nc, b.prog, es)
    es.close()
    return nc


def run_l1ffn(x_slices, weight, bias):
    n = len(x_slices)
    nc1 = build_launchD1()
    res1 = run_bass_kernel_spmd(nc1, [{"xin": x_slices[i]} for i in range(n)], core_ids=list(range(n)))
    amax8 = np.zeros((1, 8), np.float32)
    for i in range(n):
        amax8[0, i] = res1.results[i]["amax_out"][0, 0]
    nc2 = build_launchD2()
    im = [{"xin": x_slices[i], "wffn": weight, "fbias1": bias, "amax8": amax8} for i in range(n)]
    res2 = run_bass_kernel_spmd(nc2, im, core_ids=list(range(n)))
    return [r["xout"] for r in res2.results]





LN127 = 4.844187086458591

def prep_w4(b, wsrc_f32, dst_bf, tag):
    p = b.prog
    H = D // 2
    toks = []
    for k in range(CH):
        prev = list(b.wprev.get(("wcast",), ()))
        ld = p.d("sync", b.xchunk[:, 0:D], wsrc_f32[128 * k:128 * (k + 1), :],
                 deps=prev + list(b.wprev.get(("w4x",), ())), sem="wp_ld")
        r1 = p.c("vector", lambda em: em.reduce_max(
            b.wd_sc[:, 2:3], b.xchunk[:, 0:D], axis=AX.X, apply_absolute_value=True), deps=[ld])
        s1 = p.c("vector", lambda em: em.tensor_scalar(
            b.wd_sc[:, 3:4], b.wd_sc[:, 2:3], 1.0 / 7.0, 1e-8, ALU.mult, ALU.max), deps=[r1])
        s2 = p.c("vector", lambda em: em.reciprocal(b.wd_sc[:, 4:5], b.wd_sc[:, 3:4]), deps=[s1])
        s3 = p.c("vector", lambda em: em.tensor_scalar_mul(b.wd_sc[:, 5:6], b.wd_sc[:, 3:4], -16.0), deps=[s1])
        hts = []
        for h in range(2):
            u = b.lnchunk.bitcast(F32)[:, 0:H]
            I = b.wd_i32[:, 0:H]
            g = b.wd_lo.bitcast(F32)[:, 0:H]
            u1 = p.c("vector", lambda em, h=h: em.tensor_scalar(
                u, b.xchunk[:, H * h:H * (h + 1)], b.wd_sc[:, 4:5], 16.5, ALU.mult, ALU.add),
                deps=[s2] + list(b.wprev.get(("w4q",), ())))
            i1 = p.c("vector", lambda em: em.tensor_copy(I, u), deps=[u1])
            g1 = p.c("vector", lambda em: em.tensor_tensor(g, I, u, ALU.is_gt), deps=[i1])
            f1 = p.c("vector", lambda em: em.tensor_sub(u, I, g), deps=[g1])
            o1 = p.c("vector", lambda em, h=h: em.tensor_scalar(
                b.scrapH, u, b.wd_sc[:, 3:4], b.wd_sc[:, 5:6], ALU.mult, ALU.add), deps=[f1])
            st = p.d("gpsimd", dst_bf[128 * k:128 * (k + 1), H * h:H * (h + 1)],
                     b.scrapH, deps=[o1], sem="wp_st")
            b.wprev[("w4q",)] = [o1, st]
            hts.append(st)
        b.wprev[("w4x",)] = [u1]
        b.lnprev[0] = list(b.lnprev.get(0, ()))[-2:] + [u1, r1]
        b.lnprev["sq"] = [o1, st]
        toks += hts
    return toks


def ln_stats_minmax(b, xsrc):
    p = b.prog
    done = []
    for t in range(TT):
        xc = b.xchunk
        prev = b.lnprev.get(0, ())
        ld = p.d("sync", xc[:], xsrc[128 * t:128 * (t + 1), :], deps=prev, sem="x_ld")
        s1 = p.c("vector", lambda em, t=t, xc=xc: em.reduce_sum(
            b.ssum[:, t:t + 1], xc[:], axis=AX.X), deps=[ld])
        s2 = p.c("scalar", lambda em, t=t, xc=xc: em.activation(
            b.lnchunk[:, 0:D], xc[:], AF.Square, bias=0.0, scale=1.0,
            accum_out=b.ssq[:, t:t + 1]), deps=[ld] + list(b.lnprev.get("sq", ())))
        s3 = p.c("vector", lambda em, t=t, xc=xc: em.reduce_max(
            b.rmax[:, t:t + 1], xc[:], axis=AX.X), deps=[ld])
        s4 = p.c("vector", lambda em, t=t, xc=xc: em.tensor_reduce(
            b.rmin[:, t:t + 1], xc[:], axis=AX.X, op=ALU.min), deps=[ld])
        b.lnprev[0] = [s1, s2, s3, s4]
        b.lnprev["sq"] = [s2]
        done += [s1, s2, s3, s4]
    t0 = p.c("vector", lambda em: em.tensor_scalar_mul(b.mean[:], b.ssum[:, 0:TT], 1.0 / D), deps=done)
    t1 = p.c("vector", lambda em: em.tensor_scalar_mul(b.var[:], b.ssq[:, 0:TT], 1.0 / D), deps=done)
    t2 = p.c("vector", lambda em: em.tensor_tensor(b.msq[:], b.mean[:], b.mean[:], ALU.mult), deps=[t0])
    t3 = p.c("vector", lambda em: em.tensor_sub(b.var[:], b.var[:], b.msq[:]), deps=[t1, t2])
    t3b = p.c("vector", lambda em: em.tensor_scalar_add(b.var[:], b.var[:], 1e-5), deps=[t3])
    t4 = p.c("scalar", lambda em: em.activation(b.std[:], b.var[:], AF.Sqrt, bias=0.0, scale=1.0), deps=[t3b])
    t5 = p.c("vector", lambda em: em.reciprocal(b.rstd[:], b.std[:]), deps=[t4])
    t6 = p.c("vector", lambda em: em.tensor_tensor(b.nmr[:], b.mean[:], b.rstd[:], ALU.mult), deps=[t5])
    t7 = p.c("vector", lambda em: em.tensor_scalar_mul(b.nmr[:], b.nmr[:], -1.0), deps=[t6])
    m1 = p.c("vector", lambda em: em.tensor_sub(b.rmax[:, 0:TT], b.rmax[:, 0:TT], b.mean[:]), deps=[t0] + done)
    m2 = p.c("vector", lambda em: em.tensor_sub(b.rmin[:, 0:TT], b.mean[:], b.rmin[:, 0:TT]), deps=[m1])
    m3 = p.c("vector", lambda em: em.tensor_max(b.rmax[:, 0:TT], b.rmax[:, 0:TT], b.rmin[:, 0:TT]), deps=[m2])
    m4 = p.c("vector", lambda em: em.tensor_mul(b.sx[:, 0:TT], b.rmax[:, 0:TT], b.rstd[:]), deps=[m3, t5])
    m5 = p.c("vector", lambda em: em.tensor_scalar(b.sx[:, 0:TT], b.sx[:, 0:TT], 1.0 / 127.0, 1e-8,
                                                   ALU.mult, ALU.max), deps=[m4])
    m6 = p.c("vector", lambda em: em.reciprocal(b.rsx[:, 0:TT], b.sx[:, 0:TT]), deps=[m5])
    return [t5, t7, m5, m6]


def quantize_x_to_fm(b, xsrc, stat_toks):
    p = b.prog
    H = D // 2
    wr = []
    for t in range(TT):
        for h in range(2):
            xc = b.xchunk
            u = b.lnchunk.bitcast(F32)[:, 0:H]
            I = b.wd_i32[:, 0:H]
            g = b.wd_lo.bitcast(F32)[:, 0:H]
            prev = b.lnprev.get(0, ())
            ld = p.d("sync", xc[:, 0:H], xsrc[128 * t:128 * (t + 1), H * h:H * (h + 1)],
                     deps=prev, sem="x_ld")
            c1 = p.c("vector", lambda em, t=t: em.tensor_mul(
                b.wk1[:, 0:1], b.rstd[:, t:t + 1], b.rsx[:, t:t + 1]), deps=stat_toks + prev)
            c2 = p.c("vector", lambda em, t=t: em.tensor_scalar(
                b.wk1[:, 1:2], b.nmr[:, t:t + 1], b.rsx[:, t:t + 1], 256.5, ALU.mult, ALU.add), deps=[c1])
            u1 = p.c("scalar", lambda em: em.activation(
                u, xc[:, 0:H], AF.Identity, bias=b.wk1[:, 1:2], scale=b.wk1[:, 0:1]), deps=[ld, c2])
            i1 = p.c("vector", lambda em: em.tensor_copy(I, u), deps=[u1])
            g1 = p.c("vector", lambda em: em.tensor_tensor(g, I, u, ALU.is_gt), deps=[i1])
            n1 = p.c("vector", lambda em: em.scalar_tensor_tensor(
                b.scrapH, I, 256.0, g, ALU.subtract, ALU.subtract), deps=[g1])
            stq = p.d("gpsimd", b.ln_dram[128 * t:128 * (t + 1), H * h:H * (h + 1)],
                      b.scrapH, deps=[n1], sem="ln_st")
            b.lnprev[0] = [u1, n1, stq]
            wr.append(stq)
    fm = []
    for c in range(CH):
        fm.append(p.d("scalar", b.fmA[:, TOK * c:TOK * (c + 1)],
                      b.ln_dram[:, 128 * c:128 * (c + 1)], deps=wr, transpose=True, sem="fm_tr"))
    return fm


def ep_rope_quant(b, dst_dram, fold_sk):
    p = b.prog
    cosb, sinb = carve(b, "cosb"), carve(b, "sinb")
    tA = carve(b, "ropeA")
    qstg = carve(b, "qstg")
    if "rope_state" not in b:
        b.rope_state = {}
    state = b.rope_state

    def ep(t, og, ps, mm_tok):
        prev = state.get("dma", ())
        cos_t = cosb[:, 128 * t:128 * (t + 1)]
        sin_t = sinb[:, 128 * t:128 * (t + 1)]
        last = ()
        toks = []
        for hh in range(4):
            ph = ps[:, 128 * hh:128 * (hh + 1)]
            rt = tA[:, 0:128]
            tB = tA[:, 128:256]
            c1 = p.c("vector", lambda em, ph=ph, rt=rt, cos_t=cos_t, t=t: em.scalar_tensor_tensor(
                rt, ph, b.sx[:, t:t + 1], cos_t, ALU.mult, ALU.mult),
                deps=[mm_tok] + list(prev) + list(last) + list(b.get("sx_toks", ())))
            c2 = p.c("vector", lambda em, ph=ph, tB=tB, sin_t=sin_t, t=t: em.scalar_tensor_tensor(
                tB[:, 0:64], ph[:, 64:128], b.nsx[:, t:t + 1], sin_t[:, 0:64], ALU.mult, ALU.mult),
                deps=[mm_tok, c1])
            c3 = p.c("vector", lambda em, ph=ph, tB=tB, sin_t=sin_t, t=t: em.scalar_tensor_tensor(
                tB[:, 64:128], ph[:, 0:64], b.sx[:, t:t + 1], sin_t[:, 64:128], ALU.mult, ALU.mult),
                deps=[mm_tok, c2])
            c4 = p.c("vector", lambda em, rt=rt, tB=tB: em.tensor_add(rt, rt, tB), deps=[c3])
            r1 = p.c("vector", lambda em, rt=rt: em.reduce_max(
                b.wk1[:, 2:3], rt, axis=AX.X, apply_absolute_value=True), deps=[c4])
            r2 = p.c("vector", lambda em: em.tensor_scalar(
                b.wk1[:, 3:4], b.wk1[:, 2:3], 1.0 / 127.0, 1e-8, ALU.mult, ALU.max), deps=[r1])
            r3 = p.c("vector", lambda em: em.reciprocal(b.wk1[:, 4:5], b.wk1[:, 3:4]), deps=[r2])
            u = tA[:, 128:256]
            I = b.wk128i[:, 0:128]
            g = b.wk128f[:, 0:128]
            u1 = p.c("vector", lambda em, rt=rt, u=u: em.tensor_scalar(
                u, rt, b.wk1[:, 4:5], 256.5, ALU.mult, ALU.add), deps=[r3, c4])
            i1 = p.c("vector", lambda em, I=I, u=u: em.tensor_copy(I, u), deps=[u1])
            g1 = p.c("vector", lambda em, I=I, u=u, g=g: em.tensor_tensor(g, I, u, ALU.is_gt), deps=[i1])
            if fold_sk:
                n1 = p.c("vector", lambda em, I=I, g=g, u=u: em.scalar_tensor_tensor(
                    u, I, 256.0, g, ALU.subtract, ALU.subtract), deps=[g1])
                n2 = p.c("vector", lambda em, hh=hh, u=u: em.tensor_scalar_mul(
                    qstg[:, 128 * hh:128 * (hh + 1)], u, b.wk1[:, 3:4]), deps=[n1])
                last = [n2]
            else:
                n2 = p.c("vector", lambda em, I=I, g=g, hh=hh: em.scalar_tensor_tensor(
                    qstg[:, 128 * hh:128 * (hh + 1)], I, 256.0, g, ALU.subtract, ALU.subtract),
                    deps=[g1])
                last = [n2]
            toks.append(n2)
        dm = p.d("gpsimd", dst_dram[128 * t:128 * (t + 1), 512 * og:512 * (og + 1)],
                 qstg[:, 0:512], deps=toks, sem="rope_st")
        state["dma"] = [dm]
        return [toks[0], dm]
    return ep


def ep_vquant(b, dst_dram):
    p = b.prog
    tA = carve(b, "ropeA")
    qstg = carve(b, "qstg")
    state = b.rope_state

    def ep(t, og, ps, mm_tok):
        prev = state.get("dma", ())
        last = ()
        toks = []
        for hh in range(4):
            ph = ps[:, 128 * hh:128 * (hh + 1)]
            v = tA[:, 0:128]
            c1 = p.c("vector", lambda em, ph=ph, v=v, t=t: em.tensor_scalar_mul(
                v, ph, b.sx[:, t:t + 1]), deps=[mm_tok] + list(prev) + list(last) + list(b.get("sx_toks", ())))
            r1 = p.c("vector", lambda em, v=v: em.reduce_max(
                b.wk1[:, 2:3], v, axis=AX.X, apply_absolute_value=True), deps=[c1])
            r2 = p.c("vector", lambda em: em.tensor_scalar(
                b.wk1[:, 3:4], b.wk1[:, 2:3], 1.0 / 127.0, 1e-8, ALU.mult, ALU.max), deps=[r1])
            r3 = p.c("vector", lambda em: em.reciprocal(b.wk1[:, 4:5], b.wk1[:, 3:4]), deps=[r2])
            u = tA[:, 128:256]
            I = b.wk128i[:, 0:128]
            g = b.wk128f[:, 0:128]
            u1 = p.c("vector", lambda em, v=v, u=u: em.tensor_scalar(
                u, v, b.wk1[:, 4:5], 256.5, ALU.mult, ALU.add), deps=[r3])
            i1 = p.c("vector", lambda em, I=I, u=u: em.tensor_copy(I, u), deps=[u1])
            g1 = p.c("vector", lambda em, I=I, u=u, g=g: em.tensor_tensor(g, I, u, ALU.is_gt), deps=[i1])
            n1 = p.c("vector", lambda em, I=I, g=g, u=u: em.scalar_tensor_tensor(
                u, I, 256.0, g, ALU.subtract, ALU.subtract), deps=[g1])
            n2 = p.c("vector", lambda em, hh=hh, u=u: em.tensor_scalar_mul(
                qstg[:, 128 * hh:128 * (hh + 1)], u, b.wk1[:, 3:4]), deps=[n1])
            last = [n2]
            toks.append(n2)
        dm = p.d("gpsimd", dst_dram[128 * t:128 * (t + 1), 512 * og:512 * (og + 1)],
                 qstg[:, 0:512], deps=toks, sem="rope_st")
        state["dma"] = [dm]
        return [toks[0], dm]
    return ep


def attn1_launchE(b, xsrc):
    p = b.prog
    wq4 = prep_w4(b, b.wq, b.wq4_bf, "wq4")
    wk4 = prep_w4(b, b.wk, b.wk4_bf, "wk4")
    wv4 = prep_w4(b, b.wv, b.wv4_bf, "wv4")
    st = ln_stats_minmax(b, xsrc)
    ns = p.c("vector", lambda em: em.tensor_scalar_mul(b.nsx[:, 0:TT], b.sx[:, 0:TT], -1.0), deps=st)
    b.sx_toks = st + [ns]
    fm = quantize_x_to_fm(b, xsrc, st)
    rt = build_rope_tables(b, ())
    p.wait("vector", rt)

    def mv(wbf, wtk):
        def f(og, c, buf, prev):
            i = (og * CH + c) % 2
            return p.d("sync", buf, wbf[512 * og:512 * (og + 1), 128 * c:128 * (c + 1)],
                       deps=prev + wtk, transpose=True, sem=f"mv{i}")
        return f

    t2_matmul(b, b.fmA, fm + [ns], mv(b.wq4_bf, wq4), ep_rope_quant(b, b.q_dram, True))
    t2_matmul(b, b.fmA, fm, mv(b.wk4_bf, wk4), ep_rope_quant(b, b.k_dram, True))
    t2_matmul(b, b.fmA, fm, mv(b.wv4_bf, wv4), ep_vquant(b, b.v_dram))
    return


def build_band2(b, deps):
    p = b.prog
    W2 = 2944
    band = carve(b, "btmp0")[:, 0:W2]
    tmp = carve(b, "btmp1")[:, 0:W2]
    onesw = carve(b, "bones")[:, 0:W2]
    aband = carve(b, "aband2")
    g0 = broadcast_scalar(b, b.g0in, b.gvec[:, 0:1], 6, deps)
    g1 = broadcast_scalar(b, b.g1in, b.gvec[:, 1:2], 6, deps)
    on = p.c("vector", lambda em: em.memset(onesw, 1.0), deps=())
    a0 = p.c("gpsimd", lambda em: em.affine_select(band, onesw, pattern=[[-1, W2]],
             compare_op=ALU.is_ge, fill=0.0, channel_multiplier=1, base=896), deps=[on])
    m0 = p.c("vector", lambda em: em.tensor_scalar_mul(band, band, b.gvec[:, 0:1]), deps=[a0] + g0)
    a1 = p.c("gpsimd", lambda em: em.affine_select(tmp, onesw, pattern=[[-1, W2]],
             compare_op=ALU.is_ge, fill=0.0, channel_multiplier=1, base=1920), deps=[on, m0])
    m1 = p.c("vector", lambda em: em.tensor_scalar_mul(tmp, tmp, b.gvec[:, 1:2]), deps=[a1] + g1)
    m2 = p.c("vector", lambda em: em.tensor_add(band, band, tmp), deps=[m1])
    m3 = p.c("vector", lambda em: em.tensor_scalar(aband, band, 1e30, -1e30, ALU.mult, ALU.add), deps=[m2])
    return [m3]


def band2_slice(b, t1b, t2c):
    uu0 = 896 - 128 * t1b + 512 * t2c
    return carve(b, "aband2")[:, uu0:uu0 + 512]


def attn1_launchF(b, xsrc, xdst):
    p = b.prog
    wo4 = prep_w4(b, b.wo, b.wo4_bf, "wo4")
    b.lnprev["np"] = wo4[-2:]
    bd = build_band2(b, ())
    kstg = carve(b, "kstg")
    prevk = list(bd)
    qfm_wr, kfm_wr = [], []
    for c in range(CH):
        t0 = p.d("scalar", kstg[:, 0:1024], b.q_dram[:, 128 * c:128 * (c + 1)],
                 deps=prevk, transpose=True, sem="kstg")
        w0 = p.d("gpsimd", b.qfm_dram[128 * c:128 * (c + 1), :], kstg[:, 0:1024],
                 deps=[t0], sem="kstg_wr")
        t1 = p.d("scalar", kstg[:, 0:1024], b.kt0_dram[:, 128 * c:128 * (c + 1)],
                 deps=[w0], transpose=True, sem="kstg")
        t2 = p.d("scalar", kstg[:, 1024:2048], b.kt1_dram[:, 128 * c:128 * (c + 1)],
                 deps=[w0], transpose=True, sem="kstg")
        w1 = p.d("gpsimd", b.kfm_dram[128 * c:128 * (c + 1), :], kstg[:, 0:2048],
                 deps=[t1, t2], sem="kstg_wr")
        prevk = [w1]
        qfm_wr.append(w0)
        kfm_wr.append(w1)
    qh = carve(b, "qh")
    kh = carve(b, "kh2")
    vh = carve(b, "vh2")
    pT = carve(b, "pT")
    e127 = b.lnchunk.bitcast(F32)[:, 0:D // 2]
    npstg = b.scrapH
    kh_guard, vh_guard, qh_guard = [], [], []
    pt_guard = [list(bd)]
    ao_state = {}
    for h in range(NH):
        kl = p.d("sync", kh[:, 0:2048], b.kfm_dram[128 * h:128 * (h + 1), :],
                 deps=kh_guard + kfm_wr, sem="khl")
        ql = p.d("sync", qh[:, 0:1024], b.qfm_dram[128 * h:128 * (h + 1), :],
                 deps=qh_guard + qfm_wr, sem="qhl")
        v0 = p.d("sync", vh.rearrange("p (a c) -> p a c", c=128)[:, 0:8, :],
                 b.vt0_dram.rearrange("(a p) c -> p a c", p=128)[:, :, 128 * h:128 * (h + 1)],
                 deps=vh_guard, sem="vhl")
        v1 = p.d("sync", vh.rearrange("p (a c) -> p a c", c=128)[:, 8:16, :],
                 b.vt1_dram.rearrange("(a p) c -> p a c", p=128)[:, :, 128 * h:128 * (h + 1)],
                 deps=vh_guard, sem="vhl")
        kh_guard, vh_guard, qh_guard = [], [], []
        np_wr = []
        for t1b in range(TT):
            mm_toks = []
            for t2c in range(4):
                bank = 4 + t2c
                mm = p.c("tensor", lambda em, t1b=t1b, t2c=t2c, bank=bank: em.matmul(
                    b.psum[bank][:, 0:512], qh[:, 128 * t1b:128 * (t1b + 1)],
                    kh[:, 512 * t2c:512 * (t2c + 1)], start=True, stop=True),
                    deps=[kl, ql] + b.bank_guard[bank])
                md = p.c("vector", lambda em, bank=bank, bs=band2_slice(b, t1b, t2c): em.tensor_add(
                    b.psum[bank][:, 0:512], b.psum[bank][:, 0:512], bs),
                    deps=[mm] + bd)
                mm_toks.append(md)
            r = []
            for t2c in range(4):
                r.append(p.c("vector", lambda em, t2c=t2c: em.reduce_max(
                    b.wk1[:, 4 + t2c:5 + t2c], b.psum[4 + t2c][:, 0:512], axis=AX.X),
                    deps=[mm_toks[t2c]] + list(b.lnprev.get("wk1h", ()))))
            x1 = p.c("vector", lambda em: em.tensor_max(b.wk1[:, 4:5], b.wk1[:, 4:5], b.wk1[:, 5:6]), deps=r)
            x2 = p.c("vector", lambda em: em.tensor_max(b.wk1[:, 6:7], b.wk1[:, 6:7], b.wk1[:, 7:8]), deps=r)
            x3 = p.c("vector", lambda em: em.tensor_max(b.wk1[:, 4:5], b.wk1[:, 4:5], b.wk1[:, 6:7]), deps=[x1, x2])
            bch = p.c("vector", lambda em: em.tensor_scalar(
                b.wk1[:, 5:6], b.wk1[:, 4:5], -INV_SQRT_DH, LN127, ALU.mult, ALU.add), deps=[x3])
            lac = []
            for t2c in range(4):
                exq = p.c("scalar", lambda em, t2c=t2c: em.activation(
                    e127[:, 512 * t2c:512 * (t2c + 1)], b.psum[4 + t2c][:, 0:512], AF.Exp,
                    bias=b.wk1[:, 5:6], scale=INV_SQRT_DH, accum_out=b.wk1[:, 8 + t2c:9 + t2c]),
                    deps=[bch] + list(b.lnprev.get("e", ())))
                b.bank_guard[4 + t2c] = [exq]
                lac.append(exq)
            l1 = p.c("vector", lambda em: em.tensor_add(b.wk1[:, 8:9], b.wk1[:, 8:9], b.wk1[:, 9:10]), deps=lac)
            l2 = p.c("vector", lambda em: em.tensor_add(b.wk1[:, 10:11], b.wk1[:, 10:11], b.wk1[:, 11:12]), deps=lac)
            l3 = p.c("vector", lambda em: em.tensor_add(b.wk1[:, 8:9], b.wk1[:, 8:9], b.wk1[:, 10:11]), deps=[l1, l2])
            lr0 = p.c("scalar", lambda em: em.activation(b.wk1[:, 3:4], b.wk1[:, 8:9], AF.Ln,
                                                         bias=0.0, scale=1.0), deps=[l3])
            lr1 = p.c("scalar", lambda em, t1b=t1b: em.activation(
                b.rcptab[:, t1b:t1b + 1], b.wk1[:, 3:4], AF.Exp, bias=0.0, scale=-1.0),
                deps=[lr0] + list(b.lnprev.get("rc", ())) + list(b.lnprev.get(("rct", t1b), ())))
            b.lnprev["rc"] = [lr1]
            b.lnprev["wk1h"] = lac + [lr0, lr1, l3]
            rt_ = []
            for hlf in range(2):
                u = e127[:, 1024 * hlf:1024 * (hlf + 1)]
                I = b.wd_i32[:, 0:1024]
                g = b.wd_lo.bitcast(F32)[:, 0:1024]
                u1 = p.c("vector", lambda em, u=u: em.tensor_scalar_add(u, u, 0.5),
                         deps=lac + list(b.lnprev.get("np", ())))
                i1 = p.c("vector", lambda em, I=I, u=u: em.tensor_copy(I, u), deps=[u1])
                g1 = p.c("vector", lambda em, I=I, u=u, g=g: em.tensor_tensor(g, I, u, ALU.is_gt), deps=[i1])
                n1 = p.c("vector", lambda em, I=I, g=g, hlf=hlf: em.tensor_sub(
                    npstg[:, 1024 * hlf:1024 * (hlf + 1)], I, g), deps=[g1])
                rt_.append(n1)
            b.lnprev["e"] = rt_
            dm = p.d("gpsimd", b.np_dram[128 * t1b:128 * (t1b + 1), :], npstg[:, 0:2048],
                     deps=rt_ + list(b.lnprev.get("npd", ())), sem="np_st")
            b.lnprev["npd"] = [dm]
            b.lnprev["np"] = rt_ + [dm]
            np_wr.append(dm)
        ptr = []
        for t2b in range(NKB):
            ptr.append(p.d("scalar", pT[:, 1024 * t2b:1024 * (t2b + 1)],
                           b.np_dram[:, 128 * t2b:128 * (t2b + 1)], deps=np_wr + pt_guard[0],
                           transpose=True, sem="np_tr"))
        av_mms = []
        for pas in range(2):
            for t1bq in range(4):
                t1b = 4 * pas + t1bq
                bank = t1bq
                mms = []
                for t2b in range(NKB):
                    deps = ptr + [v0, v1]
                    if t2b == 0:
                        deps = deps + b.bank_guard[bank]
                    mm = p.c("tensor", lambda em, t1b=t1b, t2b=t2b, bank=bank: em.matmul(
                        b.psum[bank][:, 0:128],
                        pT[:, 1024 * t2b + 128 * t1b: 1024 * t2b + 128 * (t1b + 1)],
                        vh[:, 128 * t2b:128 * (t2b + 1)],
                        start=(t2b == 0), stop=(t2b == NKB - 1)), deps=deps)
                    mms.append(mm)
                av_mms.append(mms[-1])
                i = len(av_mms) % 2
                prev = ao_state.get(i, ())
                sc = p.c("vector", lambda em, bank=bank, t1b=t1b, i=i: em.tensor_scalar_mul(
                    b.aostgF[i][:, 0:128], b.psum[bank][:, 0:128], b.rcptab[:, t1b:t1b + 1]),
                    deps=[mms[-1]] + list(prev))
                b.lnprev[("rct", t1b)] = [sc]
                dm = p.d("gpsimd", b.ao_dram_f[128 * t1b:128 * (t1b + 1), 128 * h:128 * (h + 1)],
                         b.aostgF[i][:, 0:128], deps=[sc], sem=f"ao_st{i}")
                b.bank_guard[bank] = [sc]
                ao_state[i] = [dm]
                kh_guard.append(mms[-1])
                vh_guard.append(mms[-1])
                qh_guard.append(mms[-1])
        pt_guard[0] = av_mms
    ao_done = p.barrier_token()
    st2 = []
    H = D // 2
    for t in range(TT):
        xc = b.xchunk
        prev = b.lnprev.get(1, ())
        ld = p.d("sync", xc[:, 0:D], b.ao_dram_f[128 * t:128 * (t + 1), :],
                 deps=list(prev) + (ao_done if t == 0 else []), sem="x_ld")
        r1 = p.c("vector", lambda em, xc=xc: em.reduce_max(
            b.wk1[:, 0:1], xc[:, 0:D], axis=AX.X, apply_absolute_value=True), deps=[ld])
        r2 = p.c("vector", lambda em, t=t: em.tensor_scalar(
            b.sa[:, t:t + 1], b.wk1[:, 0:1], 1.0 / 127.0, 1e-8, ALU.mult, ALU.max), deps=[r1])
        r3 = p.c("vector", lambda em, t=t: em.reciprocal(b.wk1[:, 1:2], b.sa[:, t:t + 1]), deps=[r2])
        for hlf in range(2):
            u = b.lnchunk.bitcast(F32)[:, 0:H]
            I = b.wd_i32[:, 0:H]
            g = b.wd_lo.bitcast(F32)[:, 0:H]
            u1 = p.c("vector", lambda em, hlf=hlf, xc=xc, u=u: em.tensor_scalar(
                u, xc[:, H * hlf:H * (hlf + 1)], b.wk1[:, 1:2], 256.5, ALU.mult, ALU.add),
                deps=[r3] + list(b.lnprev.get(1, ())))
            i1 = p.c("vector", lambda em, I=I, u=u: em.tensor_copy(I, u), deps=[u1])
            g1 = p.c("vector", lambda em, I=I, u=u, g=g: em.tensor_tensor(g, I, u, ALU.is_gt), deps=[i1])
            n1 = p.c("vector", lambda em, I=I, g=g: em.scalar_tensor_tensor(
                b.scrapH, I, 256.0, g, ALU.subtract, ALU.subtract), deps=[g1])
            stq = p.d("gpsimd", b.ln_dram[128 * t:128 * (t + 1), H * hlf:H * (hlf + 1)],
                      b.scrapH, deps=[n1] + list(b.lnprev.get("st2", ())), sem="ln_st")
            b.lnprev[1] = [u1, n1, stq, r1]
            b.lnprev["st2"] = [stq]
            st2.append(stq)
    nafm = []
    for c in range(CH):
        nafm.append(p.d("scalar", b.fmA[:, TOK * c:TOK * (c + 1)],
                        b.ln_dram[:, 128 * c:128 * (c + 1)], deps=st2, transpose=True, sem="fm_tr"))

    def mv_wo(og, c, buf, prev):
        i = (og * CH + c) % 2
        return p.d("sync", buf, b.wo4_bf[512 * og:512 * (og + 1), 128 * c:128 * (c + 1)],
                   deps=prev + wo4, transpose=True, sem=f"mv{i}")

    ep_state = {}

    def ep(t, og, ps, mm_tok):
        i = t % 2
        prev = ep_state.get(i, ())
        ld = p.d("gpsimd", b.epi_x[i][:, 0:512],
                 xsrc[128 * t:128 * (t + 1), 512 * og:512 * (og + 1)], deps=prev, sem=f"epix{i}")
        a1 = p.c("vector", lambda em, i=i, ps=ps, t=t: em.scalar_tensor_tensor(
            b.epi_y[:, 0:512], ps, b.sa[:, t:t + 1], b.epi_x[i][:, 0:512], ALU.mult, ALU.add),
            deps=[ld, mm_tok] + list(ep_state.get("y", ())))
        stq = p.d("gpsimd", xdst[128 * t:128 * (t + 1), 512 * og:512 * (og + 1)],
                  b.epi_y[:, 0:512], deps=[a1], sem="epi_st")
        ep_state[i] = [ld, a1, stq]
        ep_state["y"] = [stq]
        return [a1, stq]

    t2_matmul(b, b.fmA, nafm, mv_wo, ep)
    return


def alloc_attn1_buffers(b, nc, es):
    sb = lambda name, shape, dt: es.enter_context(nc.sbuf_tensor(name, shape, dt)).ap()
    b.sx = sb("sx", [128, TT], F32)
    b.nsx = sb("nsx", [128, TT], F32)
    b.rsx = sb("rsx", [128, TT], F32)
    b.rmax = sb("rmax", [128, TT], F32)
    b.rmin = sb("rmin", [128, TT], F32)
    b.wk1 = sb("wk1", [128, 12], F32)
    b.wk128i = sb("wk128i", [128, 128], I32)
    b.wk128f = sb("wk128f", [128, 128], F32)
    b.rcptab = sb("rcptab", [128, TT], F32)
    b.sa = sb("sa", [128, TT], F32)
    b.aostgF = [sb(f"aostgF{i}", [128, 128], F32) for i in range(2)]
    b.scrapH = b.tmB[:, 0:2048]


def setup_carves_f(b):
    b.carves = {
        "pT":    (8192, 24576, "bf"),
        "qh":    (24576, 25600, "bf"),
        "kh2":   (25600, 27648, "bf"),
        "vh2":   (27648, 29696, "bf"),
        "aband2": (29696, 32640, "bf"),
        "btmp0": (2048, 5504, "bf"),
        "btmp1": (5504, 8960, "bf"),
        "bones": (24576, 27520, "bf"),
        "kstg":  (2048, 4096, "bf"),
    }


def build_launchE():
    nc = bass.Bass("TRN2", target_bir_lowering=False, debug=False)
    es = ExitStack()
    b = new_builder(nc)
    b.xin = nc.dram_tensor("xin", [TOK, D], F32, kind="ExternalInput").ap()
    b.wq = nc.dram_tensor("wq", [D, D], F32, kind="ExternalInput").ap()
    b.wk = nc.dram_tensor("wk", [D, D], F32, kind="ExternalInput").ap()
    b.wv = nc.dram_tensor("wv", [D, D], F32, kind="ExternalInput").ap()
    b.poff = nc.dram_tensor("poff", [1, 1], F32, kind="ExternalInput").ap()
    b.q_dram = nc.dram_tensor("q_out", [TOK, D], BF16, kind="ExternalOutput").ap()
    b.k_dram = nc.dram_tensor("k_out", [TOK, D], BF16, kind="ExternalOutput").ap()
    b.v_dram = nc.dram_tensor("v_out", [TOK, D], BF16, kind="ExternalOutput").ap()
    b.wq4_bf = nc.dram_tensor("wq4_bf", [D, D], BF16, kind="Internal").ap()
    b.wk4_bf = nc.dram_tensor("wk4_bf", [D, D], BF16, kind="Internal").ap()
    b.wv4_bf = nc.dram_tensor("wv4_bf", [D, D], BF16, kind="Internal").ap()
    b.ln_dram = nc.dram_tensor("ln_dram", [TOK, D], BF16, kind="Internal").ap()
    alloc_buffers(b, nc, es)
    alloc_attn_buffers(b, nc, es)
    alloc_attn1_buffers(b, nc, es)
    setup_carves(b)
    attn1_launchE(b, b.xin)
    b.prog.wait("gpsimd", b.prog.barrier_token())
    build_block(nc, b.prog, es)
    es.close()
    return nc


def build_launchF():
    nc = bass.Bass("TRN2", target_bir_lowering=False, debug=False)
    es = ExitStack()
    b = new_builder(nc)
    b.xin = nc.dram_tensor("xin", [TOK, D], F32, kind="ExternalInput").ap()
    b.wo = nc.dram_tensor("wo", [D, D], F32, kind="ExternalInput").ap()
    b.q_dram = nc.dram_tensor("q_in", [TOK, D], BF16, kind="ExternalInput").ap()
    b.kt0_dram = nc.dram_tensor("kt0", [TOK, D], BF16, kind="ExternalInput").ap()
    b.kt1_dram = nc.dram_tensor("kt1", [TOK, D], BF16, kind="ExternalInput").ap()
    b.vt0_dram = nc.dram_tensor("vt0", [TOK, D], BF16, kind="ExternalInput").ap()
    b.vt1_dram = nc.dram_tensor("vt1", [TOK, D], BF16, kind="ExternalInput").ap()
    b.g0in = nc.dram_tensor("g0in", [1, 1], F32, kind="ExternalInput").ap()
    b.g1in = nc.dram_tensor("g1in", [1, 1], F32, kind="ExternalInput").ap()
    b.xout = nc.dram_tensor("xout", [TOK, D], F32, kind="ExternalOutput").ap()
    b.wo4_bf = nc.dram_tensor("wo4_bf", [D, D], BF16, kind="Internal").ap()
    b.qfm_dram = nc.dram_tensor("qfm_dram", [D, TOK], BF16, kind="Internal").ap()
    b.kfm_dram = nc.dram_tensor("kfm_dram", [D, TK], BF16, kind="Internal").ap()
    b.np_dram = nc.dram_tensor("np_dram", [TOK, TK], BF16, kind="Internal").ap()
    b.ao_dram_f = nc.dram_tensor("ao_dram_f", [TOK, D], F32, kind="Internal").ap()
    b.ln_dram = nc.dram_tensor("ln_dram", [TOK, D], BF16, kind="Internal").ap()
    alloc_buffers(b, nc, es)
    alloc_attn_buffers(b, nc, es)
    alloc_attn1_buffers(b, nc, es)
    setup_carves_f(b)
    attn1_launchF(b, b.xin, b.xout)
    b.prog.wait("gpsimd", b.prog.barrier_token())
    build_block(nc, b.prog, es)
    es.close()
    return nc


def run_attn1(x_slices, wq, wk, wv, wo):
    n = len(x_slices)
    ncE = build_launchE()
    imE = []
    for i in range(n):
        poff = np.array([[1024.0 * (i % 2)]], dtype=np.float32)
        imE.append({"xin": x_slices[i], "wq": wq, "wk": wk, "wv": wv, "poff": poff})
    resE = run_bass_kernel_spmd(ncE, imE, core_ids=list(range(n)))
    ncF = build_launchF()
    imF = []
    for i in range(n):
        pair = i // 2
        e, o = 2 * pair, 2 * pair + 1
        g0 = np.array([[1.0 if i % 2 == 0 else 0.0]], dtype=np.float32)
        g1 = np.array([[0.0 if i % 2 == 0 else 1.0]], dtype=np.float32)
        imF.append({
            "xin": x_slices[i], "wo": wo,
            "q_in": resE.results[i]["q_out"],
            "kt0": resE.results[e]["k_out"], "kt1": resE.results[o]["k_out"],
            "vt0": resE.results[e]["v_out"], "vt1": resE.results[o]["v_out"],
            "g0in": g0, "g1in": g1})
    resF = run_bass_kernel_spmd(ncF, imF, core_ids=list(range(n)))
    return [r["xout"] for r in resF.results]



def build_fused(n_cores=NCORE):
    nc = bass.Bass("TRN2", target_bir_lowering=False, debug=False)
    es = ExitStack()
    b = new_builder(nc)
    p = b.prog
    dram = lambda name, shape, dt, kind="Internal": nc.dram_tensor(name, shape, dt, kind=kind).ap()

    b.xin = dram("xin", [TOK, D], F32, "ExternalInput")
    b.had = dram("had", [D, D], F32, "ExternalInput")
    b.wdata = dram("wdata", [D, 2048], I32, "ExternalInput")
    b.scales = dram("scales", [D, 32], F32, "ExternalInput")
    b.ffn0_bias = dram("fbias", [D], F32, "ExternalInput")
    wq0 = dram("wq0", [D, D], F32, "ExternalInput")
    wk0 = dram("wk0", [D, D], F32, "ExternalInput")
    wv0 = dram("wv0", [D, D], F32, "ExternalInput")
    wo0 = dram("wo0", [D, D], F32, "ExternalInput")
    wq1 = dram("wq1_in", [D, D], F32, "ExternalInput")
    wk1 = dram("wk1_in", [D, D], F32, "ExternalInput")
    wv1 = dram("wv1_in", [D, D], F32, "ExternalInput")
    wo1 = dram("wo1_in", [D, D], F32, "ExternalInput")
    b.wffn = dram("wffn", [D, D], F32, "ExternalInput")
    b.ffn1_bias = dram("fbias1", [D], F32, "ExternalInput")
    b.poff = dram("poff", [1, 1], F32, "ExternalInput")
    b.g0in = dram("g0in", [1, 1], F32, "ExternalInput")
    b.g1in = dram("g1in", [1, 1], F32, "ExternalInput")
    xout = dram("xout", [TOK, D], F32, "ExternalOutput")

    b.had_bf = dram("had_bf", [D, D], BF16)
    b.wdq_bf = dram("wdq_bf", [D, D], BF16)
    b.wq_bf = dram("wq_bf", [D, D], BF16)
    b.wk_bf = dram("wk_bf", [D, D], BF16)
    b.wv_bf = dram("wv_bf", [D, D], BF16)
    b.wo_bf = dram("wo_bf", [D, D], BF16)
    b.wq4_bf = dram("wq4_bf", [D, D], BF16)
    b.wk4_bf = dram("wk4_bf", [D, D], BF16)
    b.wv4_bf = dram("wv4_bf", [D, D], BF16)
    b.wo4_bf = dram("wo4_bf", [D, D], BF16)
    b.w8_bf = dram("w8_bf", [D, D], BF16)
    b.ln_dram = dram("ln_dram", [TOK, D], BF16)
    b.xr_dram = dram("xr_dram", [TOK, D], BF16)
    b.q_dram = dram("q01", [TOK, D], BF16)
    b.k_dram = dram("k01", [TOK, D], BF16)
    b.vaug_dram = dram("v0aug", [TOK, 4128], BF16)
    b.v_dram = dram("v1", [TOK, D], BF16)
    kg = dram("kg", [TK, D], BF16)
    vg0 = dram("vg0", [TK, 4128], BF16)
    vg1 = dram("vg1", [TK, D], BF16)
    b.kfm_dram = dram("kfm_dram", [D, TK], BF16)
    b.qfm_dram = dram("qfm_dram", [D, TOK], BF16)
    b.np_dram = dram("np_dram", [TOK, TK], BF16)
    b.ao_dram = dram("ao_dram", [TOK, D], BF16)
    b.ao_dram_f = dram("ao_dram_f", [TOK, D], F32)
    b.amax_loc = dram("amax_loc", [1, 1], F32)
    b.amax_red = dram("amax_red", [1, 1], F32)
    xa = dram("xa", [TOK, D], F32)
    xb = dram("xb", [TOK, D], F32)

    alloc_buffers(b, nc, es)
    alloc_attn_buffers(b, nc, es)
    alloc_l1_buffers(b, nc, es)
    alloc_attn1_buffers(b, nc, es)
    for flag in ("HEAD_BARRIER", "STAGE_BARRIER", "AV_ALLGATES", "SKIP_WO",
                 "PT_FULLGUARD", "KH_SERIAL", "GATE_POOL", "SKIP_PREP"):
        b[flag] = False

    RGp = [[2 * i, 2 * i + 1] for i in range(n_cores // 2)]
    RGall = [list(range(n_cores))]

    def allgather(src, dst, deps):
        return p.c("gpsimd", lambda em: em.collective_compute(
            "AllGather", ALU.bypass, replica_groups=RGp,
            ins=[src.opt()], outs=[dst.opt()]), deps=deps)

    def amax_allreduce(deps):
        return p.c("gpsimd", lambda em: em.collective_compute(
            "AllReduce", ALU.max, replica_groups=RGall,
            ins=[b.amax_loc.opt()], outs=[b.amax_red.opt()]), deps=deps)

    wtA = prep_cast_bf16(b, b.had, b.had_bf, "had") + prep_wdq(b)
    ffn0_sublayer(b, b.xin, xa, wtA)
    p.set_floor()

    setup_carves(b)
    wtB = (prep_cast_bf16(b, wq0, b.wq_bf, "wq") + prep_cast_bf16(b, wk0, b.wk_bf, "wk")
           + prep_cast_bf16(b, wv0, b.wv_bf, "wv"))
    attn0_launchB(b, xa, b.wq_bf, b.wk_bf, b.wv_bf, wtB)
    p.set_floor()
    cc1 = allgather(b.k_dram, kg, ())
    cc2 = allgather(b.vaug_dram, vg0, [cc1])
    p.wait("gpsimd", [cc2])
    p.set_floor()

    b.kt0_dram, b.kt1_dram = kg[0:TOK, :], kg[TOK:2 * TOK, :]
    b.vaug0_dram, b.vaug1_dram = vg0[0:TOK, :], vg0[TOK:2 * TOK, :]
    wtC = prep_cast_bf16(b, wo0, b.wo_bf, "wo")
    attn0_launchC(b, xa, xb, b.wo_bf, wtC)
    p.set_floor()

    ffn0_sublayer(b, xb, xa, [])
    p.set_floor()

    z = p.c("vector", lambda em: em.memset(b.zero11[0:1, 0:1], 0.0), deps=())
    b.zero11_tok = [z]
    st = ln_stats(b, xa)
    am = ln_absmax(b, xa, st)
    dA = p.d("gpsimd", b.amax_loc, b.s11[0:1, 2:3], deps=am, sem="amx_st")
    cca = amax_allreduce([dA])
    b.wprev[("wcast",)] = am
    wt8 = prep_w8(b, b.wffn, b.w8_bf)
    l1ffn_main(b, xa, xb, b.amax_red, wt8 + [z], st=st, amax_w=1, amax_deps=[cca])
    p.set_floor()

    setup_carves(b)
    b.wq, b.wk, b.wv, b.wo = wq1, wk1, wv1, wo1
    attn1_launchE(b, xb)
    p.set_floor()
    cc3 = allgather(b.k_dram, kg, ())
    cc4 = allgather(b.v_dram, vg1, [cc3])
    p.wait("gpsimd", [cc4])
    p.set_floor()

    setup_carves_f(b)
    b.kt0_dram, b.kt1_dram = kg[0:TOK, :], kg[TOK:2 * TOK, :]
    b.vt0_dram, b.vt1_dram = vg1[0:TOK, :], vg1[TOK:2 * TOK, :]
    attn1_launchF(b, xb, xa)
    p.set_floor()

    st2 = ln_stats(b, xa)
    am2 = ln_absmax(b, xa, st2)
    dA2 = p.d("gpsimd", b.amax_loc, b.s11[0:1, 2:3], deps=am2, sem="amx_st")
    cca2 = amax_allreduce([dA2])
    l1ffn_main(b, xa, xout, b.amax_red, [], st=st2, amax_w=1, amax_deps=[cca2])

    p.wait("gpsimd", p.barrier_token())
    build_block(nc, b.prog, es)
    es.close()
    return nc


_UNFUSED = False

_nc_cache = {}


def _get_nc(name, builder):
    if name not in _nc_cache:
        _nc_cache[name] = builder()
    return _nc_cache[name]


def kernel(**inputs):
    x = np.asarray(inputs["L0_attn_x"], np.float32)
    had = np.asarray(inputs["L0_ffn_hadamard"], np.float32)
    wsc = np.asarray(inputs["L0_ffn_weight_scales"], np.float32)
    fb0 = np.asarray(inputs["L0_ffn_bias"], np.float32)
    wd = np.asarray(inputs["L0_ffn_weight_data"], np.int32)
    wq0 = np.asarray(inputs["L0_attn_wq"], np.float32)
    wk0 = np.asarray(inputs["L0_attn_wk"], np.float32)
    wv0 = np.asarray(inputs["L0_attn_wv"], np.float32)
    wo0 = np.asarray(inputs["L0_attn_wo"], np.float32)
    wq1 = np.asarray(inputs["L1_attn_wq"], np.float32)
    wk1 = np.asarray(inputs["L1_attn_wk"], np.float32)
    wv1 = np.asarray(inputs["L1_attn_wv"], np.float32)
    wo1 = np.asarray(inputs["L1_attn_wo"], np.float32)
    w1 = np.asarray(inputs["L1_ffn_weight"], np.float32)
    fb1 = np.asarray(inputs["L1_ffn_bias"], np.float32)

    n = 8
    xs = [np.ascontiguousarray(x[i // 2, 1024 * (i % 2):1024 * (i % 2) + 1024, :])
          for i in range(n)]
    cores = list(range(n))
    poffs = [np.array([[1024.0 * (i % 2)]], np.float32) for i in range(n)]
    g0s = [np.array([[1.0 if i % 2 == 0 else 0.0]], np.float32) for i in range(n)]
    g1s = [np.array([[0.0 if i % 2 == 0 else 1.0]], np.float32) for i in range(n)]

    def ffn0(xs):
        nc = _get_nc("A", build_launchA)
        im = [{"xin": xs[i], "had": had, "wdata": wd, "scales": wsc, "fbias": fb0}
              for i in range(n)]
        res = run_bass_kernel_spmd(nc, im, core_ids=cores)
        return [r["xout"] for r in res.results]

    def attn0(xs):
        ncB = _get_nc("B", build_launchB)
        imB = [{"xin": xs[i], "wq": wq0, "wk": wk0, "wv": wv0, "poff": poffs[i]}
               for i in range(n)]
        resB = run_bass_kernel_spmd(ncB, imB, core_ids=cores)
        ncC = _get_nc("C", build_launchC)
        imC = []
        for i in range(n):
            e, o = 2 * (i // 2), 2 * (i // 2) + 1
            imC.append({
                "xin": xs[i], "wo": wo0,
                "q_in": resB.results[i]["q_out"],
                "kt0": resB.results[e]["k_out"], "kt1": resB.results[o]["k_out"],
                "vaug0": resB.results[e]["v_out"], "vaug1": resB.results[o]["v_out"],
                "g0in": g0s[i], "g1in": g1s[i]})
        resC = run_bass_kernel_spmd(ncC, imC, core_ids=cores)
        return [r["xout"] for r in resC.results]

    def ffn1(xs):
        nc1 = _get_nc("D1", build_launchD1)
        res1 = run_bass_kernel_spmd(nc1, [{"xin": xs[i]} for i in range(n)], core_ids=cores)
        amax8 = np.zeros((1, 8), np.float32)
        for i in range(n):
            amax8[0, i] = res1.results[i]["amax_out"][0, 0]
        nc2 = _get_nc("D2", build_launchD2)
        im = [{"xin": xs[i], "wffn": w1, "fbias1": fb1, "amax8": amax8} for i in range(n)]
        res2 = run_bass_kernel_spmd(nc2, im, core_ids=cores)
        return [r["xout"] for r in res2.results]

    def attn1(xs):
        ncE = _get_nc("E", build_launchE)
        imE = [{"xin": xs[i], "wq": wq1, "wk": wk1, "wv": wv1, "poff": poffs[i]}
               for i in range(n)]
        resE = run_bass_kernel_spmd(ncE, imE, core_ids=cores)
        ncF = _get_nc("F", build_launchF)
        imF = []
        for i in range(n):
            e, o = 2 * (i // 2), 2 * (i // 2) + 1
            imF.append({
                "xin": xs[i], "wo": wo1,
                "q_in": resE.results[i]["q_out"],
                "kt0": resE.results[e]["k_out"], "kt1": resE.results[o]["k_out"],
                "vt0": resE.results[e]["v_out"], "vt1": resE.results[o]["v_out"],
                "g0in": g0s[i], "g1in": g1s[i]})
        resF = run_bass_kernel_spmd(ncF, imF, core_ids=cores)
        return [r["xout"] for r in resF.results]

    if _UNFUSED:
        xs = ffn0(xs)
        xs = attn0(xs)
        xs = ffn0(xs)
        xs = ffn1(xs)
        xs = attn1(xs)
        xs = ffn1(xs)
    else:
        nc = _get_nc("fused", build_fused)
        im = [{"xin": xs[i], "had": had, "wdata": wd, "scales": wsc, "fbias": fb0,
               "wq0": wq0, "wk0": wk0, "wv0": wv0, "wo0": wo0,
               "wq1_in": wq1, "wk1_in": wk1, "wv1_in": wv1, "wo1_in": wo1,
               "wffn": w1, "fbias1": fb1,
               "poff": poffs[i], "g0in": g0s[i], "g1in": g1s[i]} for i in range(n)]
        res = run_bass_kernel_spmd(nc, im, core_ids=cores)
        xs = [r["xout"] for r in res.results]

    out = np.empty((4, 2048, 4096), np.float32)
    for i in range(n):
        out[i // 2, 1024 * (i % 2):1024 * (i % 2) + 1024, :] = xs[i]
    return out


# version 03
# speedup vs baseline: 1.1263x; 1.1263x over previous
import numpy as np
import ml_dtypes
from contextlib import ExitStack

import concourse.bass as bass
import concourse.mybir as mybir
from concourse.bass_utils import run_bass_kernel_spmd

F32 = mybir.dt.float32
BF16 = mybir.dt.bfloat16
I32 = mybir.dt.int32
AF = mybir.ActivationFunctionType
ALU = mybir.AluOpType
AX = mybir.AxisListType
BF = ml_dtypes.bfloat16

TOK = 1024
D = 4096
CH = D // 128
TT = TOK // 128
NH, DH = 32, 128
NCORE = 8

COMPUTE = ("scalar", "vector", "tensor", "gpsimd")
DMAQ = ("sync", "scalar", "gpsimd")
ALL_ENG = ("sync", "scalar", "vector", "tensor", "gpsimd")


class Prog:

    def __init__(self):
        self.ops = {e: [] for e in ALL_ENG}
        self.ccnt = {e: 0 for e in COMPUTE}
        self.dcnt = {}
        self.water = {e: {} for e in ALL_ENG}
        self.floor = []

    def set_floor(self):
        self.floor = self.barrier_token()

    def _waits(self, e, deps):
        deps = list(deps) + self.floor
        best = {}
        for dep in deps:
            if dep is None:
                continue
            kind, name, n = dep
            val = n if kind == "c" else 16 * n
            key = (kind, name)
            if val > best.get(key, -1):
                best[key] = val
        waits = []
        for (kind, name), val in best.items():
            if self.water[e].get((kind, name), -1) >= val:
                continue
            self.water[e][(kind, name)] = val
            waits.append((kind, name, val))
        return waits

    def c(self, e, emit, deps=()):
        assert e in COMPUTE
        waits = self._waits(e, deps)
        self.ccnt[e] += 1
        n = self.ccnt[e]

        def run(em, env):
            for kind, name, val in waits:
                em.wait_ge(env.csem[name] if kind == "c" else env.dsem[name], val)
            emit(em).then_inc(env.csem[e], 1)

        self.ops[e].append(run)
        return ("c", e, n)

    def d(self, q, out, in_, deps=(), transpose=False, sem=None, **kw):
        assert q in DMAQ
        sem = sem or ("q_" + q)
        waits = self._waits(q, deps)
        self.dcnt[sem] = self.dcnt.get(sem, 0) + 1
        n = self.dcnt[sem]

        def run(em, env):
            for kind, name, val in waits:
                em.wait_ge(env.csem[name] if kind == "c" else env.dsem[name], val)
            o = out() if callable(out) else out
            i = in_() if callable(in_) else in_
            em.dma_start(out=o, in_=i, transpose=transpose, **kw).then_inc(env.dsem[sem], 16)

        self.ops[q].append(run)
        return ("d", sem, n)

    def wait(self, e, deps):
        waits = self._waits(e, deps)
        if not waits:
            return

        def run(em, env):
            for kind, name, val in waits:
                em.wait_ge(env.csem[name] if kind == "c" else env.dsem[name], val)

        self.ops[e].append(run)

    def barrier_token(self):
        toks = [("c", e, self.ccnt[e]) for e in COMPUTE if self.ccnt[e] > 0]
        toks += [("d", s, n) for s, n in self.dcnt.items() if n > 0]
        return toks


class Fill:

    def __init__(self):
        self.gens = []

    def add(self, gen):
        self.gens.append(gen)

    def pump(self, units=1):
        for _ in range(units):
            while self.gens:
                try:
                    next(self.gens[0])
                    break
                except StopIteration:
                    self.gens.pop(0)
            if not self.gens:
                return

    def drain(self):
        while self.gens:
            self.pump()


class Env:
    def __init__(self, csem, dsem):
        self.csem = csem
        self.dsem = dsem


def build_block(nc, prog, es):
    csem = {e: es.enter_context(nc.semaphore("cs_" + e)) for e in COMPUTE}
    dsem = {s: es.enter_context(nc.semaphore("ds_" + s)) for s in prog.dcnt}
    env = Env(csem, dsem)
    block = es.enter_context(nc.Block())

    def section(name):
        def body(em):
            for run in prog.ops[name]:
                run(em, env)
        return body

    block.sync(section("sync"))
    block.scalar(section("scalar"))
    block.vector(section("vector"))
    block.tensor(section("tensor"))
    block.gpsimd(section("gpsimd"))


class NS(dict):
    __getattr__ = dict.__getitem__
    __setattr__ = dict.__setitem__



def prep_cast_bf16(b, src_f32, dst_bf, tag):
    p = b.prog
    toks = []
    H = D // 2
    for k in range(CH):
        for h in range(2):
            buf = b.wchunk_f32
            obuf = b.wchunk_bf
            prev = b.wprev.get(("wcast",), ())
            ld = p.d("sync", buf[:, 0:H], src_f32[128 * k:128 * (k + 1), H * h:H * (h + 1)], deps=prev, sem="wp_ld")
            ct = p.c("scalar", lambda em, buf=buf, obuf=obuf, h=h: em.activation(
                obuf[:, H * h:H * (h + 1)], buf[:, 0:H], AF.Copy, bias=0.0, scale=1.0), deps=[ld])
            st = p.d("gpsimd", dst_bf[128 * k:128 * (k + 1), H * h:H * (h + 1)],
                     obuf[:, H * h:H * (h + 1)], deps=[ct], sem="wp_st")
            b.wprev[("wcast",)] = [ct, st]
            toks.append(st)
    return toks


def prep_wdq(b):
    p = b.prog
    toks = []
    for k in range(CH):
        wd, lo = b.wd_i32, b.wd_lo
        sc, m8 = b.wd_sc, b.wd_m8
        out = b.wchunk_bf
        prev = list(b.wprev.get(("wdq",), ())) + list(b.wprev.get(("wcast",), ()))
        ld = p.d("sync", wd[:], b.wdata[128 * k:128 * (k + 1), :], deps=prev, sem="wdq_ld")
        ls = p.d("sync", sc[:], b.scales[128 * k:128 * (k + 1), :], deps=prev, sem="wdq_ld")
        tl = p.c("vector", lambda em, lo=lo, wd=wd: em.tensor_single_scalar(lo, wd, 15, ALU.bitwise_and), deps=[ld, ls])
        th = p.c("vector", lambda em, wd=wd: em.tensor_single_scalar(wd, wd, 4, ALU.logical_shift_right), deps=[tl])
        tm = p.c("vector", lambda em, m8=m8, sc=sc: em.tensor_scalar_mul(m8, sc, -8.0), deps=[ld, ls])
        deps0 = [tl, th, tm]
        gt = []
        for g in range(32):
            for off, srct in ((0, lo), (1, wd)):
                def mk(g=g, off=off, srct=srct, sc=sc, m8=m8, out=out):
                    def f(em):
                        return em.tensor_scalar(
                            out[:, 128 * g + off: 128 * (g + 1): 2],
                            srct[:, 64 * g: 64 * (g + 1)],
                            sc[:, g:g + 1], m8[:, g:g + 1], ALU.mult, ALU.add)
                    return f
                gt.append(p.c("vector", mk(), deps=deps0))
                deps0 = ()
        st = p.d("gpsimd", b.wdq_bf[128 * k:128 * (k + 1), :], out[:, 0:D], deps=gt, sem="wdq_st")
        b.wprev[("wdq",)] = [st] + gt[-1:]
        b.wprev[("wcast",)] = [st]
        toks.append(st)
    return toks



def ln_stats(b, xsrc):
    p = b.prog
    done = []
    for t in range(TT):
        xc = b.xchunk
        prev = b.lnprev.get(0, ())
        ld = p.d("sync", xc[:], xsrc[128 * t:128 * (t + 1), :], deps=prev, sem="x_ld")
        s1 = p.c("vector", lambda em, t=t, xc=xc: em.reduce_sum(
            b.ssum[:, t:t + 1], xc[:], axis=AX.X), deps=[ld])
        s2 = p.c("scalar", lambda em, t=t, xc=xc: em.activation(
            b.lnchunk[:, 0:D], xc[:], AF.Square, bias=0.0, scale=1.0,
            accum_out=b.ssq[:, t:t + 1]), deps=[ld])
        b.lnprev[0] = [s1, s2]
        done += [s1, s2]
    t0 = p.c("vector", lambda em: em.tensor_scalar_mul(b.mean[:], b.ssum[:, 0:TT], 1.0 / D), deps=done)
    t1 = p.c("vector", lambda em: em.tensor_scalar_mul(b.var[:], b.ssq[:, 0:TT], 1.0 / D), deps=done)
    t2 = p.c("vector", lambda em: em.tensor_tensor(b.msq[:], b.mean[:], b.mean[:], ALU.mult), deps=[t0])
    t3 = p.c("vector", lambda em: em.tensor_sub(b.var[:], b.var[:], b.msq[:]), deps=[t1, t2])
    t3b = p.c("vector", lambda em: em.tensor_scalar_add(b.var[:], b.var[:], 1e-5), deps=[t3])
    t4 = p.c("scalar", lambda em: em.activation(b.std[:], b.var[:], AF.Sqrt, bias=0.0, scale=1.0), deps=[t3b])
    t5 = p.c("vector", lambda em: em.reciprocal(b.rstd[:], b.std[:]), deps=[t4])
    t6 = p.c("vector", lambda em: em.tensor_tensor(b.nmr[:], b.mean[:], b.rstd[:], ALU.mult), deps=[t5])
    t7 = p.c("vector", lambda em: em.tensor_scalar_mul(b.nmr[:], b.nmr[:], -1.0), deps=[t6])
    return [t5, t7]


def ln_normalize_to_fm(b, xsrc, fm_buf, stat_toks):
    p = b.prog
    wr = []
    for t in range(TT):
        xc = b.xchunk
        lc = b.lnchunk
        prev = b.lnprev.get(0, ())
        ld = p.d("sync", xc[:], xsrc[128 * t:128 * (t + 1), :], deps=prev, sem="x_ld")
        ct = p.c("scalar", lambda em, t=t, xc=xc, lc=lc: em.activation(
            lc[:, 0:D], xc[:], AF.Identity, bias=b.nmr[:, t:t + 1], scale=b.rstd[:, t:t + 1]),
            deps=[ld] + stat_toks)
        st = p.d("gpsimd", b.ln_dram[128 * t:128 * (t + 1), :], lc[:, 0:D], deps=[ct], sem="ln_st")
        b.lnprev[0] = [ct, st]
        wr.append(st)
    tr = []
    for c in range(CH):
        tr.append(p.d("scalar", fm_buf[:, TOK * c:TOK * (c + 1)],
                      b.ln_dram[:, 128 * c:128 * (c + 1)], deps=wr, transpose=True, sem="fm_tr"))
    return tr



def t2_matmul(b, fm_buf, fm_ready, moving_loader, epilogue, n_out=D, og_size=512,
              bias_row=None):
    p = b.prog
    n_og = n_out // og_size
    mv_prev = b.mv_prev
    fill = b.get("fill")
    for og in range(n_og):
        if fill is not None:
            fill.pump(1)
        for c in range(CH):
            i = (og * CH + c) % 2
            ld = moving_loader(og, c, b.mvchunk[i][:, 0:og_size], list(mv_prev[i]))
            mm_toks = []
            for t in range(TT):
                deps = [ld] + fm_ready
                if c == 0:
                    deps = deps + b.bank_guard[t]
                def mk(og=og, c=c, t=t, i=i):
                    def f(em):
                        return em.matmul(
                            b.psum[t][:, 0:og_size],
                            fm_buf[:, TOK * c + 128 * t: TOK * c + 128 * (t + 1)],
                            b.mvchunk[i][:, 0:og_size],
                            start=(c == 0), stop=(c == CH - 1 and bias_row is None))
                    return f
                mm_toks.append(p.c("tensor", mk(), deps=deps))
            mv_prev[i] = mm_toks
            if c == CH - 1:
                if bias_row is not None:
                    brow, btoks = bias_row
                    bias_mm = []
                    for t in range(TT):
                        def mkb(og=og, t=t):
                            def f(em):
                                return em.matmul(
                                    b.psum[t][:, 0:og_size],
                                    b.onesbf[0:1, 128 * 0:128 * 1],
                                    brow[0:1, og_size * og:og_size * (og + 1)],
                                    start=False, stop=True)
                            return f
                        bias_mm.append(p.c("tensor", mkb(), deps=[mm_toks[t]] + btoks))
                    mm_toks = bias_mm
                for t in range(TT):
                    b.bank_guard[t] = list(epilogue(t, og, b.psum[t][:, 0:og_size], mm_toks[t]))
    out = []
    for t in range(TT):
        out += b.bank_guard[t]
    return out



def load_bias_row(b, bias_row_dram, deps):
    p = b.prog
    on = p.c("vector", lambda em: em.memset(b.onesbf[0:1, 0:128], 1.0), deps=())
    toks = [on]
    prev = ()
    for g in range(D // 512):
        ld = p.d("sync", b.browf[0:1, 0:512], bias_row_dram[None, 512 * g:512 * (g + 1)],
                 deps=list(deps) + list(prev), sem="bias_ld")
        ct = p.c("scalar", lambda em, g=g: em.activation(
            b.browbf[0:1, 512 * g:512 * (g + 1)], b.browf[0:1, 0:512], AF.Copy,
            bias=0.0, scale=1.0), deps=[ld])
        prev = [ct]
        toks.append(ct)
    return toks



def ffn0_sublayer(b, xsrc, xdst, wprep_toks):
    p = b.prog
    st = ln_stats(b, xsrc)
    fm_ready = ln_normalize_to_fm(b, xsrc, b.fmA, st)

    def mv_had(og, c, buf, prev):
        i = (og * CH + c) % 2
        return p.d("sync", buf, b.had_bf[128 * c:128 * (c + 1), 512 * og:512 * (og + 1)],
                   deps=prev + wprep_toks, sem=f"mv{i}")

    def ep_mm1(t, og, ps, mm_tok):
        ct = p.c("scalar", lambda em, t=t, og=og, ps=ps: em.activation(
            b.tmB[:, D * t + 512 * og: D * t + 512 * (og + 1)], ps, AF.Copy, bias=0.0, scale=1.0),
            deps=[mm_tok])
        return [ct]

    mm1_done = t2_matmul(b, b.fmA, fm_ready, mv_had, ep_mm1)

    xr_wr = []
    for t in range(TT):
        xr_wr.append(p.d("gpsimd", b.xr_dram[128 * t:128 * (t + 1), :],
                         b.tmB[:, D * t:D * (t + 1)], deps=mm1_done, sem="xr_st"))
    xr_fm = []
    for c in range(CH):
        xr_fm.append(p.d("scalar", b.fmA[:, TOK * c:TOK * (c + 1)],
                         b.xr_dram[:, 128 * c:128 * (c + 1)], deps=xr_wr, transpose=True, sem="fm_tr"))

    bb = load_bias_row(b, b.ffn0_bias, wprep_toks)

    def mv_wdq(og, c, buf, prev):
        i = (og * CH + c) % 2
        return p.d("sync", buf, b.wdq_bf[512 * og:512 * (og + 1), 128 * c:128 * (c + 1)],
                   deps=prev + wprep_toks, transpose=True, sem=f"mv{i}")

    ep_state = {}

    def ep_mm2(t, og, ps, mm_tok):
        i = t % 2
        prev = ep_state.get(i, ())
        ld = p.d("gpsimd", b.epi_x[i][:, 0:512],
                 xsrc[128 * t:128 * (t + 1), 512 * og:512 * (og + 1)], deps=prev, sem=f"epix{i}")
        a1 = p.c("vector", lambda em, i=i, ps=ps: em.scalar_tensor_tensor(
            b.epi_y[:, 0:512], ps, 0.5, b.epi_x[i][:, 0:512], ALU.mult, ALU.add),
            deps=[ld, mm_tok] + list(ep_state.get("y", ())))
        stq = p.d("gpsimd", xdst[128 * t:128 * (t + 1), 512 * og:512 * (og + 1)],
                  b.epi_y[:, 0:512], deps=[a1], sem="epi_st")
        ep_state[i] = [ld, a1, stq]
        ep_state["y"] = [stq]
        return [a1, stq]

    done = t2_matmul(b, b.fmA, xr_fm, mv_wdq, ep_mm2, bias_row=(b.browbf, bb))
    return done



def alloc_buffers(b, nc, es):
    sb = lambda name, shape, dt: es.enter_context(nc.sbuf_tensor(name, shape, dt)).ap()
    b.fmA = sb("fmA", [128, CH * TOK], BF16)
    b.tmB = sb("tmB", [128, CH * TOK], BF16)
    b.xchunk = sb("xchunk", [128, D], F32)
    b.lnchunk = sb("lnchunk", [128, D], BF16)
    b.mvchunk = [sb(f"mv{i}", [128, 512], BF16) for i in range(2)]
    b.epi_x = [sb(f"epix{i}", [128, 512], F32) for i in range(2)]
    b.epi_y = sb("epiy", [128, 512], F32)
    b.browf = sb("browf", [1, 512], F32)
    b.browbf = sb("browbf", [1, D], BF16)
    b.onesbf = sb("onesbf", [1, 128], BF16)
    for n in ("ssum", "ssq", "mean", "var", "msq", "std", "rstd", "nmr"):
        b[n] = sb(n, [128, TT], F32)
    b.wchunk_f32 = sb("wcf", [128, D // 2], F32)
    b.wchunk_bf = sb("wcb", [128, D], BF16)
    b.wd_i32 = sb("wdi", [128, 2048], I32)
    b.wd_lo = sb("wdlo", [128, 2048], I32)
    b.wd_sc = sb("wdsc", [128, 32], F32)
    b.wd_m8 = sb("wdm8", [128, 32], F32)
    b.psum = [es.enter_context(nc.psum_tensor(f"pb{i}", [128, 512], F32)).ap() for i in range(8)]
    b.bank_guard = {i: [] for i in range(8)}


def new_builder(nc):
    b = NS()
    b.prog = Prog()
    b.wprev = {}
    b.lnprev = {}
    b.mv_prev = [(), ()]
    b.nc = nc
    return b



def build_launchA():
    nc = bass.Bass("TRN2", target_bir_lowering=False, debug=False)
    es = ExitStack()
    b = new_builder(nc)

    b.xin = nc.dram_tensor("xin", [TOK, D], F32, kind="ExternalInput").ap()
    b.had = nc.dram_tensor("had", [D, D], F32, kind="ExternalInput").ap()
    b.wdata = nc.dram_tensor("wdata", [D, 2048], I32, kind="ExternalInput").ap()
    b.scales = nc.dram_tensor("scales", [D, 32], F32, kind="ExternalInput").ap()
    b.ffn0_bias = nc.dram_tensor("fbias", [D], F32, kind="ExternalInput").ap()
    b.xout = nc.dram_tensor("xout", [TOK, D], F32, kind="ExternalOutput").ap()

    b.had_bf = nc.dram_tensor("had_bf", [D, D], BF16, kind="Internal").ap()
    b.wdq_bf = nc.dram_tensor("wdq_bf", [D, D], BF16, kind="Internal").ap()
    b.ln_dram = nc.dram_tensor("ln_dram", [TOK, D], BF16, kind="Internal").ap()
    b.xr_dram = nc.dram_tensor("xr_dram", [TOK, D], BF16, kind="Internal").ap()

    alloc_buffers(b, nc, es)

    wt = prep_cast_bf16(b, b.had, b.had_bf, "had") + prep_wdq(b)
    ffn0_sublayer(b, b.xin, b.xout, wt)

    b.prog.wait("gpsimd", b.prog.barrier_token())
    build_block(nc, b.prog, es)
    es.close()
    return nc


def run_launchA(x_slices, had, wdata, scales, fbias):
    nc = build_launchA()
    in_maps = [{"xin": x_slices[i], "had": had, "wdata": wdata,
                "scales": scales, "fbias": fbias} for i in range(len(x_slices))]
    res = run_bass_kernel_spmd(nc, in_maps, core_ids=list(range(len(x_slices))))
    return [r["xout"] for r in res.results]





TK = 2048
NKB = TK // 128
INV_SQRT_DH = 0.08838834764831845
C_SHIFT = 40.0
BANDW = 3456

def carve(b, name):
    e0, e1, dt = b.carves[name]
    ap = b.tmB[:, e0:e1]
    return ap.bitcast(F32) if dt == "f32" else ap

def setup_carves(b):
    b.carves = {
        "pTa":   (0, 8192, "bf"),
        "pT":    (8192, 24576, "bf"),
        "kh":    (24576, 26624, "bf"),
        "vh":    (26624, 28688, "bf"),
        "aband": (28688, 32144, "bf"),
        "btmp0": (0, 3456, "bf"),
        "btmp1": (3456, 6912, "bf"),
        "bones": (8192, 11648, "bf"),
        "kstg":  (0, 2048, "bf"),
        "cosb":  (24576, 25600, "bf"),
        "sinb":  (25600, 26624, "bf"),
        "vst0":  (26624, 27140, "bf"),
        "vst1":  (27140, 27656, "bf"),
        "ropeA": (27688, 28200, "f32"),
        "ropeB": (28200, 28712, "f32"),
        "qstg":  (28712, 29224, "bf"),
    }

def broadcast_scalar(b, src_dram11, dst_f32col, bank, deps):
    p = b.prog
    ld = p.d("sync", b.sc11[0:1, 0:1], src_dram11[0:1, 0:1],
             deps=list(deps) + list(b.get("sc11_guard", ())), sem="sc11")
    on = p.c("vector", lambda em: em.memset(b.onesf1[0:1, 0:128], 1.0), deps=())
    mm = p.c("tensor", lambda em, bank=bank: em.matmul(
        b.psum[bank][:, 0:1], b.onesf1[0:1, 0:128], b.sc11[0:1, 0:1],
        start=True, stop=True), deps=[ld, on] + b.bank_guard[bank])
    cp = p.c("vector", lambda em, bank=bank: em.tensor_copy(dst_f32col, b.psum[bank][:, 0:1]), deps=[mm])
    b.bank_guard[bank] = [cp]
    b.sc11_guard = [mm]
    return [cp]

def build_rope_tables(b, deps):
    p = b.prog
    cosb, sinb = carve(b, "cosb"), carve(b, "sinb")
    tA, tB = carve(b, "ropeA"), carve(b, "ropeB")
    th = tA[:, 0:128]
    ang = tA[:, 128:256]
    u = tB[:, 0:128]
    red = tB[:, 128:256]
    gi = b.gint
    pv = broadcast_scalar(b, b.poff, b.poffv[:, 0:1], 7, deps)
    j0 = p.c("gpsimd", lambda em: em.iota(th, pattern=[[1, 128]], base=0, channel_multiplier=0,
                                          allow_small_or_imprecise_dtypes=True), deps=())
    j1 = p.c("vector", lambda em: em.tensor_single_scalar(ang, th, 64.0, ALU.is_ge), deps=[j0])
    j2 = p.c("vector", lambda em: em.scalar_tensor_tensor(th, ang, -64.0, th, ALU.mult, ALU.add), deps=[j1])
    j3 = p.c("scalar", lambda em: em.activation(th, th, AF.Exp, bias=0.0,
                                                scale=-2.0 * float(np.log(10000.0)) / 128.0), deps=[j2])
    prev = [j3]
    for t in range(TT):
        ps = p.c("gpsimd", lambda em, t=t: em.iota(b.posv[:, 0:1], pattern=[[0, 1]], base=128 * t,
                                                   channel_multiplier=1,
                                                   allow_small_or_imprecise_dtypes=True), deps=prev)
        pa = p.c("vector", lambda em: em.tensor_add(b.posv[:, 1:2], b.posv[:, 0:1], b.poffv[:, 0:1]),
                 deps=[ps] + pv)
        for trig in range(2):
            if trig == 0:
                a0 = p.c("vector", lambda em: em.tensor_scalar_mul(ang, th, b.posv[:, 1:2]),
                         deps=[pa, j3] + prev)
            else:
                a0 = p.c("vector", lambda em: em.tensor_scalar(ang, th, b.posv[:, 1:2],
                         float(np.pi / 2), ALU.mult, ALU.add), deps=[pa, j3] + prev)
            u1 = p.c("vector", lambda em: em.tensor_scalar(u, ang, 1.0 / (2 * np.pi), 0.5,
                                                           ALU.mult, ALU.add), deps=[a0])
            i1 = p.c("vector", lambda em: em.tensor_copy(b.kint[:, 0:128], u), deps=[u1])
            g1 = p.c("vector", lambda em: em.tensor_tensor(red, b.kint[:, 0:128], u, ALU.is_gt), deps=[i1])
            k1 = p.c("vector", lambda em: em.tensor_sub(u, b.kint[:, 0:128], red), deps=[g1])
            r1 = p.c("vector", lambda em: em.scalar_tensor_tensor(red, u, -6.28125, ang,
                                                                  ALU.mult, ALU.add), deps=[k1])
            r2 = p.c("vector", lambda em: em.scalar_tensor_tensor(red, u, -0.0019353071795864769, red,
                                                                  ALU.mult, ALU.add), deps=[r1])
            dst = (cosb if trig else sinb)[:, 128 * t:128 * (t + 1)]
            s1 = p.c("scalar", lambda em, dst=dst: em.activation(dst, red, AF.Sin, bias=0.0, scale=1.0),
                     deps=[r2])
            prev = [s1]
    return [prev[0]]

def build_band(b, deps):
    p = b.prog
    band = carve(b, "btmp0")
    tmp = carve(b, "btmp1")
    onesw = carve(b, "bones")
    aband = carve(b, "aband")
    g0 = broadcast_scalar(b, b.g0in, b.gvec[:, 0:1], 6, deps)
    g1 = broadcast_scalar(b, b.g1in, b.gvec[:, 1:2], 6, deps)
    on = p.c("vector", lambda em: em.memset(onesw, 1.0), deps=())
    a0 = p.c("gpsimd", lambda em: em.affine_select(band, onesw, pattern=[[1, BANDW]],
             compare_op=ALU.is_ge, fill=0.0, channel_multiplier=-1, base=-1920), deps=[on])
    m0 = p.c("vector", lambda em: em.tensor_scalar_mul(band, band, b.gvec[:, 0:1]), deps=[a0] + g0)
    a1 = p.c("gpsimd", lambda em: em.affine_select(tmp, onesw, pattern=[[1, BANDW]],
             compare_op=ALU.is_ge, fill=0.0, channel_multiplier=-1, base=-896), deps=[on, m0])
    m1 = p.c("vector", lambda em: em.tensor_scalar_mul(tmp, tmp, b.gvec[:, 1:2]), deps=[a1] + g1)
    m2 = p.c("vector", lambda em: em.tensor_add(band, band, tmp), deps=[m1])
    m3 = p.c("vector", lambda em: em.tensor_scalar(aband, band, 1e30, -1e30, ALU.mult, ALU.add),
             deps=[m2])
    return [m3]

def band_slice(b, t2b, t1b, width):
    uu0 = 128 * (t1b - t2b) + 1920
    return carve(b, "aband")[:, uu0:uu0 + width]


def ep_rope(b, dst_dram, wname):
    p = b.prog
    cosb, sinb = carve(b, "cosb"), carve(b, "sinb")
    tA, tB = carve(b, "ropeA"), carve(b, "ropeB")
    qstg = carve(b, "qstg")
    if "rope_state" not in b:
        b.rope_state = {}
    state = b.rope_state

    def ep(t, og, ps, mm_tok):
        toks = []
        prev = state.get("dma", ())
        cos_t = cosb[:, 128 * t:128 * (t + 1)]
        sin_t = sinb[:, 128 * t:128 * (t + 1)]
        last = ()
        for hh in range(4):
            ph = ps[:, 128 * hh:128 * (hh + 1)]
            tAh, tBh = tA[:, 0:128], tA[:, 128:256]
            c1 = p.c("vector", lambda em, ph=ph, tAh=tAh, cos_t=cos_t: em.tensor_tensor(
                tAh, ph, cos_t, ALU.mult), deps=[mm_tok] + list(prev) + list(last))
            c2 = p.c("vector", lambda em, ph=ph, tBh=tBh, sin_t=sin_t: em.scalar_tensor_tensor(
                tBh[:, 0:64], ph[:, 64:128], -1.0, sin_t[:, 0:64], ALU.mult, ALU.mult), deps=[mm_tok, c1])
            c3 = p.c("vector", lambda em, ph=ph, tBh=tBh, sin_t=sin_t: em.tensor_tensor(
                tBh[:, 64:128], ph[:, 0:64], sin_t[:, 64:128], ALU.mult), deps=[mm_tok, c2])
            c4 = p.c("vector", lambda em, hh=hh, tAh=tAh, tBh=tBh: em.tensor_add(
                qstg[:, 128 * hh:128 * (hh + 1)], tAh, tBh), deps=[c3])
            last = [c4]
            toks.append(c4)
        dm = p.d("gpsimd", dst_dram[128 * t:128 * (t + 1), 512 * og:512 * (og + 1)],
                 qstg[:, 0:512], deps=toks, sem="rope_st")
        state["dma"] = [dm]
        return [toks[0], dm]
    return ep

def ep_vaug(b, dst_dram):
    p = b.prog
    state = {"i": 0}

    def ep(t, og, ps, mm_tok):
        i = state["i"] % 2
        state["i"] += 1
        vst = carve(b, f"vst{i}")
        prev = state.get(("dma", i), ())
        toks = []
        for hh in range(4):
            ct = p.c("scalar", lambda em, hh=hh, ps=ps, vst=vst: em.activation(
                vst[:, 129 * hh:129 * hh + 128], ps[:, 128 * hh:128 * (hh + 1)],
                AF.Copy, bias=0.0, scale=1.0), deps=[mm_tok] + list(prev))
            toks.append(ct)
        dm = p.d("gpsimd", dst_dram[128 * t:128 * (t + 1), 516 * og:516 * (og + 1)],
                 vst[:, 0:516], deps=toks, sem=f"vaug_st{i}")
        state[("dma", i)] = [dm]
        return [toks[0], dm]
    return ep

def attn0_launchB(b, xsrc, wq_bf, wk_bf, wv_bf, wprep):
    p = b.prog
    st = ln_stats(b, xsrc)
    fm_ready = ln_normalize_to_fm(b, xsrc, b.fmA, st)
    rt = build_rope_tables(b, ())
    vst0, vst1 = carve(b, "vst0"), carve(b, "vst1")
    on0 = p.c("vector", lambda em, vst0=vst0: em.memset(vst0[:, 0:516], 1.0), deps=())
    on1 = p.c("vector", lambda em, vst1=vst1: em.memset(vst1[:, 0:516], 1.0), deps=())

    def mv(wbf):
        def f(og, c, buf, prev):
            i = (og * CH + c) % 2
            return p.d("sync", buf, wbf[512 * og:512 * (og + 1), 128 * c:128 * (c + 1)],
                       deps=prev + wprep, transpose=True, sem=f"mv{i}")
        return f

    p.wait("vector", rt)
    t2_matmul(b, b.fmA, fm_ready, mv(wq_bf), ep_rope(b, b.q_dram, "q"))
    t2_matmul(b, b.fmA, fm_ready, mv(wk_bf), ep_rope(b, b.k_dram, "k"))
    t2_matmul(b, b.fmA, fm_ready, mv(wv_bf), ep_vaug(b, b.vaug_dram))
    p.wait("gpsimd", [on0, on1])
    return


def attn0_launchC(b, xsrc, xdst, wo_bf, wprep):
    p = b.prog
    bd = build_band(b, ())
    pTa = carve(b, "pTa")
    pT = carve(b, "pT")
    aband = carve(b, "aband")
    qfm = []
    for c in range(CH):
        qfm.append(p.d("scalar", b.fmA[:, TOK * c:TOK * (c + 1)],
                       b.q_dram[:, 128 * c:128 * (c + 1)], deps=(), transpose=True, sem="fm_tr"))
    kstg = carve(b, "kstg")
    kfm_wr = []
    ks_prev = list(bd)
    for c in range(CH):
        t0 = p.d("scalar", kstg[:, 0:1024], b.kt0_dram[:, 128 * c:128 * (c + 1)],
                 deps=ks_prev, transpose=True, sem="kstg")
        t1 = p.d("scalar", kstg[:, 1024:2048], b.kt1_dram[:, 128 * c:128 * (c + 1)],
                 deps=ks_prev, transpose=True, sem="kstg")
        wr = p.d("gpsimd", b.kfm_dram[128 * c:128 * (c + 1), :], kstg[:, 0:2048],
                 deps=[t0, t1], sem="kstg_wr")
        ks_prev = [wr]
        kfm_wr.append(wr)
    neg40 = b.cm40
    n40 = p.c("vector", lambda em: em.memset(neg40[:, 0:1], -C_SHIFT), deps=())
    kh = carve(b, "kh")
    vh = carve(b, "vh")
    kh_guard = []
    vh_guard = []
    pta_guard = {}
    pt_guard = [list(bd)]
    ao_state = {}
    ao_cnt = [0]
    fill = b.get("fill")
    for h in range(NH):
        if fill is not None:
            fill.pump(6)
        kl = p.d("sync", kh[:, 0:2048], b.kfm_dram[128 * h:128 * (h + 1), :],
                 deps=kh_guard + kfm_wr, sem="kh")
        v0 = p.d("sync", vh.rearrange("p (a c) -> p a c", c=129)[:, 0:8, :],
                 b.vaug0_dram.rearrange("(a p) c -> p a c", p=128)[:, :, 129 * h:129 * (h + 1)],
                 deps=vh_guard, sem="vh")
        v1 = p.d("sync", vh.rearrange("p (a c) -> p a c", c=129)[:, 8:16, :],
                 b.vaug1_dram.rearrange("(a p) c -> p a c", p=128)[:, :, 129 * h:129 * (h + 1)],
                 deps=vh_guard, sem="vh")
        kh_guard = []
        vh_guard = []
        cp_toks = []
        for half in range(2):
            ex_toks = []
            for t2q in range(8):
                t2b = 8 * half + t2q
                for t1h in range(2):
                    bank = 4 + (t2q * 2 + t1h) % 4
                    mm = p.c("tensor", lambda em, h=h, t2b=t2b, t1h=t1h, bank=bank: em.matmul(
                        b.psum[bank][:, 0:512], kh[:, 128 * t2b:128 * (t2b + 1)],
                        b.fmA[:, TOK * h + 512 * t1h: TOK * h + 512 * (t1h + 1)],
                        start=True, stop=True),
                        deps=[kl] + qfm + b.bank_guard[bank])
                    md = p.c("vector", lambda em, bank=bank, bs=band_slice(b, t2b, 4 * t1h, 512):
                             em.tensor_add(b.psum[bank][:, 0:512], b.psum[bank][:, 0:512], bs),
                             deps=[mm] + bd)
                    ex = p.c("scalar", lambda em, t2q=t2q, t1h=t1h, bank=bank: em.activation(
                        pTa[:, 1024 * t2q + 512 * t1h: 1024 * t2q + 512 * (t1h + 1)],
                        b.psum[bank][:, 0:512], AF.Exp, bias=neg40[:, 0:1], scale=INV_SQRT_DH),
                        deps=[md, n40] + pta_guard.get((t2q, t1h), []))
                    b.bank_guard[bank] = [ex]
                    ex_toks.append(ex)
            cp = p.d("gpsimd", pT[:, 8192 * half:8192 * (half + 1)], pTa[:, 0:8192],
                     deps=ex_toks + pt_guard[0], sem="ptcp")
            for t2q in range(8):
                for t1h in range(2):
                    pta_guard[(t2q, t1h)] = [cp]
            cp_toks.append(cp)
        av_mms = []
        for pas in range(2):
            for t1bq in range(4):
                t1b = 4 * pas + t1bq
                bank = t1bq
                mms = []
                for t2b in range(NKB):
                    deps = [cp_toks[t2b // 8], v0, v1]
                    if t2b == 0:
                        deps = deps + b.bank_guard[bank]
                    mm = p.c("tensor", lambda em, t1b=t1b, t2b=t2b, bank=bank: em.matmul(
                        b.psum[bank][:, 0:129],
                        pT[:, 1024 * t2b + 128 * t1b: 1024 * t2b + 128 * (t1b + 1)],
                        vh[:, 129 * t2b:129 * (t2b + 1)],
                        start=(t2b == 0), stop=(t2b == NKB - 1)), deps=deps)
                    mms.append(mm)
                av_mms.append(mms[-1])
                i = ao_cnt[0] % 2
                ao_cnt[0] += 1
                prev = ao_state.get(i, ())
                rc0 = p.c("scalar", lambda em, bank=bank: em.activation(
                    b.rcpl[:, 1:2], b.psum[bank][:, 128:129], AF.Ln, bias=0.0, scale=1.0),
                    deps=[mms[-1]] + list(prev) + list(ao_state.get("rc", ())))
                rc = p.c("scalar", lambda em: em.activation(
                    b.rcpl[:, 0:1], b.rcpl[:, 1:2], AF.Exp, bias=0.0, scale=-1.0),
                    deps=[rc0] + list(ao_state.get("sc", ())))
                sc = p.c("vector", lambda em, bank=bank, i=i: em.tensor_scalar_mul(
                    b.aostg[i][:, 0:128], b.psum[bank][:, 0:128], b.rcpl[:, 0:1]), deps=[rc])
                dm = p.d("gpsimd", b.ao_dram[128 * t1b:128 * (t1b + 1), 128 * h:128 * (h + 1)],
                         b.aostg[i][:, 0:128], deps=[sc], sem=f"ao_st{i}")
                b.bank_guard[bank] = [sc]
                ao_state[i] = [dm]
                ao_state["sc"] = [sc]
                ao_state["rc"] = [rc]
                kh_guard.append(mms[-1])
                vh_guard.append(mms[-1])
        pt_guard[0] = av_mms
    if b.get("SKIP_WO"):
        return
    ao_done = p.barrier_token()
    aofm = []
    for c in range(CH):
        aofm.append(p.d("scalar", b.fmA[:, TOK * c:TOK * (c + 1)],
                        b.ao_dram[:, 128 * c:128 * (c + 1)], deps=ao_done, transpose=True, sem="fm_tr"))

    def mv_wo(og, c, buf, prev):
        i = (og * CH + c) % 2
        return p.d("sync", buf, wo_bf[512 * og:512 * (og + 1), 128 * c:128 * (c + 1)],
                   deps=prev + wprep, transpose=True, sem=f"mv{i}")

    ep_state = {}

    def ep_wo(t, og, ps, mm_tok):
        i = t % 2
        prev = ep_state.get(i, ())
        ld = p.d("gpsimd", b.epi_x[i][:, 0:512],
                 xsrc[128 * t:128 * (t + 1), 512 * og:512 * (og + 1)], deps=prev, sem=f"epix{i}")
        a1 = p.c("vector", lambda em, i=i, ps=ps: em.tensor_add(
            b.epi_y[:, 0:512], ps, b.epi_x[i][:, 0:512]),
            deps=[ld, mm_tok] + list(ep_state.get("y", ())))
        stq = p.d("gpsimd", xdst[128 * t:128 * (t + 1), 512 * og:512 * (og + 1)],
                  b.epi_y[:, 0:512], deps=[a1], sem="epi_st")
        ep_state[i] = [ld, a1, stq]
        ep_state["y"] = [stq]
        return [a1, stq]

    t2_matmul(b, b.fmA, aofm, mv_wo, ep_wo)
    return


def alloc_attn_buffers(b, nc, es):
    sb = lambda name, shape, dt: es.enter_context(nc.sbuf_tensor(name, shape, dt)).ap()
    b.sc11 = sb("sc11", [1, 1], F32)
    b.onesf1 = sb("onesf1", [1, 128], F32)
    b.poffv = sb("poffv", [128, 1], F32)
    b.posv = sb("posv", [128, 2], F32)
    b.gvec = sb("gvec", [128, 2], F32)
    b.kint = sb("kint", [128, 128], I32)
    b.gint = None
    b.cm40 = sb("cm40", [128, 1], F32)
    b.rcpl = sb("rcpl", [128, 2], F32)
    b.aostg = [sb(f"aostg{i}", [128, 128], BF16) for i in range(2)]


def build_launchB():
    nc = bass.Bass("TRN2", target_bir_lowering=False, debug=False)
    es = ExitStack()
    b = new_builder(nc)
    b.xin = nc.dram_tensor("xin", [TOK, D], F32, kind="ExternalInput").ap()
    b.wq = nc.dram_tensor("wq", [D, D], F32, kind="ExternalInput").ap()
    b.wk = nc.dram_tensor("wk", [D, D], F32, kind="ExternalInput").ap()
    b.wv = nc.dram_tensor("wv", [D, D], F32, kind="ExternalInput").ap()
    b.poff = nc.dram_tensor("poff", [1, 1], F32, kind="ExternalInput").ap()
    b.q_dram = nc.dram_tensor("q_out", [TOK, D], BF16, kind="ExternalOutput").ap()
    b.k_dram = nc.dram_tensor("k_out", [TOK, D], BF16, kind="ExternalOutput").ap()
    b.vaug_dram = nc.dram_tensor("v_out", [TOK, 4128], BF16, kind="ExternalOutput").ap()
    b.wq_bf = nc.dram_tensor("wq_bf", [D, D], BF16, kind="Internal").ap()
    b.wk_bf = nc.dram_tensor("wk_bf", [D, D], BF16, kind="Internal").ap()
    b.wv_bf = nc.dram_tensor("wv_bf", [D, D], BF16, kind="Internal").ap()
    b.ln_dram = nc.dram_tensor("ln_dram", [TOK, D], BF16, kind="Internal").ap()
    alloc_buffers(b, nc, es)
    alloc_attn_buffers(b, nc, es)
    setup_carves(b)
    wt = (prep_cast_bf16(b, b.wq, b.wq_bf, "wq") + prep_cast_bf16(b, b.wk, b.wk_bf, "wk")
          + prep_cast_bf16(b, b.wv, b.wv_bf, "wv"))
    attn0_launchB(b, b.xin, b.wq_bf, b.wk_bf, b.wv_bf, wt)
    b.prog.wait("gpsimd", b.prog.barrier_token())
    build_block(nc, b.prog, es)
    es.close()
    return nc


def build_launchC():
    nc = bass.Bass("TRN2", target_bir_lowering=False, debug=False)
    es = ExitStack()
    b = new_builder(nc)
    b.xin = nc.dram_tensor("xin", [TOK, D], F32, kind="ExternalInput").ap()
    b.wo = nc.dram_tensor("wo", [D, D], F32, kind="ExternalInput").ap()
    b.q_dram = nc.dram_tensor("q_in", [TOK, D], BF16, kind="ExternalInput").ap()
    b.kt0_dram = nc.dram_tensor("kt0", [TOK, D], BF16, kind="ExternalInput").ap()
    b.kt1_dram = nc.dram_tensor("kt1", [TOK, D], BF16, kind="ExternalInput").ap()
    b.vaug0_dram = nc.dram_tensor("vaug0", [TOK, 4128], BF16, kind="ExternalInput").ap()
    b.vaug1_dram = nc.dram_tensor("vaug1", [TOK, 4128], BF16, kind="ExternalInput").ap()
    b.g0in = nc.dram_tensor("g0in", [1, 1], F32, kind="ExternalInput").ap()
    b.g1in = nc.dram_tensor("g1in", [1, 1], F32, kind="ExternalInput").ap()
    b.xout = nc.dram_tensor("xout", [TOK, D], F32, kind="ExternalOutput").ap()
    b.wo_bf = nc.dram_tensor("wo_bf", [D, D], BF16, kind="Internal").ap()
    b.ao_dram = nc.dram_tensor("ao_dram", [TOK, D], BF16, kind="ExternalOutput").ap()
    b.kfm_dram = nc.dram_tensor("kfm_dram", [D, TK], BF16, kind="Internal").ap()
    alloc_buffers(b, nc, es)
    alloc_attn_buffers(b, nc, es)
    setup_carves(b)
    import os
    b.HEAD_BARRIER = bool(int(os.environ.get("KERN_HEAD_BARRIER", "0")))
    b.STAGE_BARRIER = bool(int(os.environ.get("KERN_STAGE_BARRIER", "0")))
    b.AV_ALLGATES = bool(int(os.environ.get("KERN_AV_ALLGATES", "0")))
    b.SKIP_WO = bool(int(os.environ.get("KERN_SKIP_WO", "0")))
    b.PT_FULLGUARD = bool(int(os.environ.get("KERN_PT_FULLGUARD", "0")))
    b.KH_SERIAL = bool(int(os.environ.get("KERN_KH_SERIAL", "0")))
    b.GATE_POOL = bool(int(os.environ.get("KERN_GATE_POOL", "0")))
    b.SKIP_PREP = b.SKIP_WO
    wt = [] if b.SKIP_PREP else prep_cast_bf16(b, b.wo, b.wo_bf, "wo")
    attn0_launchC(b, b.xin, b.xout, b.wo_bf, wt)
    b.prog.wait("gpsimd", b.prog.barrier_token())
    build_block(nc, b.prog, es)
    es.close()
    return nc


def run_attn0(x_slices, wq, wk, wv, wo):
    n = len(x_slices)
    ncB = build_launchB()
    in_mapsB = []
    for i in range(n):
        poff = np.array([[1024.0 * (i % 2)]], dtype=np.float32)
        in_mapsB.append({"xin": x_slices[i], "wq": wq, "wk": wk, "wv": wv, "poff": poff})
    resB = run_bass_kernel_spmd(ncB, in_mapsB, core_ids=list(range(n)))
    ncC = build_launchC()
    in_mapsC = []
    for i in range(n):
        pair = i // 2
        e, o = 2 * pair, 2 * pair + 1
        g0 = np.array([[1.0 if i % 2 == 0 else 0.0]], dtype=np.float32)
        g1 = np.array([[0.0 if i % 2 == 0 else 1.0]], dtype=np.float32)
        in_mapsC.append({
            "xin": x_slices[i], "wo": wo,
            "q_in": resB.results[i]["q_out"],
            "kt0": resB.results[e]["k_out"], "kt1": resB.results[o]["k_out"],
            "vaug0": resB.results[e]["v_out"], "vaug1": resB.results[o]["v_out"],
            "g0in": g0, "g1in": g1})
    resC = run_bass_kernel_spmd(ncC, in_mapsC, core_ids=list(range(n)))
    return [r["xout"] for r in resC.results]





def e4m3_emulate(b, p, y_f32, out_bf, bufs, deps):
    iview = y_f32.bitcast(I32)
    e, fA, fB = bufs["i0"], bufs["f0"], bufs["f1"]
    t1 = p.c("vector", lambda em: em.tensor_single_scalar(e, iview, 0x7F800000, ALU.bitwise_and), deps=deps)
    t2 = p.c("vector", lambda em: em.tensor_scalar_max(e, e, 1015021568), deps=[t1])
    t3 = p.c("vector", lambda em: em.tensor_scalar_add(e, e, -25165824), deps=[t2])
    step = e.bitcast(F32)
    t4 = p.c("vector", lambda em: em.reciprocal(fA, step), deps=[t3])
    t5 = p.c("vector", lambda em: em.tensor_mul(fB, y_f32, fA), deps=[t4])
    t6 = p.c("vector", lambda em: em.tensor_scalar_add(fB, fB, 0.5), deps=[t5])
    I = fA.bitcast(I32)
    t7 = p.c("vector", lambda em: em.tensor_copy(I, fB), deps=[t6])
    g = y_f32
    t8 = p.c("vector", lambda em: em.tensor_tensor(g, I, fB, ALU.is_gt), deps=[t7])
    t9 = p.c("vector", lambda em: em.tensor_sub(fB, I, g), deps=[t8])
    t10 = p.c("vector", lambda em: em.tensor_mul(out_bf, fB, step), deps=[t9])
    return [t10]


def cross_partition_max(b, p, col_ap, out11, deps, tag):
    t = p.d("gpsimd", b.xrow[0:1, 0:128], col_ap,
            deps=list(deps) + list(b.get("xrow_prev", ())), sem="xpm_" + tag,
            allow_slow_non_contiguous=True)
    r = p.c("vector", lambda em: em.reduce_max(out11, b.xrow[0:1, 0:128], axis=AX.X), deps=[t])
    b.xrow_prev = [r]
    return [r]


def prep_w8(b, wsrc_f32, dst_bf):
    toks = []
    for _ in prep_w8_gen(b, wsrc_f32, dst_bf, toks):
        pass
    return toks


def prep_w8_gen(b, wsrc_f32, dst_bf, out_toks):
    p = b.prog
    H = D // 2
    stg = b.get("pstage_w8", b.scrapH)
    mx = b.wd_m8[:, 0:1]
    z = p.c("vector", lambda em: em.memset(mx, 0.0), deps=())
    run = [z]
    for k in range(CH):
        for h in range(2):
            buf = b.wchunk_f32
            prev = b.wprev.get(("wcast",), ())
            ld = p.d("sync", buf[:, 0:H], wsrc_f32[128 * k:128 * (k + 1), H * h:H * (h + 1)],
                     deps=list(prev) + run, sem="wp_ld")
            r1 = p.c("vector", lambda em, buf=buf: em.reduce_max(
                b.wd_sc[:, 0:1], buf[:, 0:H], axis=AX.X, apply_absolute_value=True), deps=[ld])
            r2 = p.c("vector", lambda em: em.tensor_max(mx, mx, b.wd_sc[:, 0:1]), deps=[r1])
            run = [r2]
            b.wprev[("wcast",)] = [r1]
        yield
    cm = cross_partition_max(b, p, mx, b.s11[0:1, 0:1], run, "w8")
    c1 = p.c("vector", lambda em: em.tensor_scalar_max(b.s11[0:1, 0:1], b.s11[0:1, 0:1], 1e-12), deps=cm)
    c2 = p.c("vector", lambda em: em.reciprocal(b.s11[0:1, 1:2], b.s11[0:1, 0:1]), deps=[c1])
    c3 = p.c("vector", lambda em: em.tensor_scalar_mul(b.s11[0:1, 1:2], b.s11[0:1, 1:2], 448.0), deps=[c2])
    toks = out_toks
    bufs = {"i0": b.wd_i32[:, 0:H], "f0": b.wd_lo.bitcast(F32)[:, 0:H],
            "f1": b.wchunk_bf.bitcast(F32)[:, 0:H]}
    yh = b.lnchunk.bitcast(F32)[:, 0:H]
    for k in range(CH):
        for h in range(2):
            prev = b.wprev.get(("wcast",), ())
            ld = p.d("sync", b.wchunk_f32[:, 0:H], wsrc_f32[128 * k:128 * (k + 1), H * h:H * (h + 1)],
                     deps=list(prev) + [c3] + toks[-1:], sem="wp_ld")
            if k == 0 and h == 0:
                bc = broadcast_scalar_sb(b, b.s11[0:1, 1:2], b.s448w[:, 0:1], 7, [c3])
                b.wprev[("s448w",)] = bc
            y1 = p.c("vector", lambda em: em.tensor_scalar_mul(
                yh, b.wchunk_f32[:, 0:H], b.s448w[:, 0:1]), deps=[ld] + b.wprev[("s448w",)])
            qt = e4m3_emulate(b, p, yh, stg, bufs, [y1])
            st = p.d("gpsimd", dst_bf[128 * k:128 * (k + 1), H * h:H * (h + 1)],
                     stg, deps=qt, sem="wp_st")
            b.wprev[("wcast",)] = [st] + qt
            b.lnprev[0] = list(qt) + [st]
            toks.append(st)
        yield


def broadcast_scalar_sb(b, src11_sb, dst_col, bank, deps):
    p = b.prog
    on = b.get("onesf1_tok")
    if on is None:
        on = [p.c("vector", lambda em: em.memset(b.onesf1[0:1, 0:128], 1.0), deps=())]
        b.onesf1_tok = on
    mm = p.c("tensor", lambda em, bank=bank: em.matmul(
        b.psum[bank][:, 0:1], b.onesf1[0:1, 0:128], src11_sb,
        start=True, stop=True), deps=list(deps) + on + b.bank_guard[bank])
    cp = p.c("vector", lambda em, bank=bank: em.tensor_copy(dst_col, b.psum[bank][:, 0:1]), deps=[mm])
    b.bank_guard[bank] = [cp]
    return [cp]


def ln_absmax(b, xsrc, stat_toks):
    p = b.prog
    H = D // 2
    mx = b.wd_m8[:, 1:2]
    yh = b.lnchunk.bitcast(F32)[:, 0:H]
    z = p.c("vector", lambda em: em.memset(mx, 0.0), deps=())
    run = [z]
    for t in range(TT):
        for h in range(2):
            xc = b.xchunk
            prev = b.lnprev.get(0, ())
            ld = p.d("sync", xc[:, 0:H], xsrc[128 * t:128 * (t + 1), H * h:H * (h + 1)],
                     deps=prev, sem="x_ld")
            ct = p.c("scalar", lambda em, t=t: em.activation(
                yh, xc[:, 0:H], AF.Identity, bias=b.nmr[:, t:t + 1], scale=b.rstd[:, t:t + 1]),
                deps=[ld] + stat_toks + run)
            r1 = p.c("vector", lambda em: em.reduce_max(
                b.wd_sc[:, 1:2], yh, axis=AX.X, apply_absolute_value=True), deps=[ct])
            r2 = p.c("vector", lambda em: em.tensor_max(mx, mx, b.wd_sc[:, 1:2]), deps=[r1])
            run = [r2]
            b.lnprev[0] = [ct, r1]
    return cross_partition_max(b, p, mx, b.s11[0:1, 2:3], run, "lnx")


def l1ffn_main(b, xsrc, xdst, amax8_in, w8_toks, st=None, amax_w=8, amax_deps=()):
    p = b.prog
    if st is None:
        st = ln_stats(b, xsrc)
    la = p.d("sync", b.xrow[0:1, 0:amax_w], amax8_in[0:1, 0:amax_w],
             deps=list(amax_deps) + list(b.get("xrow_prev", ())), sem="amax8")
    a1 = p.c("vector", lambda em, amax_w=amax_w: em.reduce_max(
        b.s11[0:1, 2:3], b.xrow[0:1, 0:amax_w], axis=AX.X), deps=[la])
    b.xrow_prev = [a1]
    a2 = p.c("vector", lambda em: em.tensor_scalar_max(b.s11[0:1, 2:3], b.s11[0:1, 2:3], 1e-12), deps=[a1])
    a3 = p.c("vector", lambda em: em.reciprocal(b.s11[0:1, 3:4], b.s11[0:1, 2:3]), deps=[a2])
    a4 = p.c("vector", lambda em: em.tensor_scalar_mul(b.s11[0:1, 3:4], b.s11[0:1, 3:4], 448.0), deps=[a3])
    bcx = broadcast_scalar_sb(b, b.s11[0:1, 3:4], b.s448x[:, 0:1], 7, [a4])
    s1 = p.c("vector", lambda em: em.tensor_scalar_mul(b.rs2[:, 0:TT], b.rstd[:, 0:TT], b.s448x[:, 0:1]),
             deps=st + bcx)
    s2 = p.c("vector", lambda em: em.tensor_scalar_mul(b.nb2[:, 0:TT], b.nmr[:, 0:TT], b.s448x[:, 0:1]),
             deps=st + bcx)
    H = D // 2
    bufs = {"i0": b.wd_i32[:, 0:H], "f0": b.wd_lo.bitcast(F32)[:, 0:H],
            "f1": b.wchunk_bf.bitcast(F32)[:, 0:H]}
    yh = b.lnchunk.bitcast(F32)[:, 0:H]
    wr = []
    for t in range(TT):
        for h in range(2):
            xc = b.xchunk
            prev = b.lnprev.get(0, ())
            ld = p.d("sync", xc[:, 0:H], xsrc[128 * t:128 * (t + 1), H * h:H * (h + 1)],
                     deps=prev, sem="x_ld")
            y1 = p.c("scalar", lambda em, t=t: em.activation(
                yh, xc[:, 0:H], AF.Identity, bias=b.nb2[:, t:t + 1], scale=b.rs2[:, t:t + 1]),
                deps=[ld, s1, s2])
            qt = e4m3_emulate(b, p, yh, b.scrapH, bufs, [y1])
            stq = p.d("gpsimd", b.ln_dram[128 * t:128 * (t + 1), H * h:H * (h + 1)],
                      b.scrapH, deps=qt, sem="ln_st")
            b.lnprev[0] = [y1] + qt + [stq]
            wr.append(stq)
    fm = []
    for c in range(CH):
        fm.append(p.d("scalar", b.fmA[:, TOK * c:TOK * (c + 1)],
                      b.ln_dram[:, 128 * c:128 * (c + 1)], deps=wr, transpose=True, sem="fm_tr"))
    cw1 = p.c("vector", lambda em: em.tensor_mul(b.s11[0:1, 4:5], b.s11[0:1, 3:4], b.s11[0:1, 1:2]),
              deps=[a4] + w8_toks)
    sv1 = p.c("vector", lambda em: em.reciprocal(b.s11[0:1, 5:6], b.s11[0:1, 4:5]), deps=[cw1])
    sv2 = p.c("vector", lambda em: em.tensor_scalar_mul(b.s11[0:1, 5:6], b.s11[0:1, 5:6], 0.5), deps=[sv1])
    hs = broadcast_scalar_sb(b, b.s11[0:1, 5:6], b.halfS[:, 0:1], 6, [sv2])
    bb = load_bias_row_scaled(b, b.ffn1_bias, [cw1])
    def mv_w8(og, c, buf, prev):
        i = (og * CH + c) % 2
        return p.d("sync", buf, b.w8_bf[512 * og:512 * (og + 1), 128 * c:128 * (c + 1)],
                   deps=prev + w8_toks, transpose=True, sem=f"mv{i}")
    ep_state = {}
    def ep(t, og, ps, mm_tok):
        i = t % 2
        prev = ep_state.get(i, ())
        ld = p.d("gpsimd", b.epi_x[i][:, 0:512],
                 xsrc[128 * t:128 * (t + 1), 512 * og:512 * (og + 1)], deps=prev, sem=f"epix{i}")
        a1_ = p.c("vector", lambda em, i=i, ps=ps: em.scalar_tensor_tensor(
            b.epi_y[:, 0:512], ps, b.halfS[:, 0:1], b.epi_x[i][:, 0:512], ALU.mult, ALU.add),
            deps=[ld, mm_tok] + hs + list(ep_state.get("y", ())))
        stq = p.d("gpsimd", xdst[128 * t:128 * (t + 1), 512 * og:512 * (og + 1)],
                  b.epi_y[:, 0:512], deps=[a1_], sem="epi_st")
        ep_state[i] = [ld, a1_, stq]
        ep_state["y"] = [stq]
        return [a1_, stq]
    t2_matmul(b, b.fmA, fm, mv_w8, ep, bias_row=(b.browbf, bb))
    return


def load_bias_row_scaled(b, bias_row_dram, scale_tok):
    p = b.prog
    scale_tok = list(scale_tok) + list(b.get("zero11_tok", ()))
    on = b.get("onesbf_tok")
    if on is None:
        on = [p.c("vector", lambda em: em.memset(b.onesbf[0:1, 0:128], 1.0), deps=())]
        b.onesbf_tok = on
    toks = list(on)
    prev = ()
    for g in range(D // 512):
        ld = p.d("sync", b.browf[0:1, 0:512], bias_row_dram[None, 512 * g:512 * (g + 1)],
                 deps=list(prev), sem="bias_ld")
        ct = p.c("scalar", lambda em, g=g: em.activation(
            b.browbf[0:1, 512 * g:512 * (g + 1)], b.browf[0:1, 0:512], AF.Identity,
            bias=b.zero11[0:1, 0:1], scale=b.s11[0:1, 4:5]), deps=[ld] + list(scale_tok))
        prev = [ct]
        toks.append(ct)
    return toks


def alloc_l1_buffers(b, nc, es):
    sb = lambda name, shape, dt: es.enter_context(nc.sbuf_tensor(name, shape, dt)).ap()
    if "onesf1" not in b:
        b.onesf1 = sb("onesf1", [1, 128], F32)
    b.s11 = sb("s11", [1, 8], F32)
    b.xrow = sb("xrow", [1, 128], F32)
    b.s448w = sb("s448w", [128, 1], F32)
    b.s448x = sb("s448x", [128, 1], F32)
    b.halfS = sb("halfS", [128, 1], F32)
    b.rs2 = sb("rs2", [128, TT], F32)
    b.nb2 = sb("nb2", [128, TT], F32)
    b.zero11 = sb("zero11", [1, 1], F32)
    b.scrapH = b.tmB[:, 0:D // 2]


def build_launchD1():
    nc = bass.Bass("TRN2", target_bir_lowering=False, debug=False)
    es = ExitStack()
    b = new_builder(nc)
    b.xin = nc.dram_tensor("xin", [TOK, D], F32, kind="ExternalInput").ap()
    b.amax_out = nc.dram_tensor("amax_out", [1, 1], F32, kind="ExternalOutput").ap()
    alloc_buffers(b, nc, es)
    alloc_l1_buffers(b, nc, es)
    st = ln_stats(b, b.xin)
    am = ln_absmax(b, b.xin, st)
    p = b.prog
    p.d("gpsimd", b.amax_out[0:1, 0:1], b.s11[0:1, 2:3], deps=am, sem="amx_st")
    p.wait("gpsimd", p.barrier_token())
    build_block(nc, b.prog, es)
    es.close()
    return nc


def build_launchD2():
    nc = bass.Bass("TRN2", target_bir_lowering=False, debug=False)
    es = ExitStack()
    b = new_builder(nc)
    b.xin = nc.dram_tensor("xin", [TOK, D], F32, kind="ExternalInput").ap()
    b.wffn = nc.dram_tensor("wffn", [D, D], F32, kind="ExternalInput").ap()
    b.ffn1_bias = nc.dram_tensor("fbias1", [D], F32, kind="ExternalInput").ap()
    b.amax8 = nc.dram_tensor("amax8", [1, 8], F32, kind="ExternalInput").ap()
    b.xout = nc.dram_tensor("xout", [TOK, D], F32, kind="ExternalOutput").ap()
    b.w8_bf = nc.dram_tensor("w8_bf", [D, D], BF16, kind="Internal").ap()
    b.ln_dram = nc.dram_tensor("ln_dram", [TOK, D], BF16, kind="Internal").ap()
    alloc_buffers(b, nc, es)
    alloc_l1_buffers(b, nc, es)
    p = b.prog
    z = p.c("vector", lambda em: em.memset(b.zero11[0:1, 0:1], 0.0), deps=())
    b.zero11_tok = [z]
    wt = prep_w8(b, b.wffn, b.w8_bf)
    l1ffn_main(b, b.xin, b.xout, b.amax8, wt + [z])
    p.wait("gpsimd", p.barrier_token())
    build_block(nc, b.prog, es)
    es.close()
    return nc


def run_l1ffn(x_slices, weight, bias):
    n = len(x_slices)
    nc1 = build_launchD1()
    res1 = run_bass_kernel_spmd(nc1, [{"xin": x_slices[i]} for i in range(n)], core_ids=list(range(n)))
    amax8 = np.zeros((1, 8), np.float32)
    for i in range(n):
        amax8[0, i] = res1.results[i]["amax_out"][0, 0]
    nc2 = build_launchD2()
    im = [{"xin": x_slices[i], "wffn": weight, "fbias1": bias, "amax8": amax8} for i in range(n)]
    res2 = run_bass_kernel_spmd(nc2, im, core_ids=list(range(n)))
    return [r["xout"] for r in res2.results]





LN127 = 4.844187086458591

def prep_w4(b, wsrc_f32, dst_bf, tag):
    toks = []
    for _ in prep_w4_gen(b, wsrc_f32, dst_bf, tag, toks):
        pass
    return toks


def prep_w4_gen(b, wsrc_f32, dst_bf, tag, out_toks):
    p = b.prog
    H = D // 2
    stg = b.get("pstage_w4", b.scrapH)
    toks = out_toks
    for k in range(CH):
        prev = list(b.wprev.get(("wcast",), ()))
        ld = p.d("sync", b.xchunk[:, 0:D], wsrc_f32[128 * k:128 * (k + 1), :],
                 deps=prev + list(b.wprev.get(("w4x",), ())), sem="wp_ld")
        r1 = p.c("vector", lambda em: em.reduce_max(
            b.wd_sc[:, 2:3], b.xchunk[:, 0:D], axis=AX.X, apply_absolute_value=True), deps=[ld])
        s1 = p.c("vector", lambda em: em.tensor_scalar(
            b.wd_sc[:, 3:4], b.wd_sc[:, 2:3], 1.0 / 7.0, 1e-8, ALU.mult, ALU.max), deps=[r1])
        s2 = p.c("vector", lambda em: em.reciprocal(b.wd_sc[:, 4:5], b.wd_sc[:, 3:4]), deps=[s1])
        s3 = p.c("vector", lambda em: em.tensor_scalar_mul(b.wd_sc[:, 5:6], b.wd_sc[:, 3:4], -16.0), deps=[s1])
        hts = []
        for h in range(2):
            u = b.lnchunk.bitcast(F32)[:, 0:H]
            I = b.wd_i32[:, 0:H]
            g = b.wd_lo.bitcast(F32)[:, 0:H]
            u1 = p.c("vector", lambda em, h=h: em.tensor_scalar(
                u, b.xchunk[:, H * h:H * (h + 1)], b.wd_sc[:, 4:5], 16.5, ALU.mult, ALU.add),
                deps=[s2] + list(b.wprev.get(("w4q",), ())))
            i1 = p.c("vector", lambda em: em.tensor_copy(I, u), deps=[u1])
            g1 = p.c("vector", lambda em: em.tensor_tensor(g, I, u, ALU.is_gt), deps=[i1])
            f1 = p.c("vector", lambda em: em.tensor_sub(u, I, g), deps=[g1])
            o1 = p.c("vector", lambda em, h=h, stg=stg: em.tensor_scalar(
                stg, u, b.wd_sc[:, 3:4], b.wd_sc[:, 5:6], ALU.mult, ALU.add), deps=[f1])
            st = p.d("gpsimd", dst_bf[128 * k:128 * (k + 1), H * h:H * (h + 1)],
                     stg, deps=[o1], sem="wp_st")
            b.wprev[("w4q",)] = [o1, st]
            hts.append(st)
        b.wprev[("w4x",)] = [u1]
        b.lnprev[0] = list(b.lnprev.get(0, ()))[-2:] + [u1, r1]
        b.lnprev["sq"] = [o1, st]
        toks += hts
        yield


def ln_stats_minmax(b, xsrc):
    p = b.prog
    done = []
    for t in range(TT):
        xc = b.xchunk
        prev = b.lnprev.get(0, ())
        ld = p.d("sync", xc[:], xsrc[128 * t:128 * (t + 1), :], deps=prev, sem="x_ld")
        s1 = p.c("vector", lambda em, t=t, xc=xc: em.reduce_sum(
            b.ssum[:, t:t + 1], xc[:], axis=AX.X), deps=[ld])
        s2 = p.c("scalar", lambda em, t=t, xc=xc: em.activation(
            b.lnchunk[:, 0:D], xc[:], AF.Square, bias=0.0, scale=1.0,
            accum_out=b.ssq[:, t:t + 1]), deps=[ld] + list(b.lnprev.get("sq", ())))
        s3 = p.c("vector", lambda em, t=t, xc=xc: em.reduce_max(
            b.rmax[:, t:t + 1], xc[:], axis=AX.X), deps=[ld])
        s4 = p.c("vector", lambda em, t=t, xc=xc: em.tensor_reduce(
            b.rmin[:, t:t + 1], xc[:], axis=AX.X, op=ALU.min), deps=[ld])
        b.lnprev[0] = [s1, s2, s3, s4]
        b.lnprev["sq"] = [s2]
        done += [s1, s2, s3, s4]
    t0 = p.c("vector", lambda em: em.tensor_scalar_mul(b.mean[:], b.ssum[:, 0:TT], 1.0 / D), deps=done)
    t1 = p.c("vector", lambda em: em.tensor_scalar_mul(b.var[:], b.ssq[:, 0:TT], 1.0 / D), deps=done)
    t2 = p.c("vector", lambda em: em.tensor_tensor(b.msq[:], b.mean[:], b.mean[:], ALU.mult), deps=[t0])
    t3 = p.c("vector", lambda em: em.tensor_sub(b.var[:], b.var[:], b.msq[:]), deps=[t1, t2])
    t3b = p.c("vector", lambda em: em.tensor_scalar_add(b.var[:], b.var[:], 1e-5), deps=[t3])
    t4 = p.c("scalar", lambda em: em.activation(b.std[:], b.var[:], AF.Sqrt, bias=0.0, scale=1.0), deps=[t3b])
    t5 = p.c("vector", lambda em: em.reciprocal(b.rstd[:], b.std[:]), deps=[t4])
    t6 = p.c("vector", lambda em: em.tensor_tensor(b.nmr[:], b.mean[:], b.rstd[:], ALU.mult), deps=[t5])
    t7 = p.c("vector", lambda em: em.tensor_scalar_mul(b.nmr[:], b.nmr[:], -1.0), deps=[t6])
    m1 = p.c("vector", lambda em: em.tensor_sub(b.rmax[:, 0:TT], b.rmax[:, 0:TT], b.mean[:]), deps=[t0] + done)
    m2 = p.c("vector", lambda em: em.tensor_sub(b.rmin[:, 0:TT], b.mean[:], b.rmin[:, 0:TT]), deps=[m1])
    m3 = p.c("vector", lambda em: em.tensor_max(b.rmax[:, 0:TT], b.rmax[:, 0:TT], b.rmin[:, 0:TT]), deps=[m2])
    m4 = p.c("vector", lambda em: em.tensor_mul(b.sx[:, 0:TT], b.rmax[:, 0:TT], b.rstd[:]), deps=[m3, t5])
    m5 = p.c("vector", lambda em: em.tensor_scalar(b.sx[:, 0:TT], b.sx[:, 0:TT], 1.0 / 127.0, 1e-8,
                                                   ALU.mult, ALU.max), deps=[m4])
    m6 = p.c("vector", lambda em: em.reciprocal(b.rsx[:, 0:TT], b.sx[:, 0:TT]), deps=[m5])
    return [t5, t7, m5, m6]


def quantize_x_to_fm(b, xsrc, stat_toks):
    p = b.prog
    H = D // 2
    wr = []
    for t in range(TT):
        for h in range(2):
            xc = b.xchunk
            u = b.lnchunk.bitcast(F32)[:, 0:H]
            I = b.wd_i32[:, 0:H]
            g = b.wd_lo.bitcast(F32)[:, 0:H]
            prev = b.lnprev.get(0, ())
            ld = p.d("sync", xc[:, 0:H], xsrc[128 * t:128 * (t + 1), H * h:H * (h + 1)],
                     deps=prev, sem="x_ld")
            c1 = p.c("vector", lambda em, t=t: em.tensor_mul(
                b.wk1[:, 0:1], b.rstd[:, t:t + 1], b.rsx[:, t:t + 1]), deps=stat_toks + prev)
            c2 = p.c("vector", lambda em, t=t: em.tensor_scalar(
                b.wk1[:, 1:2], b.nmr[:, t:t + 1], b.rsx[:, t:t + 1], 256.5, ALU.mult, ALU.add), deps=[c1])
            u1 = p.c("scalar", lambda em: em.activation(
                u, xc[:, 0:H], AF.Identity, bias=b.wk1[:, 1:2], scale=b.wk1[:, 0:1]), deps=[ld, c2])
            i1 = p.c("vector", lambda em: em.tensor_copy(I, u), deps=[u1])
            g1 = p.c("vector", lambda em: em.tensor_tensor(g, I, u, ALU.is_gt), deps=[i1])
            n1 = p.c("vector", lambda em: em.scalar_tensor_tensor(
                b.scrapH, I, 256.0, g, ALU.subtract, ALU.subtract), deps=[g1])
            stq = p.d("gpsimd", b.ln_dram[128 * t:128 * (t + 1), H * h:H * (h + 1)],
                      b.scrapH, deps=[n1], sem="ln_st")
            b.lnprev[0] = [u1, n1, stq]
            wr.append(stq)
    fm = []
    for c in range(CH):
        fm.append(p.d("scalar", b.fmA[:, TOK * c:TOK * (c + 1)],
                      b.ln_dram[:, 128 * c:128 * (c + 1)], deps=wr, transpose=True, sem="fm_tr"))
    return fm


def ep_rope_quant(b, dst_dram, fold_sk):
    p = b.prog
    cosb, sinb = carve(b, "cosb"), carve(b, "sinb")
    tA = carve(b, "ropeA")
    qstg = carve(b, "qstg")
    if "rope_state" not in b:
        b.rope_state = {}
    state = b.rope_state

    def ep(t, og, ps, mm_tok):
        prev = state.get("dma", ())
        cos_t = cosb[:, 128 * t:128 * (t + 1)]
        sin_t = sinb[:, 128 * t:128 * (t + 1)]
        last = ()
        toks = []
        for hh in range(4):
            ph = ps[:, 128 * hh:128 * (hh + 1)]
            rt = tA[:, 0:128]
            tB = tA[:, 128:256]
            c1 = p.c("vector", lambda em, ph=ph, rt=rt, cos_t=cos_t, t=t: em.scalar_tensor_tensor(
                rt, ph, b.sx[:, t:t + 1], cos_t, ALU.mult, ALU.mult),
                deps=[mm_tok] + list(prev) + list(last) + list(b.get("sx_toks", ())))
            c2 = p.c("vector", lambda em, ph=ph, tB=tB, sin_t=sin_t, t=t: em.scalar_tensor_tensor(
                tB[:, 0:64], ph[:, 64:128], b.nsx[:, t:t + 1], sin_t[:, 0:64], ALU.mult, ALU.mult),
                deps=[mm_tok, c1])
            c3 = p.c("vector", lambda em, ph=ph, tB=tB, sin_t=sin_t, t=t: em.scalar_tensor_tensor(
                tB[:, 64:128], ph[:, 0:64], b.sx[:, t:t + 1], sin_t[:, 64:128], ALU.mult, ALU.mult),
                deps=[mm_tok, c2])
            c4 = p.c("vector", lambda em, rt=rt, tB=tB: em.tensor_add(rt, rt, tB), deps=[c3])
            r1 = p.c("vector", lambda em, rt=rt: em.reduce_max(
                b.wk1[:, 2:3], rt, axis=AX.X, apply_absolute_value=True), deps=[c4])
            r2 = p.c("vector", lambda em: em.tensor_scalar(
                b.wk1[:, 3:4], b.wk1[:, 2:3], 1.0 / 127.0, 1e-8, ALU.mult, ALU.max), deps=[r1])
            r3 = p.c("vector", lambda em: em.reciprocal(b.wk1[:, 4:5], b.wk1[:, 3:4]), deps=[r2])
            u = tA[:, 128:256]
            I = b.wk128i[:, 0:128]
            g = b.wk128f[:, 0:128]
            u1 = p.c("vector", lambda em, rt=rt, u=u: em.tensor_scalar(
                u, rt, b.wk1[:, 4:5], 256.5, ALU.mult, ALU.add), deps=[r3, c4])
            i1 = p.c("vector", lambda em, I=I, u=u: em.tensor_copy(I, u), deps=[u1])
            g1 = p.c("vector", lambda em, I=I, u=u, g=g: em.tensor_tensor(g, I, u, ALU.is_gt), deps=[i1])
            if fold_sk:
                n1 = p.c("vector", lambda em, I=I, g=g, u=u: em.scalar_tensor_tensor(
                    u, I, 256.0, g, ALU.subtract, ALU.subtract), deps=[g1])
                n2 = p.c("vector", lambda em, hh=hh, u=u: em.tensor_scalar_mul(
                    qstg[:, 128 * hh:128 * (hh + 1)], u, b.wk1[:, 3:4]), deps=[n1])
                last = [n2]
            else:
                n2 = p.c("vector", lambda em, I=I, g=g, hh=hh: em.scalar_tensor_tensor(
                    qstg[:, 128 * hh:128 * (hh + 1)], I, 256.0, g, ALU.subtract, ALU.subtract),
                    deps=[g1])
                last = [n2]
            toks.append(n2)
        dm = p.d("gpsimd", dst_dram[128 * t:128 * (t + 1), 512 * og:512 * (og + 1)],
                 qstg[:, 0:512], deps=toks, sem="rope_st")
        state["dma"] = [dm]
        return [toks[0], dm]
    return ep


def ep_vquant(b, dst_dram):
    p = b.prog
    tA = carve(b, "ropeA")
    qstg = carve(b, "qstg")
    state = b.rope_state

    def ep(t, og, ps, mm_tok):
        prev = state.get("dma", ())
        last = ()
        toks = []
        for hh in range(4):
            ph = ps[:, 128 * hh:128 * (hh + 1)]
            v = tA[:, 0:128]
            c1 = p.c("vector", lambda em, ph=ph, v=v, t=t: em.tensor_scalar_mul(
                v, ph, b.sx[:, t:t + 1]), deps=[mm_tok] + list(prev) + list(last) + list(b.get("sx_toks", ())))
            r1 = p.c("vector", lambda em, v=v: em.reduce_max(
                b.wk1[:, 2:3], v, axis=AX.X, apply_absolute_value=True), deps=[c1])
            r2 = p.c("vector", lambda em: em.tensor_scalar(
                b.wk1[:, 3:4], b.wk1[:, 2:3], 1.0 / 127.0, 1e-8, ALU.mult, ALU.max), deps=[r1])
            r3 = p.c("vector", lambda em: em.reciprocal(b.wk1[:, 4:5], b.wk1[:, 3:4]), deps=[r2])
            u = tA[:, 128:256]
            I = b.wk128i[:, 0:128]
            g = b.wk128f[:, 0:128]
            u1 = p.c("vector", lambda em, v=v, u=u: em.tensor_scalar(
                u, v, b.wk1[:, 4:5], 256.5, ALU.mult, ALU.add), deps=[r3])
            i1 = p.c("vector", lambda em, I=I, u=u: em.tensor_copy(I, u), deps=[u1])
            g1 = p.c("vector", lambda em, I=I, u=u, g=g: em.tensor_tensor(g, I, u, ALU.is_gt), deps=[i1])
            n1 = p.c("vector", lambda em, I=I, g=g, u=u: em.scalar_tensor_tensor(
                u, I, 256.0, g, ALU.subtract, ALU.subtract), deps=[g1])
            n2 = p.c("vector", lambda em, hh=hh, u=u: em.tensor_scalar_mul(
                qstg[:, 128 * hh:128 * (hh + 1)], u, b.wk1[:, 3:4]), deps=[n1])
            last = [n2]
            toks.append(n2)
        dm = p.d("gpsimd", dst_dram[128 * t:128 * (t + 1), 512 * og:512 * (og + 1)],
                 qstg[:, 0:512], deps=toks, sem="rope_st")
        state["dma"] = [dm]
        return [toks[0], dm]
    return ep


def attn1_launchE(b, xsrc, w4toks=None):
    p = b.prog
    if w4toks is None:
        wq4 = prep_w4(b, b.wq, b.wq4_bf, "wq4")
        wk4 = prep_w4(b, b.wk, b.wk4_bf, "wk4")
        wv4 = prep_w4(b, b.wv, b.wv4_bf, "wv4")
    else:
        wq4, wk4, wv4 = w4toks
    st = ln_stats_minmax(b, xsrc)
    ns = p.c("vector", lambda em: em.tensor_scalar_mul(b.nsx[:, 0:TT], b.sx[:, 0:TT], -1.0), deps=st)
    b.sx_toks = st + [ns]
    fm = quantize_x_to_fm(b, xsrc, st)
    rt = build_rope_tables(b, ())
    p.wait("vector", rt)

    def mv(wbf, wtk):
        def f(og, c, buf, prev):
            i = (og * CH + c) % 2
            return p.d("sync", buf, wbf[512 * og:512 * (og + 1), 128 * c:128 * (c + 1)],
                       deps=prev + wtk, transpose=True, sem=f"mv{i}")
        return f

    t2_matmul(b, b.fmA, fm + [ns], mv(b.wq4_bf, wq4), ep_rope_quant(b, b.q_dram, True))
    t2_matmul(b, b.fmA, fm, mv(b.wk4_bf, wk4), ep_rope_quant(b, b.k_dram, True))
    t2_matmul(b, b.fmA, fm, mv(b.wv4_bf, wv4), ep_vquant(b, b.v_dram))
    return


def build_band2(b, deps):
    p = b.prog
    W2 = 2944
    band = carve(b, "btmp0")[:, 0:W2]
    tmp = carve(b, "btmp1")[:, 0:W2]
    onesw = carve(b, "bones")[:, 0:W2]
    aband = carve(b, "aband2")
    g0 = broadcast_scalar(b, b.g0in, b.gvec[:, 0:1], 6, deps)
    g1 = broadcast_scalar(b, b.g1in, b.gvec[:, 1:2], 6, deps)
    on = p.c("vector", lambda em: em.memset(onesw, 1.0), deps=())
    a0 = p.c("gpsimd", lambda em: em.affine_select(band, onesw, pattern=[[-1, W2]],
             compare_op=ALU.is_ge, fill=0.0, channel_multiplier=1, base=896), deps=[on])
    m0 = p.c("vector", lambda em: em.tensor_scalar_mul(band, band, b.gvec[:, 0:1]), deps=[a0] + g0)
    a1 = p.c("gpsimd", lambda em: em.affine_select(tmp, onesw, pattern=[[-1, W2]],
             compare_op=ALU.is_ge, fill=0.0, channel_multiplier=1, base=1920), deps=[on, m0])
    m1 = p.c("vector", lambda em: em.tensor_scalar_mul(tmp, tmp, b.gvec[:, 1:2]), deps=[a1] + g1)
    m2 = p.c("vector", lambda em: em.tensor_add(band, band, tmp), deps=[m1])
    m3 = p.c("vector", lambda em: em.tensor_scalar(aband, band, 1e30, -1e30, ALU.mult, ALU.add), deps=[m2])
    return [m3]


def band2_slice(b, t1b, t2c):
    uu0 = 896 - 128 * t1b + 512 * t2c
    return carve(b, "aband2")[:, uu0:uu0 + 512]


def attn1_launchF(b, xsrc, xdst, wo4toks=None):
    p = b.prog
    if wo4toks is None:
        wo4 = prep_w4(b, b.wo, b.wo4_bf, "wo4")
        b.lnprev["np"] = wo4[-2:]
    else:
        wo4 = wo4toks
    bd = build_band2(b, ())
    kstg = carve(b, "kstg")
    prevk = list(bd)
    qfm_wr, kfm_wr = [], []
    for c in range(CH):
        t0 = p.d("scalar", kstg[:, 0:1024], b.q_dram[:, 128 * c:128 * (c + 1)],
                 deps=prevk, transpose=True, sem="kstg")
        w0 = p.d("gpsimd", b.qfm_dram[128 * c:128 * (c + 1), :], kstg[:, 0:1024],
                 deps=[t0], sem="kstg_wr")
        t1 = p.d("scalar", kstg[:, 0:1024], b.kt0_dram[:, 128 * c:128 * (c + 1)],
                 deps=[w0], transpose=True, sem="kstg")
        t2 = p.d("scalar", kstg[:, 1024:2048], b.kt1_dram[:, 128 * c:128 * (c + 1)],
                 deps=[w0], transpose=True, sem="kstg")
        w1 = p.d("gpsimd", b.kfm_dram[128 * c:128 * (c + 1), :], kstg[:, 0:2048],
                 deps=[t1, t2], sem="kstg_wr")
        prevk = [w1]
        qfm_wr.append(w0)
        kfm_wr.append(w1)
    qh = carve(b, "qh")
    kh = carve(b, "kh2")
    vh = carve(b, "vh2")
    pT = carve(b, "pT")
    e127 = b.lnchunk.bitcast(F32)[:, 0:D // 2]
    npstg = b.scrapH
    kh_guard, vh_guard, qh_guard = [], [], []
    pt_guard = [list(bd)]
    ao_state = {}
    for h in range(NH):
        kl = p.d("sync", kh[:, 0:2048], b.kfm_dram[128 * h:128 * (h + 1), :],
                 deps=kh_guard + kfm_wr, sem="khl")
        ql = p.d("sync", qh[:, 0:1024], b.qfm_dram[128 * h:128 * (h + 1), :],
                 deps=qh_guard + qfm_wr, sem="qhl")
        v0 = p.d("sync", vh.rearrange("p (a c) -> p a c", c=128)[:, 0:8, :],
                 b.vt0_dram.rearrange("(a p) c -> p a c", p=128)[:, :, 128 * h:128 * (h + 1)],
                 deps=vh_guard, sem="vhl")
        v1 = p.d("sync", vh.rearrange("p (a c) -> p a c", c=128)[:, 8:16, :],
                 b.vt1_dram.rearrange("(a p) c -> p a c", p=128)[:, :, 128 * h:128 * (h + 1)],
                 deps=vh_guard, sem="vhl")
        kh_guard, vh_guard, qh_guard = [], [], []
        np_wr = []
        for t1b in range(TT):
            mm_toks = []
            for t2c in range(4):
                bank = 4 + t2c
                mm = p.c("tensor", lambda em, t1b=t1b, t2c=t2c, bank=bank: em.matmul(
                    b.psum[bank][:, 0:512], qh[:, 128 * t1b:128 * (t1b + 1)],
                    kh[:, 512 * t2c:512 * (t2c + 1)], start=True, stop=True),
                    deps=[kl, ql] + b.bank_guard[bank])
                md = p.c("vector", lambda em, bank=bank, bs=band2_slice(b, t1b, t2c): em.tensor_add(
                    b.psum[bank][:, 0:512], b.psum[bank][:, 0:512], bs),
                    deps=[mm] + bd)
                mm_toks.append(md)
            r = []
            for t2c in range(4):
                r.append(p.c("vector", lambda em, t2c=t2c: em.reduce_max(
                    b.wk1[:, 4 + t2c:5 + t2c], b.psum[4 + t2c][:, 0:512], axis=AX.X),
                    deps=[mm_toks[t2c]] + list(b.lnprev.get("wk1h", ()))))
            x1 = p.c("vector", lambda em: em.tensor_max(b.wk1[:, 4:5], b.wk1[:, 4:5], b.wk1[:, 5:6]), deps=r)
            x2 = p.c("vector", lambda em: em.tensor_max(b.wk1[:, 6:7], b.wk1[:, 6:7], b.wk1[:, 7:8]), deps=r)
            x3 = p.c("vector", lambda em: em.tensor_max(b.wk1[:, 4:5], b.wk1[:, 4:5], b.wk1[:, 6:7]), deps=[x1, x2])
            bch = p.c("vector", lambda em: em.tensor_scalar(
                b.wk1[:, 5:6], b.wk1[:, 4:5], -INV_SQRT_DH, LN127, ALU.mult, ALU.add), deps=[x3])
            lac = []
            for t2c in range(4):
                exq = p.c("scalar", lambda em, t2c=t2c: em.activation(
                    e127[:, 512 * t2c:512 * (t2c + 1)], b.psum[4 + t2c][:, 0:512], AF.Exp,
                    bias=b.wk1[:, 5:6], scale=INV_SQRT_DH, accum_out=b.wk1[:, 8 + t2c:9 + t2c]),
                    deps=[bch] + list(b.lnprev.get("e", ())))
                b.bank_guard[4 + t2c] = [exq]
                lac.append(exq)
            l1 = p.c("vector", lambda em: em.tensor_add(b.wk1[:, 8:9], b.wk1[:, 8:9], b.wk1[:, 9:10]), deps=lac)
            l2 = p.c("vector", lambda em: em.tensor_add(b.wk1[:, 10:11], b.wk1[:, 10:11], b.wk1[:, 11:12]), deps=lac)
            l3 = p.c("vector", lambda em: em.tensor_add(b.wk1[:, 8:9], b.wk1[:, 8:9], b.wk1[:, 10:11]), deps=[l1, l2])
            lr0 = p.c("scalar", lambda em: em.activation(b.wk1[:, 3:4], b.wk1[:, 8:9], AF.Ln,
                                                         bias=0.0, scale=1.0), deps=[l3])
            lr1 = p.c("scalar", lambda em, t1b=t1b: em.activation(
                b.rcptab[:, t1b:t1b + 1], b.wk1[:, 3:4], AF.Exp, bias=0.0, scale=-1.0),
                deps=[lr0] + list(b.lnprev.get("rc", ())) + list(b.lnprev.get(("rct", t1b), ())))
            b.lnprev["rc"] = [lr1]
            b.lnprev["wk1h"] = lac + [lr0, lr1, l3]
            rt_ = []
            for hlf in range(2):
                u = e127[:, 1024 * hlf:1024 * (hlf + 1)]
                I = b.wd_i32[:, 0:1024]
                g = b.wd_lo.bitcast(F32)[:, 0:1024]
                u1 = p.c("vector", lambda em, u=u: em.tensor_scalar_add(u, u, 0.5),
                         deps=lac + list(b.lnprev.get("np", ())))
                i1 = p.c("vector", lambda em, I=I, u=u: em.tensor_copy(I, u), deps=[u1])
                g1 = p.c("vector", lambda em, I=I, u=u, g=g: em.tensor_tensor(g, I, u, ALU.is_gt), deps=[i1])
                n1 = p.c("vector", lambda em, I=I, g=g, hlf=hlf: em.tensor_sub(
                    npstg[:, 1024 * hlf:1024 * (hlf + 1)], I, g), deps=[g1])
                rt_.append(n1)
            b.lnprev["e"] = rt_
            dm = p.d("gpsimd", b.np_dram[128 * t1b:128 * (t1b + 1), :], npstg[:, 0:2048],
                     deps=rt_ + list(b.lnprev.get("npd", ())), sem="np_st")
            b.lnprev["npd"] = [dm]
            b.lnprev["np"] = rt_ + [dm]
            np_wr.append(dm)
        ptr = []
        for t2b in range(NKB):
            ptr.append(p.d("scalar", pT[:, 1024 * t2b:1024 * (t2b + 1)],
                           b.np_dram[:, 128 * t2b:128 * (t2b + 1)], deps=np_wr + pt_guard[0],
                           transpose=True, sem="np_tr"))
        av_mms = []
        for pas in range(2):
            for t1bq in range(4):
                t1b = 4 * pas + t1bq
                bank = t1bq
                mms = []
                for t2b in range(NKB):
                    deps = ptr + [v0, v1]
                    if t2b == 0:
                        deps = deps + b.bank_guard[bank]
                    mm = p.c("tensor", lambda em, t1b=t1b, t2b=t2b, bank=bank: em.matmul(
                        b.psum[bank][:, 0:128],
                        pT[:, 1024 * t2b + 128 * t1b: 1024 * t2b + 128 * (t1b + 1)],
                        vh[:, 128 * t2b:128 * (t2b + 1)],
                        start=(t2b == 0), stop=(t2b == NKB - 1)), deps=deps)
                    mms.append(mm)
                av_mms.append(mms[-1])
                i = len(av_mms) % 2
                prev = ao_state.get(i, ())
                sc = p.c("vector", lambda em, bank=bank, t1b=t1b, i=i: em.tensor_scalar_mul(
                    b.aostgF[i][:, 0:128], b.psum[bank][:, 0:128], b.rcptab[:, t1b:t1b + 1]),
                    deps=[mms[-1]] + list(prev))
                b.lnprev[("rct", t1b)] = [sc]
                dm = p.d("gpsimd", b.ao_dram_f[128 * t1b:128 * (t1b + 1), 128 * h:128 * (h + 1)],
                         b.aostgF[i][:, 0:128], deps=[sc], sem=f"ao_st{i}")
                b.bank_guard[bank] = [sc]
                ao_state[i] = [dm]
                kh_guard.append(mms[-1])
                vh_guard.append(mms[-1])
                qh_guard.append(mms[-1])
        pt_guard[0] = av_mms
    ao_done = p.barrier_token()
    st2 = []
    H = D // 2
    for t in range(TT):
        xc = b.xchunk
        prev = b.lnprev.get(1, ())
        ld = p.d("sync", xc[:, 0:D], b.ao_dram_f[128 * t:128 * (t + 1), :],
                 deps=list(prev) + (ao_done if t == 0 else []), sem="x_ld")
        r1 = p.c("vector", lambda em, xc=xc: em.reduce_max(
            b.wk1[:, 0:1], xc[:, 0:D], axis=AX.X, apply_absolute_value=True), deps=[ld])
        r2 = p.c("vector", lambda em, t=t: em.tensor_scalar(
            b.sa[:, t:t + 1], b.wk1[:, 0:1], 1.0 / 127.0, 1e-8, ALU.mult, ALU.max), deps=[r1])
        r3 = p.c("vector", lambda em, t=t: em.reciprocal(b.wk1[:, 1:2], b.sa[:, t:t + 1]), deps=[r2])
        for hlf in range(2):
            u = b.lnchunk.bitcast(F32)[:, 0:H]
            I = b.wd_i32[:, 0:H]
            g = b.wd_lo.bitcast(F32)[:, 0:H]
            u1 = p.c("vector", lambda em, hlf=hlf, xc=xc, u=u: em.tensor_scalar(
                u, xc[:, H * hlf:H * (hlf + 1)], b.wk1[:, 1:2], 256.5, ALU.mult, ALU.add),
                deps=[r3] + list(b.lnprev.get(1, ())))
            i1 = p.c("vector", lambda em, I=I, u=u: em.tensor_copy(I, u), deps=[u1])
            g1 = p.c("vector", lambda em, I=I, u=u, g=g: em.tensor_tensor(g, I, u, ALU.is_gt), deps=[i1])
            n1 = p.c("vector", lambda em, I=I, g=g: em.scalar_tensor_tensor(
                b.scrapH, I, 256.0, g, ALU.subtract, ALU.subtract), deps=[g1])
            stq = p.d("gpsimd", b.ln_dram[128 * t:128 * (t + 1), H * hlf:H * (hlf + 1)],
                      b.scrapH, deps=[n1] + list(b.lnprev.get("st2", ())), sem="ln_st")
            b.lnprev[1] = [u1, n1, stq, r1]
            b.lnprev["st2"] = [stq]
            st2.append(stq)
    nafm = []
    for c in range(CH):
        nafm.append(p.d("scalar", b.fmA[:, TOK * c:TOK * (c + 1)],
                        b.ln_dram[:, 128 * c:128 * (c + 1)], deps=st2, transpose=True, sem="fm_tr"))

    def mv_wo(og, c, buf, prev):
        i = (og * CH + c) % 2
        return p.d("sync", buf, b.wo4_bf[512 * og:512 * (og + 1), 128 * c:128 * (c + 1)],
                   deps=prev + wo4, transpose=True, sem=f"mv{i}")

    ep_state = {}

    def ep(t, og, ps, mm_tok):
        i = t % 2
        prev = ep_state.get(i, ())
        ld = p.d("gpsimd", b.epi_x[i][:, 0:512],
                 xsrc[128 * t:128 * (t + 1), 512 * og:512 * (og + 1)], deps=prev, sem=f"epix{i}")
        a1 = p.c("vector", lambda em, i=i, ps=ps, t=t: em.scalar_tensor_tensor(
            b.epi_y[:, 0:512], ps, b.sa[:, t:t + 1], b.epi_x[i][:, 0:512], ALU.mult, ALU.add),
            deps=[ld, mm_tok] + list(ep_state.get("y", ())))
        stq = p.d("gpsimd", xdst[128 * t:128 * (t + 1), 512 * og:512 * (og + 1)],
                  b.epi_y[:, 0:512], deps=[a1], sem="epi_st")
        ep_state[i] = [ld, a1, stq]
        ep_state["y"] = [stq]
        return [a1, stq]

    t2_matmul(b, b.fmA, nafm, mv_wo, ep)
    return


def alloc_attn1_buffers(b, nc, es):
    sb = lambda name, shape, dt: es.enter_context(nc.sbuf_tensor(name, shape, dt)).ap()
    b.sx = sb("sx", [128, TT], F32)
    b.nsx = sb("nsx", [128, TT], F32)
    b.rsx = sb("rsx", [128, TT], F32)
    b.rmax = sb("rmax", [128, TT], F32)
    b.rmin = sb("rmin", [128, TT], F32)
    b.wk1 = sb("wk1", [128, 12], F32)
    b.wk128i = sb("wk128i", [128, 128], I32)
    b.wk128f = sb("wk128f", [128, 128], F32)
    b.rcptab = sb("rcptab", [128, TT], F32)
    b.sa = sb("sa", [128, TT], F32)
    b.aostgF = [sb(f"aostgF{i}", [128, 128], F32) for i in range(2)]
    b.scrapH = b.tmB[:, 0:2048]


def setup_carves_f(b):
    b.carves = {
        "pT":    (8192, 24576, "bf"),
        "qh":    (24576, 25600, "bf"),
        "kh2":   (25600, 27648, "bf"),
        "vh2":   (27648, 29696, "bf"),
        "aband2": (29696, 32640, "bf"),
        "btmp0": (2048, 5504, "bf"),
        "btmp1": (5504, 8960, "bf"),
        "bones": (24576, 27520, "bf"),
        "kstg":  (2048, 4096, "bf"),
    }


def build_launchE():
    nc = bass.Bass("TRN2", target_bir_lowering=False, debug=False)
    es = ExitStack()
    b = new_builder(nc)
    b.xin = nc.dram_tensor("xin", [TOK, D], F32, kind="ExternalInput").ap()
    b.wq = nc.dram_tensor("wq", [D, D], F32, kind="ExternalInput").ap()
    b.wk = nc.dram_tensor("wk", [D, D], F32, kind="ExternalInput").ap()
    b.wv = nc.dram_tensor("wv", [D, D], F32, kind="ExternalInput").ap()
    b.poff = nc.dram_tensor("poff", [1, 1], F32, kind="ExternalInput").ap()
    b.q_dram = nc.dram_tensor("q_out", [TOK, D], BF16, kind="ExternalOutput").ap()
    b.k_dram = nc.dram_tensor("k_out", [TOK, D], BF16, kind="ExternalOutput").ap()
    b.v_dram = nc.dram_tensor("v_out", [TOK, D], BF16, kind="ExternalOutput").ap()
    b.wq4_bf = nc.dram_tensor("wq4_bf", [D, D], BF16, kind="Internal").ap()
    b.wk4_bf = nc.dram_tensor("wk4_bf", [D, D], BF16, kind="Internal").ap()
    b.wv4_bf = nc.dram_tensor("wv4_bf", [D, D], BF16, kind="Internal").ap()
    b.ln_dram = nc.dram_tensor("ln_dram", [TOK, D], BF16, kind="Internal").ap()
    alloc_buffers(b, nc, es)
    alloc_attn_buffers(b, nc, es)
    alloc_attn1_buffers(b, nc, es)
    setup_carves(b)
    attn1_launchE(b, b.xin)
    b.prog.wait("gpsimd", b.prog.barrier_token())
    build_block(nc, b.prog, es)
    es.close()
    return nc


def build_launchF():
    nc = bass.Bass("TRN2", target_bir_lowering=False, debug=False)
    es = ExitStack()
    b = new_builder(nc)
    b.xin = nc.dram_tensor("xin", [TOK, D], F32, kind="ExternalInput").ap()
    b.wo = nc.dram_tensor("wo", [D, D], F32, kind="ExternalInput").ap()
    b.q_dram = nc.dram_tensor("q_in", [TOK, D], BF16, kind="ExternalInput").ap()
    b.kt0_dram = nc.dram_tensor("kt0", [TOK, D], BF16, kind="ExternalInput").ap()
    b.kt1_dram = nc.dram_tensor("kt1", [TOK, D], BF16, kind="ExternalInput").ap()
    b.vt0_dram = nc.dram_tensor("vt0", [TOK, D], BF16, kind="ExternalInput").ap()
    b.vt1_dram = nc.dram_tensor("vt1", [TOK, D], BF16, kind="ExternalInput").ap()
    b.g0in = nc.dram_tensor("g0in", [1, 1], F32, kind="ExternalInput").ap()
    b.g1in = nc.dram_tensor("g1in", [1, 1], F32, kind="ExternalInput").ap()
    b.xout = nc.dram_tensor("xout", [TOK, D], F32, kind="ExternalOutput").ap()
    b.wo4_bf = nc.dram_tensor("wo4_bf", [D, D], BF16, kind="Internal").ap()
    b.qfm_dram = nc.dram_tensor("qfm_dram", [D, TOK], BF16, kind="Internal").ap()
    b.kfm_dram = nc.dram_tensor("kfm_dram", [D, TK], BF16, kind="Internal").ap()
    b.np_dram = nc.dram_tensor("np_dram", [TOK, TK], BF16, kind="Internal").ap()
    b.ao_dram_f = nc.dram_tensor("ao_dram_f", [TOK, D], F32, kind="Internal").ap()
    b.ln_dram = nc.dram_tensor("ln_dram", [TOK, D], BF16, kind="Internal").ap()
    alloc_buffers(b, nc, es)
    alloc_attn_buffers(b, nc, es)
    alloc_attn1_buffers(b, nc, es)
    setup_carves_f(b)
    attn1_launchF(b, b.xin, b.xout)
    b.prog.wait("gpsimd", b.prog.barrier_token())
    build_block(nc, b.prog, es)
    es.close()
    return nc


def run_attn1(x_slices, wq, wk, wv, wo):
    n = len(x_slices)
    ncE = build_launchE()
    imE = []
    for i in range(n):
        poff = np.array([[1024.0 * (i % 2)]], dtype=np.float32)
        imE.append({"xin": x_slices[i], "wq": wq, "wk": wk, "wv": wv, "poff": poff})
    resE = run_bass_kernel_spmd(ncE, imE, core_ids=list(range(n)))
    ncF = build_launchF()
    imF = []
    for i in range(n):
        pair = i // 2
        e, o = 2 * pair, 2 * pair + 1
        g0 = np.array([[1.0 if i % 2 == 0 else 0.0]], dtype=np.float32)
        g1 = np.array([[0.0 if i % 2 == 0 else 1.0]], dtype=np.float32)
        imF.append({
            "xin": x_slices[i], "wo": wo,
            "q_in": resE.results[i]["q_out"],
            "kt0": resE.results[e]["k_out"], "kt1": resE.results[o]["k_out"],
            "vt0": resE.results[e]["v_out"], "vt1": resE.results[o]["v_out"],
            "g0in": g0, "g1in": g1})
    resF = run_bass_kernel_spmd(ncF, imF, core_ids=list(range(n)))
    return [r["xout"] for r in resF.results]



def build_fused(n_cores=NCORE):
    nc = bass.Bass("TRN2", target_bir_lowering=False, debug=False)
    es = ExitStack()
    b = new_builder(nc)
    p = b.prog
    dram = lambda name, shape, dt, kind="Internal": nc.dram_tensor(name, shape, dt, kind=kind).ap()

    b.xin = dram("xin", [TOK, D], F32, "ExternalInput")
    b.had = dram("had", [D, D], F32, "ExternalInput")
    b.wdata = dram("wdata", [D, 2048], I32, "ExternalInput")
    b.scales = dram("scales", [D, 32], F32, "ExternalInput")
    b.ffn0_bias = dram("fbias", [D], F32, "ExternalInput")
    wq0 = dram("wq0", [D, D], F32, "ExternalInput")
    wk0 = dram("wk0", [D, D], F32, "ExternalInput")
    wv0 = dram("wv0", [D, D], F32, "ExternalInput")
    wo0 = dram("wo0", [D, D], F32, "ExternalInput")
    wq1 = dram("wq1_in", [D, D], F32, "ExternalInput")
    wk1 = dram("wk1_in", [D, D], F32, "ExternalInput")
    wv1 = dram("wv1_in", [D, D], F32, "ExternalInput")
    wo1 = dram("wo1_in", [D, D], F32, "ExternalInput")
    b.wffn = dram("wffn", [D, D], F32, "ExternalInput")
    b.ffn1_bias = dram("fbias1", [D], F32, "ExternalInput")
    b.poff = dram("poff", [1, 1], F32, "ExternalInput")
    b.g0in = dram("g0in", [1, 1], F32, "ExternalInput")
    b.g1in = dram("g1in", [1, 1], F32, "ExternalInput")
    xout = dram("xout", [TOK, D], F32, "ExternalOutput")

    b.had_bf = dram("had_bf", [D, D], BF16)
    b.wdq_bf = dram("wdq_bf", [D, D], BF16)
    b.wq_bf = dram("wq_bf", [D, D], BF16)
    b.wk_bf = dram("wk_bf", [D, D], BF16)
    b.wv_bf = dram("wv_bf", [D, D], BF16)
    b.wo_bf = dram("wo_bf", [D, D], BF16)
    b.wq4_bf = dram("wq4_bf", [D, D], BF16)
    b.wk4_bf = dram("wk4_bf", [D, D], BF16)
    b.wv4_bf = dram("wv4_bf", [D, D], BF16)
    b.wo4_bf = dram("wo4_bf", [D, D], BF16)
    b.w8_bf = dram("w8_bf", [D, D], BF16)
    b.ln_dram = dram("ln_dram", [TOK, D], BF16)
    b.xr_dram = dram("xr_dram", [TOK, D], BF16)
    b.q_dram = dram("q01", [TOK, D], BF16)
    b.k_dram = dram("k01", [TOK, D], BF16)
    b.vaug_dram = dram("v0aug", [TOK, 4128], BF16)
    b.v_dram = dram("v1", [TOK, D], BF16)
    kg = dram("kg", [TK, D], BF16)
    vg0 = dram("vg0", [TK, 4128], BF16)
    vg1 = dram("vg1", [TK, D], BF16)
    b.kfm_dram = dram("kfm_dram", [D, TK], BF16)
    b.qfm_dram = dram("qfm_dram", [D, TOK], BF16)
    b.np_dram = dram("np_dram", [TOK, TK], BF16)
    b.ao_dram = dram("ao_dram", [TOK, D], BF16)
    b.ao_dram_f = dram("ao_dram_f", [TOK, D], F32)
    b.amax_loc = dram("amax_loc", [1, 1], F32)
    b.amax_red = dram("amax_red", [1, 1], F32)
    xa = dram("xa", [TOK, D], F32)
    xb = dram("xb", [TOK, D], F32)

    alloc_buffers(b, nc, es)
    alloc_attn_buffers(b, nc, es)
    alloc_l1_buffers(b, nc, es)
    alloc_attn1_buffers(b, nc, es)
    for flag in ("HEAD_BARRIER", "STAGE_BARRIER", "AV_ALLGATES", "SKIP_WO",
                 "PT_FULLGUARD", "KH_SERIAL", "GATE_POOL", "SKIP_PREP"):
        b[flag] = False

    RGp = [[2 * i, 2 * i + 1] for i in range(n_cores // 2)]
    RGall = [list(range(n_cores))]

    def allgather(src, dst, deps):
        return p.c("gpsimd", lambda em: em.collective_compute(
            "AllGather", ALU.bypass, replica_groups=RGp,
            ins=[src.opt()], outs=[dst.opt()]), deps=deps)

    def amax_allreduce(deps):
        return p.c("gpsimd", lambda em: em.collective_compute(
            "AllReduce", ALU.max, replica_groups=RGall,
            ins=[b.amax_loc.opt()], outs=[b.amax_red.opt()]), deps=deps)

    wtA = prep_cast_bf16(b, b.had, b.had_bf, "had") + prep_wdq(b)
    ffn0_sublayer(b, b.xin, xa, wtA)
    wtB = (prep_cast_bf16(b, wq0, b.wq_bf, "wq") + prep_cast_bf16(b, wk0, b.wk_bf, "wk")
           + prep_cast_bf16(b, wv0, b.wv_bf, "wv"))
    wtC = prep_cast_bf16(b, wo0, b.wo_bf, "wo")
    p.set_floor()

    setup_carves(b)
    attn0_launchB(b, xa, b.wq_bf, b.wk_bf, b.wv_bf, wtB)
    p.set_floor()
    cc1 = allgather(b.k_dram, kg, ())
    cc2 = allgather(b.vaug_dram, vg0, [cc1])
    p.wait("gpsimd", [cc2])
    p.set_floor()

    b.kt0_dram, b.kt1_dram = kg[0:TOK, :], kg[TOK:2 * TOK, :]
    b.vaug0_dram, b.vaug1_dram = vg0[0:TOK, :], vg0[TOK:2 * TOK, :]
    b.fill = Fill()
    b.pstage_w8 = b.xchunk.bitcast(BF16)[:, 0:2048]
    b.pstage_w4 = b.wchunk_bf[:, 0:2048]
    wt8, wq4t, wk4t, wv4t, wo4t = [], [], [], [], []
    b.fill.add(prep_w8_gen(b, b.wffn, b.w8_bf, wt8))
    b.fill.add(prep_w4_gen(b, wq1, b.wq4_bf, "wq4", wq4t))
    b.fill.add(prep_w4_gen(b, wk1, b.wk4_bf, "wk4", wk4t))
    b.fill.add(prep_w4_gen(b, wv1, b.wv4_bf, "wv4", wv4t))
    b.fill.add(prep_w4_gen(b, wo1, b.wo4_bf, "wo4", wo4t))
    attn0_launchC(b, xa, xb, b.wo_bf, wtC)
    b.fill.drain()
    b.fill = None
    b.pstage_w8 = b.pstage_w4 = None

    p.set_floor()

    ffn0_sublayer(b, xb, xa, [])
    p.set_floor()

    z = p.c("vector", lambda em: em.memset(b.zero11[0:1, 0:1], 0.0), deps=())
    b.zero11_tok = [z]
    st = ln_stats(b, xa)
    am = ln_absmax(b, xa, st)
    dA = p.d("gpsimd", b.amax_loc, b.s11[0:1, 2:3], deps=am, sem="amx_st")
    cca = amax_allreduce([dA])
    l1ffn_main(b, xa, xb, b.amax_red, wt8 + [z], st=st, amax_w=1, amax_deps=[cca])
    p.set_floor()

    setup_carves(b)
    b.wq, b.wk, b.wv, b.wo = wq1, wk1, wv1, wo1
    attn1_launchE(b, xb, w4toks=(wq4t, wk4t, wv4t))
    p.set_floor()
    cc3 = allgather(b.k_dram, kg, ())
    cc4 = allgather(b.v_dram, vg1, [cc3])
    p.wait("gpsimd", [cc4])
    p.set_floor()

    setup_carves_f(b)
    b.kt0_dram, b.kt1_dram = kg[0:TOK, :], kg[TOK:2 * TOK, :]
    b.vt0_dram, b.vt1_dram = vg1[0:TOK, :], vg1[TOK:2 * TOK, :]
    attn1_launchF(b, xb, xa, wo4toks=wo4t)
    p.set_floor()

    st2 = ln_stats(b, xa)
    am2 = ln_absmax(b, xa, st2)
    dA2 = p.d("gpsimd", b.amax_loc, b.s11[0:1, 2:3], deps=am2, sem="amx_st")
    cca2 = amax_allreduce([dA2])
    l1ffn_main(b, xa, xout, b.amax_red, [], st=st2, amax_w=1, amax_deps=[cca2])

    p.wait("gpsimd", p.barrier_token())
    build_block(nc, b.prog, es)
    es.close()
    return nc


_UNFUSED = False

_nc_cache = {}


def _get_nc(name, builder):
    if name not in _nc_cache:
        _nc_cache[name] = builder()
    return _nc_cache[name]


def kernel(**inputs):
    x = np.asarray(inputs["L0_attn_x"], np.float32)
    had = np.asarray(inputs["L0_ffn_hadamard"], np.float32)
    wsc = np.asarray(inputs["L0_ffn_weight_scales"], np.float32)
    fb0 = np.asarray(inputs["L0_ffn_bias"], np.float32)
    wd = np.asarray(inputs["L0_ffn_weight_data"], np.int32)
    wq0 = np.asarray(inputs["L0_attn_wq"], np.float32)
    wk0 = np.asarray(inputs["L0_attn_wk"], np.float32)
    wv0 = np.asarray(inputs["L0_attn_wv"], np.float32)
    wo0 = np.asarray(inputs["L0_attn_wo"], np.float32)
    wq1 = np.asarray(inputs["L1_attn_wq"], np.float32)
    wk1 = np.asarray(inputs["L1_attn_wk"], np.float32)
    wv1 = np.asarray(inputs["L1_attn_wv"], np.float32)
    wo1 = np.asarray(inputs["L1_attn_wo"], np.float32)
    w1 = np.asarray(inputs["L1_ffn_weight"], np.float32)
    fb1 = np.asarray(inputs["L1_ffn_bias"], np.float32)

    n = 8
    xs = [np.ascontiguousarray(x[i // 2, 1024 * (i % 2):1024 * (i % 2) + 1024, :])
          for i in range(n)]
    cores = list(range(n))
    poffs = [np.array([[1024.0 * (i % 2)]], np.float32) for i in range(n)]
    g0s = [np.array([[1.0 if i % 2 == 0 else 0.0]], np.float32) for i in range(n)]
    g1s = [np.array([[0.0 if i % 2 == 0 else 1.0]], np.float32) for i in range(n)]

    def ffn0(xs):
        nc = _get_nc("A", build_launchA)
        im = [{"xin": xs[i], "had": had, "wdata": wd, "scales": wsc, "fbias": fb0}
              for i in range(n)]
        res = run_bass_kernel_spmd(nc, im, core_ids=cores)
        return [r["xout"] for r in res.results]

    def attn0(xs):
        ncB = _get_nc("B", build_launchB)
        imB = [{"xin": xs[i], "wq": wq0, "wk": wk0, "wv": wv0, "poff": poffs[i]}
               for i in range(n)]
        resB = run_bass_kernel_spmd(ncB, imB, core_ids=cores)
        ncC = _get_nc("C", build_launchC)
        imC = []
        for i in range(n):
            e, o = 2 * (i // 2), 2 * (i // 2) + 1
            imC.append({
                "xin": xs[i], "wo": wo0,
                "q_in": resB.results[i]["q_out"],
                "kt0": resB.results[e]["k_out"], "kt1": resB.results[o]["k_out"],
                "vaug0": resB.results[e]["v_out"], "vaug1": resB.results[o]["v_out"],
                "g0in": g0s[i], "g1in": g1s[i]})
        resC = run_bass_kernel_spmd(ncC, imC, core_ids=cores)
        return [r["xout"] for r in resC.results]

    def ffn1(xs):
        nc1 = _get_nc("D1", build_launchD1)
        res1 = run_bass_kernel_spmd(nc1, [{"xin": xs[i]} for i in range(n)], core_ids=cores)
        amax8 = np.zeros((1, 8), np.float32)
        for i in range(n):
            amax8[0, i] = res1.results[i]["amax_out"][0, 0]
        nc2 = _get_nc("D2", build_launchD2)
        im = [{"xin": xs[i], "wffn": w1, "fbias1": fb1, "amax8": amax8} for i in range(n)]
        res2 = run_bass_kernel_spmd(nc2, im, core_ids=cores)
        return [r["xout"] for r in res2.results]

    def attn1(xs):
        ncE = _get_nc("E", build_launchE)
        imE = [{"xin": xs[i], "wq": wq1, "wk": wk1, "wv": wv1, "poff": poffs[i]}
               for i in range(n)]
        resE = run_bass_kernel_spmd(ncE, imE, core_ids=cores)
        ncF = _get_nc("F", build_launchF)
        imF = []
        for i in range(n):
            e, o = 2 * (i // 2), 2 * (i // 2) + 1
            imF.append({
                "xin": xs[i], "wo": wo1,
                "q_in": resE.results[i]["q_out"],
                "kt0": resE.results[e]["k_out"], "kt1": resE.results[o]["k_out"],
                "vt0": resE.results[e]["v_out"], "vt1": resE.results[o]["v_out"],
                "g0in": g0s[i], "g1in": g1s[i]})
        resF = run_bass_kernel_spmd(ncF, imF, core_ids=cores)
        return [r["xout"] for r in resF.results]

    if _UNFUSED:
        xs = ffn0(xs)
        xs = attn0(xs)
        xs = ffn0(xs)
        xs = ffn1(xs)
        xs = attn1(xs)
        xs = ffn1(xs)
    else:
        nc = _get_nc("fused", build_fused)
        im = [{"xin": xs[i], "had": had, "wdata": wd, "scales": wsc, "fbias": fb0,
               "wq0": wq0, "wk0": wk0, "wv0": wv0, "wo0": wo0,
               "wq1_in": wq1, "wk1_in": wk1, "wv1_in": wv1, "wo1_in": wo1,
               "wffn": w1, "fbias1": fb1,
               "poff": poffs[i], "g0in": g0s[i], "g1in": g1s[i]} for i in range(n)]
        res = run_bass_kernel_spmd(nc, im, core_ids=cores)
        xs = [r["xout"] for r in res.results]

    out = np.empty((4, 2048, 4096), np.float32)
    for i in range(n):
        out[i // 2, 1024 * (i % 2):1024 * (i % 2) + 1024, :] = xs[i]
    return out


# version 06
# speedup vs baseline: 1.2244x; 1.0871x over previous
import numpy as np
import ml_dtypes
from contextlib import ExitStack

import concourse.bass as bass
import concourse.mybir as mybir
from concourse.bass_utils import run_bass_kernel_spmd

F32 = mybir.dt.float32
BF16 = mybir.dt.bfloat16
I32 = mybir.dt.int32
AF = mybir.ActivationFunctionType
ALU = mybir.AluOpType
AX = mybir.AxisListType
BF = ml_dtypes.bfloat16

TOK = 1024
D = 4096
CH = D // 128
TT = TOK // 128
NH, DH = 32, 128
NCORE = 8

COMPUTE = ("scalar", "vector", "tensor", "gpsimd")
DMAQ = ("sync", "scalar", "gpsimd")
ALL_ENG = ("sync", "scalar", "vector", "tensor", "gpsimd")


class Prog:

    def __init__(self):
        self.ops = {e: [] for e in ALL_ENG}
        self.ccnt = {e: 0 for e in COMPUTE}
        self.dcnt = {}
        self.water = {e: {} for e in ALL_ENG}
        self.floor = []

    def set_floor(self):
        self.floor = self.barrier_token()

    def _waits(self, e, deps):
        deps = list(deps) + self.floor
        best = {}
        for dep in deps:
            if dep is None:
                continue
            kind, name, n = dep
            val = n if kind == "c" else 16 * n
            key = (kind, name)
            if val > best.get(key, -1):
                best[key] = val
        waits = []
        for (kind, name), val in best.items():
            if self.water[e].get((kind, name), -1) >= val:
                continue
            self.water[e][(kind, name)] = val
            waits.append((kind, name, val))
        return waits

    def c(self, e, emit, deps=()):
        assert e in COMPUTE
        waits = self._waits(e, deps)
        self.ccnt[e] += 1
        n = self.ccnt[e]

        def run(em, env):
            for kind, name, val in waits:
                em.wait_ge(env.csem[name] if kind == "c" else env.dsem[name], val)
            emit(em).then_inc(env.csem[e], 1)

        self.ops[e].append(run)
        return ("c", e, n)

    def d(self, q, out, in_, deps=(), transpose=False, sem=None, **kw):
        assert q in DMAQ
        sem = sem or ("q_" + q)
        waits = self._waits(q, deps)
        self.dcnt[sem] = self.dcnt.get(sem, 0) + 1
        n = self.dcnt[sem]

        def run(em, env):
            for kind, name, val in waits:
                em.wait_ge(env.csem[name] if kind == "c" else env.dsem[name], val)
            o = out() if callable(out) else out
            i = in_() if callable(in_) else in_
            em.dma_start(out=o, in_=i, transpose=transpose, **kw).then_inc(env.dsem[sem], 16)

        self.ops[q].append(run)
        return ("d", sem, n)

    def wait(self, e, deps):
        waits = self._waits(e, deps)
        if not waits:
            return

        def run(em, env):
            for kind, name, val in waits:
                em.wait_ge(env.csem[name] if kind == "c" else env.dsem[name], val)

        self.ops[e].append(run)

    def barrier_token(self):
        toks = [("c", e, self.ccnt[e]) for e in COMPUTE if self.ccnt[e] > 0]
        toks += [("d", s, n) for s, n in self.dcnt.items() if n > 0]
        return toks


class Fill:

    def __init__(self):
        self.gens = []

    def add(self, gen):
        self.gens.append(gen)

    def pump(self, units=1):
        for _ in range(units):
            while self.gens:
                try:
                    next(self.gens[0])
                    break
                except StopIteration:
                    self.gens.pop(0)
            if not self.gens:
                return

    def drain(self):
        while self.gens:
            self.pump()


class Env:
    def __init__(self, csem, dsem):
        self.csem = csem
        self.dsem = dsem


def build_block(nc, prog, es):
    csem = {e: es.enter_context(nc.semaphore("cs_" + e)) for e in COMPUTE}
    dsem = {s: es.enter_context(nc.semaphore("ds_" + s)) for s in prog.dcnt}
    env = Env(csem, dsem)
    block = es.enter_context(nc.Block())

    def section(name):
        def body(em):
            for run in prog.ops[name]:
                run(em, env)
        return body

    block.sync(section("sync"))
    block.scalar(section("scalar"))
    block.vector(section("vector"))
    block.tensor(section("tensor"))
    block.gpsimd(section("gpsimd"))


class NS(dict):
    __getattr__ = dict.__getitem__
    __setattr__ = dict.__setitem__



def prep_cast_bf16(b, src_f32, dst_bf, tag):
    p = b.prog
    toks = []
    H = D // 2
    for k in range(CH):
        for h in range(2):
            prev = b.wprev.get(("wcast",), ())
            if (2 * k + h) % 2 == 0:
                buf, obuf = b.wchunk_f32[:, 0:H], b.wchunk_bf
                ld = p.d("sync", buf, src_f32[128 * k:128 * (k + 1), H * h:H * (h + 1)],
                         deps=prev, sem="wp_ld")
                ct = p.c("scalar", lambda em, buf=buf, obuf=obuf, h=h: em.activation(
                    obuf[:, H * h:H * (h + 1)], buf, AF.Copy, bias=0.0, scale=1.0), deps=[ld])
                st = p.d("gpsimd", dst_bf[128 * k:128 * (k + 1), H * h:H * (h + 1)],
                         obuf[:, H * h:H * (h + 1)], deps=[ct], sem="wp_st")
                b.wprev[("wcast",)] = [ct, st]
            else:
                buf = b.wd_i32.bitcast(F32)[:, 0:H]
                obuf = b.wd_lo.bitcast(BF16)[:, 0:H]
                prevx = list(b.wprev.get(("wcx",), ())) + list(b.wprev.get(("wdq",), ()))
                ld = p.d("scalar", buf, src_f32[128 * k:128 * (k + 1), H * h:H * (h + 1)],
                         deps=list(prevx), sem="wpx_ld")
                ct = p.c("vector", lambda em, buf=buf, obuf=obuf: em.tensor_copy(obuf, buf),
                         deps=[ld])
                st = p.d("sync", dst_bf[128 * k:128 * (k + 1), H * h:H * (h + 1)],
                         obuf, deps=[ct], sem="wpx_st")
                b.wprev[("wcx",)] = [ct, st]
            toks.append(st)
    b.wprev[("wcast",)] = list(b.wprev.get(("wcast",), ())) + list(b.wprev.get(("wcx",), ()))
    return toks


def prep_wdq(b):
    p = b.prog
    toks = []
    for k in range(CH):
        wd, lo = b.wd_i32, b.wd_lo
        sc, m8 = b.wd_sc, b.wd_m8
        out = b.wchunk_bf
        prev = list(b.wprev.get(("wdq",), ())) + list(b.wprev.get(("wcast",), ()))
        ld = p.d("sync", wd[:], b.wdata[128 * k:128 * (k + 1), :], deps=prev, sem="wdq_ld")
        ls = p.d("sync", sc[:], b.scales[128 * k:128 * (k + 1), :], deps=prev, sem="wdq_ld")
        tl = p.c("vector", lambda em, lo=lo, wd=wd: em.tensor_single_scalar(lo, wd, 15, ALU.bitwise_and), deps=[ld, ls])
        th = p.c("vector", lambda em, wd=wd: em.tensor_single_scalar(wd, wd, 4, ALU.logical_shift_right), deps=[tl])
        tm = p.c("vector", lambda em, m8=m8, sc=sc: em.tensor_scalar_mul(m8, sc, -8.0), deps=[ld, ls])
        deps0 = [tl, th, tm]
        gt = []
        for g in range(32):
            for off, srct in ((0, lo), (1, wd)):
                def mk(g=g, off=off, srct=srct, sc=sc, m8=m8, out=out):
                    def f(em):
                        return em.tensor_scalar(
                            out[:, 128 * g + off: 128 * (g + 1): 2],
                            srct[:, 64 * g: 64 * (g + 1)],
                            sc[:, g:g + 1], m8[:, g:g + 1], ALU.mult, ALU.add)
                    return f
                gt.append(p.c("vector", mk(), deps=deps0))
                deps0 = ()
        st = p.d("gpsimd", b.wdq_bf[128 * k:128 * (k + 1), :], out[:, 0:D], deps=gt, sem="wdq_st")
        b.wprev[("wdq",)] = [st] + gt[-1:]
        b.wprev[("wcast",)] = [st]
        toks.append(st)
    return toks



def ln_stats(b, xsrc):
    p = b.prog
    done = []
    for t in range(TT):
        xc = b.xchunk
        prev = b.lnprev.get(0, ())
        ld = p.d("sync", xc[:], xsrc[128 * t:128 * (t + 1), :], deps=prev, sem="x_ld")
        s1 = p.c("vector", lambda em, t=t, xc=xc: em.reduce_sum(
            b.ssum[:, t:t + 1], xc[:], axis=AX.X), deps=[ld])
        s2 = p.c("scalar", lambda em, t=t, xc=xc: em.activation(
            b.lnchunk[:, 0:D], xc[:], AF.Square, bias=0.0, scale=1.0,
            accum_out=b.ssq[:, t:t + 1]), deps=[ld])
        b.lnprev[0] = [s1, s2]
        done += [s1, s2]
    t0 = p.c("vector", lambda em: em.tensor_scalar_mul(b.mean[:], b.ssum[:, 0:TT], 1.0 / D), deps=done)
    t1 = p.c("vector", lambda em: em.tensor_scalar_mul(b.var[:], b.ssq[:, 0:TT], 1.0 / D), deps=done)
    t2 = p.c("vector", lambda em: em.tensor_tensor(b.msq[:], b.mean[:], b.mean[:], ALU.mult), deps=[t0])
    t3 = p.c("vector", lambda em: em.tensor_sub(b.var[:], b.var[:], b.msq[:]), deps=[t1, t2])
    t3b = p.c("vector", lambda em: em.tensor_scalar_add(b.var[:], b.var[:], 1e-5), deps=[t3])
    t4 = p.c("scalar", lambda em: em.activation(b.std[:], b.var[:], AF.Sqrt, bias=0.0, scale=1.0), deps=[t3b])
    t5 = p.c("vector", lambda em: em.reciprocal(b.rstd[:], b.std[:]), deps=[t4])
    t6 = p.c("vector", lambda em: em.tensor_tensor(b.nmr[:], b.mean[:], b.rstd[:], ALU.mult), deps=[t5])
    t7 = p.c("vector", lambda em: em.tensor_scalar_mul(b.nmr[:], b.nmr[:], -1.0), deps=[t6])
    return [t5, t7]


def ln_normalize_to_fm(b, xsrc, fm_buf, stat_toks):
    p = b.prog
    wr = []
    for t in range(TT):
        xc = b.xchunk
        lc = b.lnchunk
        prev = b.lnprev.get(0, ())
        ld = p.d("sync", xc[:], xsrc[128 * t:128 * (t + 1), :], deps=prev, sem="x_ld")
        ct = p.c("scalar", lambda em, t=t, xc=xc, lc=lc: em.activation(
            lc[:, 0:D], xc[:], AF.Identity, bias=b.nmr[:, t:t + 1], scale=b.rstd[:, t:t + 1]),
            deps=[ld] + stat_toks)
        st = p.d("gpsimd", b.ln_dram[128 * t:128 * (t + 1), :], lc[:, 0:D], deps=[ct], sem="ln_st")
        b.lnprev[0] = [ct, st]
        wr.append(st)
    tr = []
    for c in range(CH):
        tr.append(p.d("scalar", fm_buf[:, TOK * c:TOK * (c + 1)],
                      b.ln_dram[:, 128 * c:128 * (c + 1)], deps=wr, transpose=True, sem="fm_tr"))
    return tr



def t2_matmul(b, fm_buf, fm_ready, moving_loader, epilogue, n_out=D, og_size=512,
              bias_row=None):
    p = b.prog
    n_og = n_out // og_size
    mv_prev = b.mv_prev
    fill = b.get("fill")
    for og in range(n_og):
        if fill is not None:
            fill.pump(2)
        for c in range(CH):
            i = (og * CH + c) % 2
            ld = moving_loader(og, c, b.mvchunk[i][:, 0:og_size], list(mv_prev[i]))
            mm_toks = []
            for t in range(TT):
                deps = [ld] + fm_ready
                if c == 0:
                    deps = deps + b.bank_guard[t]
                def mk(og=og, c=c, t=t, i=i):
                    def f(em):
                        return em.matmul(
                            b.psum[t][:, 0:og_size],
                            fm_buf[:, TOK * c + 128 * t: TOK * c + 128 * (t + 1)],
                            b.mvchunk[i][:, 0:og_size],
                            start=(c == 0), stop=(c == CH - 1 and bias_row is None))
                    return f
                mm_toks.append(p.c("tensor", mk(), deps=deps))
            mv_prev[i] = mm_toks
            if c == CH - 1:
                if bias_row is not None:
                    brow, btoks = bias_row
                    bias_mm = []
                    for t in range(TT):
                        def mkb(og=og, t=t):
                            def f(em):
                                return em.matmul(
                                    b.psum[t][:, 0:og_size],
                                    b.onesbf[0:1, 128 * 0:128 * 1],
                                    brow[0:1, og_size * og:og_size * (og + 1)],
                                    start=False, stop=True)
                            return f
                        bias_mm.append(p.c("tensor", mkb(), deps=[mm_toks[t]] + btoks))
                    mm_toks = bias_mm
                for t in range(TT):
                    b.bank_guard[t] = list(epilogue(t, og, b.psum[t][:, 0:og_size], mm_toks[t]))
    out = []
    for t in range(TT):
        out += b.bank_guard[t]
    return out



def load_bias_row(b, bias_row_dram, deps):
    p = b.prog
    on = p.c("vector", lambda em: em.memset(b.onesbf[0:1, 0:128], 1.0), deps=())
    toks = [on]
    prev = ()
    for g in range(D // 512):
        ld = p.d("sync", b.browf[0:1, 0:512], bias_row_dram[None, 512 * g:512 * (g + 1)],
                 deps=list(deps) + list(prev), sem="bias_ld")
        ct = p.c("scalar", lambda em, g=g: em.activation(
            b.browbf[0:1, 512 * g:512 * (g + 1)], b.browf[0:1, 0:512], AF.Copy,
            bias=0.0, scale=1.0), deps=[ld])
        prev = [ct]
        toks.append(ct)
    return toks



def ffn0_sublayer(b, xsrc, xdst, wprep_toks):
    p = b.prog
    st = ln_stats(b, xsrc)
    fm_ready = ln_normalize_to_fm(b, xsrc, b.fmA, st)

    def mv_had(og, c, buf, prev):
        i = (og * CH + c) % 2
        return p.d("sync", buf, b.had_bf[128 * c:128 * (c + 1), 512 * og:512 * (og + 1)],
                   deps=prev + wprep_toks, sem=f"mv{i}")

    def ep_mm1(t, og, ps, mm_tok):
        ct = p.c("scalar", lambda em, t=t, og=og, ps=ps: em.activation(
            b.tmB[:, D * t + 512 * og: D * t + 512 * (og + 1)], ps, AF.Copy, bias=0.0, scale=1.0),
            deps=[mm_tok])
        return [ct]

    mm1_done = t2_matmul(b, b.fmA, fm_ready, mv_had, ep_mm1)

    xr_wr = []
    for t in range(TT):
        xr_wr.append(p.d("gpsimd", b.xr_dram[128 * t:128 * (t + 1), :],
                         b.tmB[:, D * t:D * (t + 1)], deps=mm1_done, sem="xr_st"))
    xr_fm = []
    for c in range(CH):
        xr_fm.append(p.d("scalar", b.fmA[:, TOK * c:TOK * (c + 1)],
                         b.xr_dram[:, 128 * c:128 * (c + 1)], deps=xr_wr, transpose=True, sem="fm_tr"))

    bb = load_bias_row(b, b.ffn0_bias, wprep_toks)

    def mv_wdq(og, c, buf, prev):
        i = (og * CH + c) % 2
        return p.d("sync", buf, b.wdq_bf[512 * og:512 * (og + 1), 128 * c:128 * (c + 1)],
                   deps=prev + wprep_toks, transpose=True, sem=f"mv{i}")

    ep_state = {}

    def ep_mm2(t, og, ps, mm_tok):
        i = t % 2
        prev = ep_state.get(i, ())
        ld = p.d("gpsimd", b.epi_x[i][:, 0:512],
                 xsrc[128 * t:128 * (t + 1), 512 * og:512 * (og + 1)], deps=prev, sem=f"epix{i}")
        a1 = p.c("vector", lambda em, i=i, ps=ps: em.scalar_tensor_tensor(
            b.epi_y[:, 0:512], ps, 0.5, b.epi_x[i][:, 0:512], ALU.mult, ALU.add),
            deps=[ld, mm_tok] + list(ep_state.get("y", ())))
        stq = p.d("gpsimd", xdst[128 * t:128 * (t + 1), 512 * og:512 * (og + 1)],
                  b.epi_y[:, 0:512], deps=[a1], sem="epi_st")
        ep_state[i] = [ld, a1, stq]
        ep_state["y"] = [stq]
        return [a1, stq]

    done = t2_matmul(b, b.fmA, xr_fm, mv_wdq, ep_mm2, bias_row=(b.browbf, bb))
    return done



def alloc_buffers(b, nc, es):
    sb = lambda name, shape, dt: es.enter_context(nc.sbuf_tensor(name, shape, dt)).ap()
    b.fmA = sb("fmA", [128, CH * TOK], BF16)
    b.tmB = sb("tmB", [128, CH * TOK], BF16)
    b.xchunk = sb("xchunk", [128, D], F32)
    b.lnchunk = sb("lnchunk", [128, D], BF16)
    b.mvchunk = [sb(f"mv{i}", [128, 512], BF16) for i in range(2)]
    b.epi_x = [sb(f"epix{i}", [128, 512], F32) for i in range(2)]
    b.epi_y = sb("epiy", [128, 512], F32)
    b.browf = sb("browf", [1, 512], F32)
    b.browbf = sb("browbf", [1, D], BF16)
    b.onesbf = sb("onesbf", [1, 128], BF16)
    for n in ("ssum", "ssq", "mean", "var", "msq", "std", "rstd", "nmr"):
        b[n] = sb(n, [128, TT], F32)
    b.wchunk_f32 = sb("wcf", [128, D // 2], F32)
    b.wchunk_bf = sb("wcb", [128, D], BF16)
    b.wd_i32 = sb("wdi", [128, 2048], I32)
    b.wd_lo = sb("wdlo", [128, 2048], I32)
    b.wd_sc = sb("wdsc", [128, 32], F32)
    b.wd_m8 = sb("wdm8", [128, 32], F32)
    b.psum = [es.enter_context(nc.psum_tensor(f"pb{i}", [128, 512], F32)).ap() for i in range(8)]
    b.bank_guard = {i: [] for i in range(8)}


def new_builder(nc):
    b = NS()
    b.prog = Prog()
    b.wprev = {}
    b.lnprev = {}
    b.mv_prev = [(), ()]
    b.nc = nc
    return b



def build_launchA():
    nc = bass.Bass("TRN2", target_bir_lowering=False, debug=False)
    es = ExitStack()
    b = new_builder(nc)

    b.xin = nc.dram_tensor("xin", [TOK, D], F32, kind="ExternalInput").ap()
    b.had = nc.dram_tensor("had", [D, D], F32, kind="ExternalInput").ap()
    b.wdata = nc.dram_tensor("wdata", [D, 2048], I32, kind="ExternalInput").ap()
    b.scales = nc.dram_tensor("scales", [D, 32], F32, kind="ExternalInput").ap()
    b.ffn0_bias = nc.dram_tensor("fbias", [D], F32, kind="ExternalInput").ap()
    b.xout = nc.dram_tensor("xout", [TOK, D], F32, kind="ExternalOutput").ap()

    b.had_bf = nc.dram_tensor("had_bf", [D, D], BF16, kind="Internal").ap()
    b.wdq_bf = nc.dram_tensor("wdq_bf", [D, D], BF16, kind="Internal").ap()
    b.ln_dram = nc.dram_tensor("ln_dram", [TOK, D], BF16, kind="Internal").ap()
    b.xr_dram = nc.dram_tensor("xr_dram", [TOK, D], BF16, kind="Internal").ap()

    alloc_buffers(b, nc, es)

    wt = prep_cast_bf16(b, b.had, b.had_bf, "had") + prep_wdq(b)
    ffn0_sublayer(b, b.xin, b.xout, wt)

    b.prog.wait("gpsimd", b.prog.barrier_token())
    build_block(nc, b.prog, es)
    es.close()
    return nc


def run_launchA(x_slices, had, wdata, scales, fbias):
    nc = build_launchA()
    in_maps = [{"xin": x_slices[i], "had": had, "wdata": wdata,
                "scales": scales, "fbias": fbias} for i in range(len(x_slices))]
    res = run_bass_kernel_spmd(nc, in_maps, core_ids=list(range(len(x_slices))))
    return [r["xout"] for r in res.results]





TK = 2048
NKB = TK // 128
INV_SQRT_DH = 0.08838834764831845
C_SHIFT = 40.0
BANDW = 3456

def carve(b, name):
    e0, e1, dt = b.carves[name]
    ap = b.tmB[:, e0:e1]
    return ap.bitcast(F32) if dt == "f32" else ap

def setup_carves(b):
    b.carves = {
        "pTa":   (0, 8192, "bf"),
        "pT":    (8192, 24576, "bf"),
        "kh":    (24576, 26624, "bf"),
        "vh":    (26624, 28688, "bf"),
        "aband": (28688, 32144, "bf"),
        "btmp0": (0, 3456, "bf"),
        "btmp1": (3456, 6912, "bf"),
        "bones": (8192, 11648, "bf"),
        "kstg":  (0, 2048, "bf"),
        "cosb":  (24576, 25600, "bf"),
        "sinb":  (25600, 26624, "bf"),
        "vst0":  (26624, 27140, "bf"),
        "vst1":  (27140, 27656, "bf"),
        "ropeA": (27688, 28200, "f32"),
        "ropeB": (28200, 28712, "f32"),
        "qstg":  (28712, 29224, "bf"),
        "cosb4": (8192, 12288, "bf"),
        "sinb4": (12288, 16384, "bf"),
        "ropeW": (16384, 18432, "f32"),
        "kstg2": (2048, 4096, "bf"),
    }

def broadcast_scalar(b, src_dram11, dst_f32col, bank, deps):
    p = b.prog
    ld = p.d("sync", b.sc11[0:1, 0:1], src_dram11[0:1, 0:1],
             deps=list(deps) + list(b.get("sc11_guard", ())), sem="sc11")
    on = p.c("vector", lambda em: em.memset(b.onesf1[0:1, 0:128], 1.0), deps=())
    mm = p.c("tensor", lambda em, bank=bank: em.matmul(
        b.psum[bank][:, 0:1], b.onesf1[0:1, 0:128], b.sc11[0:1, 0:1],
        start=True, stop=True), deps=[ld, on] + b.bank_guard[bank])
    cp = p.c("vector", lambda em, bank=bank: em.tensor_copy(dst_f32col, b.psum[bank][:, 0:1]), deps=[mm])
    b.bank_guard[bank] = [cp]
    b.sc11_guard = [mm]
    return [cp]

def build_rope_tables(b, deps):
    p = b.prog
    cosb, sinb = carve(b, "cosb"), carve(b, "sinb")
    tA, tB = carve(b, "ropeA"), carve(b, "ropeB")
    th = tA[:, 0:128]
    ang = tA[:, 128:256]
    u = tB[:, 0:128]
    red = tB[:, 128:256]
    gi = b.gint
    pv = broadcast_scalar(b, b.poff, b.poffv[:, 0:1], 7, deps)
    j0 = p.c("gpsimd", lambda em: em.iota(th, pattern=[[1, 128]], base=0, channel_multiplier=0,
                                          allow_small_or_imprecise_dtypes=True), deps=())
    j1 = p.c("vector", lambda em: em.tensor_single_scalar(ang, th, 64.0, ALU.is_ge), deps=[j0])
    j2 = p.c("vector", lambda em: em.scalar_tensor_tensor(th, ang, -64.0, th, ALU.mult, ALU.add), deps=[j1])
    j3 = p.c("scalar", lambda em: em.activation(th, th, AF.Exp, bias=0.0,
                                                scale=-2.0 * float(np.log(10000.0)) / 128.0), deps=[j2])
    prev = [j3]
    for t in range(TT):
        ps = p.c("gpsimd", lambda em, t=t: em.iota(b.posv[:, 0:1], pattern=[[0, 1]], base=128 * t,
                                                   channel_multiplier=1,
                                                   allow_small_or_imprecise_dtypes=True), deps=prev)
        pa = p.c("vector", lambda em: em.tensor_add(b.posv[:, 1:2], b.posv[:, 0:1], b.poffv[:, 0:1]),
                 deps=[ps] + pv)
        for trig in range(2):
            if trig == 0:
                a0 = p.c("vector", lambda em: em.tensor_scalar_mul(ang, th, b.posv[:, 1:2]),
                         deps=[pa, j3] + prev)
            else:
                a0 = p.c("vector", lambda em: em.tensor_scalar(ang, th, b.posv[:, 1:2],
                         float(np.pi / 2), ALU.mult, ALU.add), deps=[pa, j3] + prev)
            u1 = p.c("vector", lambda em: em.tensor_scalar(u, ang, 1.0 / (2 * np.pi), 0.5,
                                                           ALU.mult, ALU.add), deps=[a0])
            i1 = p.c("vector", lambda em: em.tensor_copy(b.kint[:, 0:128], u), deps=[u1])
            g1 = p.c("vector", lambda em: em.tensor_tensor(red, b.kint[:, 0:128], u, ALU.is_gt), deps=[i1])
            k1 = p.c("vector", lambda em: em.tensor_sub(u, b.kint[:, 0:128], red), deps=[g1])
            r1 = p.c("vector", lambda em: em.scalar_tensor_tensor(red, u, -6.28125, ang,
                                                                  ALU.mult, ALU.add), deps=[k1])
            r2 = p.c("vector", lambda em: em.scalar_tensor_tensor(red, u, -0.0019353071795864769, red,
                                                                  ALU.mult, ALU.add), deps=[r1])
            dst = (cosb if trig else sinb)[:, 128 * t:128 * (t + 1)]
            s1 = p.c("scalar", lambda em, dst=dst: em.activation(dst, red, AF.Sin, bias=0.0, scale=1.0),
                     deps=[r2])
            prev = [s1]
    return [prev[0]]

def build_band(b, deps):
    p = b.prog
    band = carve(b, "btmp0")
    tmp = carve(b, "btmp1")
    onesw = carve(b, "bones")
    aband = carve(b, "aband")
    g0 = broadcast_scalar(b, b.g0in, b.gvec[:, 0:1], 6, deps)
    g1 = broadcast_scalar(b, b.g1in, b.gvec[:, 1:2], 6, deps)
    on = p.c("vector", lambda em: em.memset(onesw, 1.0), deps=())
    a0 = p.c("gpsimd", lambda em: em.affine_select(band, onesw, pattern=[[1, BANDW]],
             compare_op=ALU.is_ge, fill=0.0, channel_multiplier=-1, base=-1920), deps=[on])
    m0 = p.c("vector", lambda em: em.tensor_scalar_mul(band, band, b.gvec[:, 0:1]), deps=[a0] + g0)
    a1 = p.c("gpsimd", lambda em: em.affine_select(tmp, onesw, pattern=[[1, BANDW]],
             compare_op=ALU.is_ge, fill=0.0, channel_multiplier=-1, base=-896), deps=[on, m0])
    m1 = p.c("vector", lambda em: em.tensor_scalar_mul(tmp, tmp, b.gvec[:, 1:2]), deps=[a1] + g1)
    m2 = p.c("vector", lambda em: em.tensor_add(band, band, tmp), deps=[m1])
    m3 = p.c("vector", lambda em: em.tensor_scalar(aband, band, 1e30, -1e30, ALU.mult, ALU.add),
             deps=[m2])
    return [m3]

def band_slice(b, t2b, t1b, width):
    uu0 = 128 * (t1b - t2b) + 1920
    return carve(b, "aband")[:, uu0:uu0 + width]


def ep_rope(b, dst_dram, wname):
    p = b.prog
    cosb, sinb = carve(b, "cosb"), carve(b, "sinb")
    tA, tB = carve(b, "ropeA"), carve(b, "ropeB")
    qstg = carve(b, "qstg")
    if "rope_state" not in b:
        b.rope_state = {}
    state = b.rope_state

    def ep(t, og, ps, mm_tok):
        toks = []
        prev = state.get("dma", ())
        cos_t = cosb[:, 128 * t:128 * (t + 1)]
        sin_t = sinb[:, 128 * t:128 * (t + 1)]
        last = ()
        for hh in range(4):
            ph = ps[:, 128 * hh:128 * (hh + 1)]
            tAh, tBh = tA[:, 0:128], tA[:, 128:256]
            c1 = p.c("vector", lambda em, ph=ph, tAh=tAh, cos_t=cos_t: em.tensor_tensor(
                tAh, ph, cos_t, ALU.mult), deps=[mm_tok] + list(prev) + list(last))
            c2 = p.c("vector", lambda em, ph=ph, tBh=tBh, sin_t=sin_t: em.scalar_tensor_tensor(
                tBh[:, 0:64], ph[:, 64:128], -1.0, sin_t[:, 0:64], ALU.mult, ALU.mult), deps=[mm_tok, c1])
            c3 = p.c("vector", lambda em, ph=ph, tBh=tBh, sin_t=sin_t: em.tensor_tensor(
                tBh[:, 64:128], ph[:, 0:64], sin_t[:, 64:128], ALU.mult), deps=[mm_tok, c2])
            c4 = p.c("vector", lambda em, hh=hh, tAh=tAh, tBh=tBh: em.tensor_add(
                qstg[:, 128 * hh:128 * (hh + 1)], tAh, tBh), deps=[c3])
            last = [c4]
            toks.append(c4)
        dm = p.d("gpsimd", dst_dram[128 * t:128 * (t + 1), 512 * og:512 * (og + 1)],
                 qstg[:, 0:512], deps=toks, sem="rope_st")
        state["dma"] = [dm]
        return [toks[0], dm]
    return ep

def ep_vaug(b, dst_dram):
    p = b.prog
    state = {"i": 0}

    def ep(t, og, ps, mm_tok):
        i = state["i"] % 2
        state["i"] += 1
        vst = carve(b, f"vst{i}")
        prev = state.get(("dma", i), ())
        toks = []
        for hh in range(4):
            ct = p.c("scalar", lambda em, hh=hh, ps=ps, vst=vst: em.activation(
                vst[:, 129 * hh:129 * hh + 128], ps[:, 128 * hh:128 * (hh + 1)],
                AF.Copy, bias=0.0, scale=1.0), deps=[mm_tok] + list(prev))
            toks.append(ct)
        dm = p.d("gpsimd", dst_dram[128 * t:128 * (t + 1), 516 * og:516 * (og + 1)],
                 vst[:, 0:516], deps=toks, sem=f"vaug_st{i}")
        state[("dma", i)] = [dm]
        return [toks[0], dm]
    return ep

def attn0_launchB(b, xsrc, wq_bf, wk_bf, wv_bf, wprep):
    p = b.prog
    st = ln_stats(b, xsrc)
    fm_ready = ln_normalize_to_fm(b, xsrc, b.fmA, st)
    rt = build_rope_tables(b, ())
    vst0, vst1 = carve(b, "vst0"), carve(b, "vst1")
    on0 = p.c("vector", lambda em, vst0=vst0: em.memset(vst0[:, 0:516], 1.0), deps=())
    on1 = p.c("vector", lambda em, vst1=vst1: em.memset(vst1[:, 0:516], 1.0), deps=())

    def mv(wbf):
        def f(og, c, buf, prev):
            i = (og * CH + c) % 2
            return p.d("sync", buf, wbf[512 * og:512 * (og + 1), 128 * c:128 * (c + 1)],
                       deps=prev + wprep, transpose=True, sem=f"mv{i}")
        return f

    p.wait("vector", rt)
    t2_matmul(b, b.fmA, fm_ready, mv(wq_bf), ep_rope(b, b.q_dram, "q"))
    t2_matmul(b, b.fmA, fm_ready, mv(wk_bf), ep_rope(b, b.k_dram, "k"))
    t2_matmul(b, b.fmA, fm_ready, mv(wv_bf), ep_vaug(b, b.vaug_dram))
    p.wait("gpsimd", [on0, on1])
    return


def attn0_launchC(b, xsrc, xdst, wo_bf, wprep):
    p = b.prog
    bd = build_band(b, ())
    pTa = carve(b, "pTa")
    pT = carve(b, "pT")
    aband = carve(b, "aband")
    qfm = []
    for c in range(CH):
        qfm.append(p.d("scalar", b.fmA[:, TOK * c:TOK * (c + 1)],
                       b.q_dram[:, 128 * c:128 * (c + 1)], deps=(), transpose=True, sem="fm_tr"))
    kstg_bufs = [carve(b, "kstg"), carve(b, "kstg2")]
    kfm_wr = []
    ks_prev = [list(bd), list(bd)]
    for c in range(CH):
        i = c % 2
        kstg = kstg_bufs[i]
        t0 = p.d("scalar", kstg[:, 0:1024], b.kt0_dram[:, 128 * c:128 * (c + 1)],
                 deps=ks_prev[i], transpose=True, sem=f"kstg{i}")
        t1 = p.d("scalar", kstg[:, 1024:2048], b.kt1_dram[:, 128 * c:128 * (c + 1)],
                 deps=ks_prev[i], transpose=True, sem=f"kstg{i}")
        wr = p.d("gpsimd", b.kfm_dram[128 * c:128 * (c + 1), :], kstg[:, 0:2048],
                 deps=[t0, t1], sem=f"kstg_wr{i}")
        ks_prev[i] = [wr]
        kfm_wr.append(wr)
    neg40 = b.cm40
    n40 = p.c("vector", lambda em: em.memset(neg40[:, 0:1], -C_SHIFT), deps=())
    kh = carve(b, "kh")
    vh = carve(b, "vh")
    kh_guard = []
    vh_guard = []
    pta_guard = {}
    pt_guard = [list(bd)]
    ao_state = {}
    ao_cnt = [0]
    fill = b.get("fill")
    for h in range(NH):
        kl = p.d("sync", kh[:, 0:2048], b.kfm_dram[128 * h:128 * (h + 1), :],
                 deps=kh_guard + kfm_wr, sem="kh")
        v0 = p.d("sync", vh.rearrange("p (a c) -> p a c", c=129)[:, 0:8, :],
                 b.vaug0_dram.rearrange("(a p) c -> p a c", p=128)[:, :, 129 * h:129 * (h + 1)],
                 deps=vh_guard, sem="vh")
        v1 = p.d("sync", vh.rearrange("p (a c) -> p a c", c=129)[:, 8:16, :],
                 b.vaug1_dram.rearrange("(a p) c -> p a c", p=128)[:, :, 129 * h:129 * (h + 1)],
                 deps=vh_guard, sem="vh")
        kh_guard = []
        vh_guard = []
        cp_toks = []
        for half in range(2):
            ex_toks = []
            for t2q in range(8):
                t2b = 8 * half + t2q
                for t1h in range(2):
                    bank = 4 + (t2q * 2 + t1h) % 4
                    mm = p.c("tensor", lambda em, h=h, t2b=t2b, t1h=t1h, bank=bank: em.matmul(
                        b.psum[bank][:, 0:512], kh[:, 128 * t2b:128 * (t2b + 1)],
                        b.fmA[:, TOK * h + 512 * t1h: TOK * h + 512 * (t1h + 1)],
                        start=True, stop=True),
                        deps=[kl] + qfm + b.bank_guard[bank])
                    md = p.c("vector", lambda em, bank=bank, bs=band_slice(b, t2b, 4 * t1h, 512):
                             em.tensor_add(b.psum[bank][:, 0:512], b.psum[bank][:, 0:512], bs),
                             deps=[mm] + bd)
                    ex = p.c("scalar", lambda em, t2q=t2q, t1h=t1h, bank=bank: em.activation(
                        pTa[:, 1024 * t2q + 512 * t1h: 1024 * t2q + 512 * (t1h + 1)],
                        b.psum[bank][:, 0:512], AF.Exp, bias=neg40[:, 0:1], scale=INV_SQRT_DH),
                        deps=[md, n40] + pta_guard.get((t2q, t1h), []))
                    b.bank_guard[bank] = [ex]
                    ex_toks.append(ex)
            cp = p.d("gpsimd", pT[:, 8192 * half:8192 * (half + 1)], pTa[:, 0:8192],
                     deps=ex_toks + pt_guard[0], sem="ptcp")
            for t2q in range(8):
                for t1h in range(2):
                    pta_guard[(t2q, t1h)] = [cp]
            cp_toks.append(cp)
            if fill is not None:
                fill.pump(2)
        av_mms = []
        for pas in range(2):
            for t1bq in range(4):
                t1b = 4 * pas + t1bq
                bank = t1bq
                mms = []
                for t2b in range(NKB):
                    deps = [cp_toks[t2b // 8], v0, v1]
                    if t2b == 0:
                        deps = deps + b.bank_guard[bank]
                    mm = p.c("tensor", lambda em, t1b=t1b, t2b=t2b, bank=bank: em.matmul(
                        b.psum[bank][:, 0:129],
                        pT[:, 1024 * t2b + 128 * t1b: 1024 * t2b + 128 * (t1b + 1)],
                        vh[:, 129 * t2b:129 * (t2b + 1)],
                        start=(t2b == 0), stop=(t2b == NKB - 1)), deps=deps)
                    mms.append(mm)
                av_mms.append(mms[-1])
                i = ao_cnt[0] % 2
                ao_cnt[0] += 1
                prev = ao_state.get(i, ())
                rc0 = p.c("scalar", lambda em, bank=bank: em.activation(
                    b.rcpl[:, 1:2], b.psum[bank][:, 128:129], AF.Ln, bias=0.0, scale=1.0),
                    deps=[mms[-1]] + list(prev) + list(ao_state.get("rc", ())))
                rc = p.c("scalar", lambda em: em.activation(
                    b.rcpl[:, 0:1], b.rcpl[:, 1:2], AF.Exp, bias=0.0, scale=-1.0),
                    deps=[rc0] + list(ao_state.get("sc", ())))
                sc = p.c("vector", lambda em, bank=bank, i=i: em.tensor_scalar_mul(
                    b.aostg[i][:, 0:128], b.psum[bank][:, 0:128], b.rcpl[:, 0:1]), deps=[rc])
                dm = p.d("gpsimd", b.ao_dram[128 * t1b:128 * (t1b + 1), 128 * h:128 * (h + 1)],
                         b.aostg[i][:, 0:128], deps=[sc], sem=f"ao_st{i}")
                b.bank_guard[bank] = [sc]
                ao_state[i] = [dm]
                ao_state["sc"] = [sc]
                ao_state["rc"] = [rc]
                kh_guard.append(mms[-1])
                vh_guard.append(mms[-1])
        pt_guard[0] = av_mms
        if fill is not None:
            fill.pump(2)
    if b.get("SKIP_WO"):
        return
    ao_done = p.barrier_token()
    aofm = []
    for c in range(CH):
        aofm.append(p.d("scalar", b.fmA[:, TOK * c:TOK * (c + 1)],
                        b.ao_dram[:, 128 * c:128 * (c + 1)], deps=ao_done, transpose=True, sem="fm_tr"))

    def mv_wo(og, c, buf, prev):
        i = (og * CH + c) % 2
        return p.d("sync", buf, wo_bf[512 * og:512 * (og + 1), 128 * c:128 * (c + 1)],
                   deps=prev + wprep, transpose=True, sem=f"mv{i}")

    ep_state = {}

    def ep_wo(t, og, ps, mm_tok):
        i = t % 2
        prev = ep_state.get(i, ())
        ld = p.d("gpsimd", b.epi_x[i][:, 0:512],
                 xsrc[128 * t:128 * (t + 1), 512 * og:512 * (og + 1)], deps=prev, sem=f"epix{i}")
        a1 = p.c("vector", lambda em, i=i, ps=ps: em.tensor_add(
            b.epi_y[:, 0:512], ps, b.epi_x[i][:, 0:512]),
            deps=[ld, mm_tok] + list(ep_state.get("y", ())))
        stq = p.d("gpsimd", xdst[128 * t:128 * (t + 1), 512 * og:512 * (og + 1)],
                  b.epi_y[:, 0:512], deps=[a1], sem="epi_st")
        ep_state[i] = [ld, a1, stq]
        ep_state["y"] = [stq]
        return [a1, stq]

    t2_matmul(b, b.fmA, aofm, mv_wo, ep_wo)
    return


def alloc_attn_buffers(b, nc, es):
    sb = lambda name, shape, dt: es.enter_context(nc.sbuf_tensor(name, shape, dt)).ap()
    b.sc11 = sb("sc11", [1, 1], F32)
    b.onesf1 = sb("onesf1", [1, 128], F32)
    b.poffv = sb("poffv", [128, 1], F32)
    b.posv = sb("posv", [128, 2], F32)
    b.gvec = sb("gvec", [128, 2], F32)
    b.kint = sb("kint", [128, 128], I32)
    b.gint = None
    b.cm40 = sb("cm40", [128, 1], F32)
    b.rcpl = sb("rcpl", [128, 2], F32)
    b.aostg = [sb(f"aostg{i}", [128, 128], BF16) for i in range(2)]


def build_launchB():
    nc = bass.Bass("TRN2", target_bir_lowering=False, debug=False)
    es = ExitStack()
    b = new_builder(nc)
    b.xin = nc.dram_tensor("xin", [TOK, D], F32, kind="ExternalInput").ap()
    b.wq = nc.dram_tensor("wq", [D, D], F32, kind="ExternalInput").ap()
    b.wk = nc.dram_tensor("wk", [D, D], F32, kind="ExternalInput").ap()
    b.wv = nc.dram_tensor("wv", [D, D], F32, kind="ExternalInput").ap()
    b.poff = nc.dram_tensor("poff", [1, 1], F32, kind="ExternalInput").ap()
    b.q_dram = nc.dram_tensor("q_out", [TOK, D], BF16, kind="ExternalOutput").ap()
    b.k_dram = nc.dram_tensor("k_out", [TOK, D], BF16, kind="ExternalOutput").ap()
    b.vaug_dram = nc.dram_tensor("v_out", [TOK, 4128], BF16, kind="ExternalOutput").ap()
    b.wq_bf = nc.dram_tensor("wq_bf", [D, D], BF16, kind="Internal").ap()
    b.wk_bf = nc.dram_tensor("wk_bf", [D, D], BF16, kind="Internal").ap()
    b.wv_bf = nc.dram_tensor("wv_bf", [D, D], BF16, kind="Internal").ap()
    b.ln_dram = nc.dram_tensor("ln_dram", [TOK, D], BF16, kind="Internal").ap()
    alloc_buffers(b, nc, es)
    alloc_attn_buffers(b, nc, es)
    setup_carves(b)
    wt = (prep_cast_bf16(b, b.wq, b.wq_bf, "wq") + prep_cast_bf16(b, b.wk, b.wk_bf, "wk")
          + prep_cast_bf16(b, b.wv, b.wv_bf, "wv"))
    attn0_launchB(b, b.xin, b.wq_bf, b.wk_bf, b.wv_bf, wt)
    b.prog.wait("gpsimd", b.prog.barrier_token())
    build_block(nc, b.prog, es)
    es.close()
    return nc


def build_launchC():
    nc = bass.Bass("TRN2", target_bir_lowering=False, debug=False)
    es = ExitStack()
    b = new_builder(nc)
    b.xin = nc.dram_tensor("xin", [TOK, D], F32, kind="ExternalInput").ap()
    b.wo = nc.dram_tensor("wo", [D, D], F32, kind="ExternalInput").ap()
    b.q_dram = nc.dram_tensor("q_in", [TOK, D], BF16, kind="ExternalInput").ap()
    b.kt0_dram = nc.dram_tensor("kt0", [TOK, D], BF16, kind="ExternalInput").ap()
    b.kt1_dram = nc.dram_tensor("kt1", [TOK, D], BF16, kind="ExternalInput").ap()
    b.vaug0_dram = nc.dram_tensor("vaug0", [TOK, 4128], BF16, kind="ExternalInput").ap()
    b.vaug1_dram = nc.dram_tensor("vaug1", [TOK, 4128], BF16, kind="ExternalInput").ap()
    b.g0in = nc.dram_tensor("g0in", [1, 1], F32, kind="ExternalInput").ap()
    b.g1in = nc.dram_tensor("g1in", [1, 1], F32, kind="ExternalInput").ap()
    b.xout = nc.dram_tensor("xout", [TOK, D], F32, kind="ExternalOutput").ap()
    b.wo_bf = nc.dram_tensor("wo_bf", [D, D], BF16, kind="Internal").ap()
    b.ao_dram = nc.dram_tensor("ao_dram", [TOK, D], BF16, kind="ExternalOutput").ap()
    b.kfm_dram = nc.dram_tensor("kfm_dram", [D, TK], BF16, kind="Internal").ap()
    alloc_buffers(b, nc, es)
    alloc_attn_buffers(b, nc, es)
    setup_carves(b)
    import os
    b.HEAD_BARRIER = bool(int(os.environ.get("KERN_HEAD_BARRIER", "0")))
    b.STAGE_BARRIER = bool(int(os.environ.get("KERN_STAGE_BARRIER", "0")))
    b.AV_ALLGATES = bool(int(os.environ.get("KERN_AV_ALLGATES", "0")))
    b.SKIP_WO = bool(int(os.environ.get("KERN_SKIP_WO", "0")))
    b.PT_FULLGUARD = bool(int(os.environ.get("KERN_PT_FULLGUARD", "0")))
    b.KH_SERIAL = bool(int(os.environ.get("KERN_KH_SERIAL", "0")))
    b.GATE_POOL = bool(int(os.environ.get("KERN_GATE_POOL", "0")))
    b.SKIP_PREP = b.SKIP_WO
    wt = [] if b.SKIP_PREP else prep_cast_bf16(b, b.wo, b.wo_bf, "wo")
    attn0_launchC(b, b.xin, b.xout, b.wo_bf, wt)
    b.prog.wait("gpsimd", b.prog.barrier_token())
    build_block(nc, b.prog, es)
    es.close()
    return nc


def run_attn0(x_slices, wq, wk, wv, wo):
    n = len(x_slices)
    ncB = build_launchB()
    in_mapsB = []
    for i in range(n):
        poff = np.array([[1024.0 * (i % 2)]], dtype=np.float32)
        in_mapsB.append({"xin": x_slices[i], "wq": wq, "wk": wk, "wv": wv, "poff": poff})
    resB = run_bass_kernel_spmd(ncB, in_mapsB, core_ids=list(range(n)))
    ncC = build_launchC()
    in_mapsC = []
    for i in range(n):
        pair = i // 2
        e, o = 2 * pair, 2 * pair + 1
        g0 = np.array([[1.0 if i % 2 == 0 else 0.0]], dtype=np.float32)
        g1 = np.array([[0.0 if i % 2 == 0 else 1.0]], dtype=np.float32)
        in_mapsC.append({
            "xin": x_slices[i], "wo": wo,
            "q_in": resB.results[i]["q_out"],
            "kt0": resB.results[e]["k_out"], "kt1": resB.results[o]["k_out"],
            "vaug0": resB.results[e]["v_out"], "vaug1": resB.results[o]["v_out"],
            "g0in": g0, "g1in": g1})
    resC = run_bass_kernel_spmd(ncC, in_mapsC, core_ids=list(range(n)))
    return [r["xout"] for r in resC.results]





def e4m3_emulate(b, p, y_f32, out_bf, bufs, deps):
    iview = y_f32.bitcast(I32)
    e, fA, fB = bufs["i0"], bufs["f0"], bufs["f1"]
    t1 = p.c("vector", lambda em: em.tensor_single_scalar(e, iview, 0x7F800000, ALU.bitwise_and), deps=deps)
    t2 = p.c("vector", lambda em: em.tensor_scalar_max(e, e, 1015021568), deps=[t1])
    t3 = p.c("vector", lambda em: em.tensor_scalar_add(e, e, -25165824), deps=[t2])
    step = e.bitcast(F32)
    t4 = p.c("vector", lambda em: em.reciprocal(fA, step), deps=[t3])
    t5 = p.c("vector", lambda em: em.tensor_mul(fB, y_f32, fA), deps=[t4])
    t6 = p.c("vector", lambda em: em.tensor_scalar_add(fB, fB, 0.5), deps=[t5])
    I = fA.bitcast(I32)
    t7 = p.c("vector", lambda em: em.tensor_copy(I, fB), deps=[t6])
    g = y_f32
    t8 = p.c("vector", lambda em: em.tensor_tensor(g, I, fB, ALU.is_gt), deps=[t7])
    t9 = p.c("vector", lambda em: em.tensor_sub(fB, I, g), deps=[t8])
    t10 = p.c("vector", lambda em: em.tensor_mul(out_bf, fB, step), deps=[t9])
    return [t10]


def cross_partition_max(b, p, col_ap, out11, deps, tag):
    t = p.d("gpsimd", b.xrow[0:1, 0:128], col_ap,
            deps=list(deps) + list(b.get("xrow_prev", ())), sem="xpm_" + tag,
            allow_slow_non_contiguous=True)
    r = p.c("vector", lambda em: em.reduce_max(out11, b.xrow[0:1, 0:128], axis=AX.X), deps=[t])
    b.xrow_prev = [r]
    return [r]


def prep_w8(b, wsrc_f32, dst_bf):
    toks = []
    for _ in prep_w8_gen(b, wsrc_f32, dst_bf, toks):
        pass
    return toks


def prep_w8_gen(b, wsrc_f32, dst_bf, out_toks):
    p = b.prog
    H = D // 2
    stg = b.get("pstage_w8", b.scrapH)
    mx = b.wd_m8[:, 0:1]
    z = p.c("vector", lambda em: em.memset(mx, 0.0), deps=())
    run = [z]
    for k in range(CH):
        for h in range(2):
            buf = b.wchunk_f32
            prev = b.wprev.get(("wcast",), ())
            ld = p.d("sync", buf[:, 0:H], wsrc_f32[128 * k:128 * (k + 1), H * h:H * (h + 1)],
                     deps=list(prev) + run, sem="wp_ld")
            r1 = p.c("vector", lambda em, buf=buf: em.reduce_max(
                b.wd_sc[:, 0:1], buf[:, 0:H], axis=AX.X, apply_absolute_value=True), deps=[ld])
            r2 = p.c("vector", lambda em: em.tensor_max(mx, mx, b.wd_sc[:, 0:1]), deps=[r1])
            run = [r2]
            b.wprev[("wcast",)] = [r1]
        yield
    cm = cross_partition_max(b, p, mx, b.s11[0:1, 0:1], run, "w8")
    c1 = p.c("vector", lambda em: em.tensor_scalar_max(b.s11[0:1, 0:1], b.s11[0:1, 0:1], 1e-12), deps=cm)
    c2 = p.c("vector", lambda em: em.reciprocal(b.s11[0:1, 1:2], b.s11[0:1, 0:1]), deps=[c1])
    c3 = p.c("vector", lambda em: em.tensor_scalar_mul(b.s11[0:1, 1:2], b.s11[0:1, 1:2], 448.0), deps=[c2])
    toks = out_toks
    bufs = {"i0": b.wd_i32[:, 0:H], "f0": b.wd_lo.bitcast(F32)[:, 0:H],
            "f1": b.wchunk_bf.bitcast(F32)[:, 0:H]}
    yh = b.lnchunk.bitcast(F32)[:, 0:H]
    for k in range(CH):
        for h in range(2):
            prev = b.wprev.get(("wcast",), ())
            ld = p.d("sync", b.wchunk_f32[:, 0:H], wsrc_f32[128 * k:128 * (k + 1), H * h:H * (h + 1)],
                     deps=list(prev) + [c3] + toks[-1:]
                     + list(b.lnprev.get(0, ()))[-2:], sem="wp_ld")
            if k == 0 and h == 0:
                bc = broadcast_scalar_sb(b, b.s11[0:1, 1:2], b.s448w[:, 0:1], 7, [c3])
                b.wprev[("s448w",)] = bc
            y1 = p.c("vector", lambda em: em.tensor_scalar_mul(
                yh, b.wchunk_f32[:, 0:H], b.s448w[:, 0:1]), deps=[ld] + b.wprev[("s448w",)])
            qt = e4m3_emulate(b, p, yh, stg, bufs, [y1])
            st = p.d("gpsimd", dst_bf[128 * k:128 * (k + 1), H * h:H * (h + 1)],
                     stg, deps=qt, sem="wp_st")
            b.wprev[("wcast",)] = [st] + qt
            b.lnprev[0] = list(qt) + [st]
            toks.append(st)
        yield


def broadcast_scalar_sb(b, src11_sb, dst_col, bank, deps):
    p = b.prog
    on = b.get("onesf1_tok")
    if on is None:
        on = [p.c("vector", lambda em: em.memset(b.onesf1[0:1, 0:128], 1.0), deps=())]
        b.onesf1_tok = on
    mm = p.c("tensor", lambda em, bank=bank: em.matmul(
        b.psum[bank][:, 0:1], b.onesf1[0:1, 0:128], src11_sb,
        start=True, stop=True), deps=list(deps) + on + b.bank_guard[bank])
    cp = p.c("vector", lambda em, bank=bank: em.tensor_copy(dst_col, b.psum[bank][:, 0:1]), deps=[mm])
    b.bank_guard[bank] = [cp]
    return [cp]


def ln_absmax(b, xsrc, stat_toks):
    p = b.prog
    H = D // 2
    mx = b.wd_m8[:, 1:2]
    yh = b.lnchunk.bitcast(F32)[:, 0:H]
    z = p.c("vector", lambda em: em.memset(mx, 0.0), deps=())
    run = [z]
    for t in range(TT):
        for h in range(2):
            xc = b.xchunk
            prev = b.lnprev.get(0, ())
            ld = p.d("sync", xc[:, 0:H], xsrc[128 * t:128 * (t + 1), H * h:H * (h + 1)],
                     deps=prev, sem="x_ld")
            ct = p.c("scalar", lambda em, t=t: em.activation(
                yh, xc[:, 0:H], AF.Identity, bias=b.nmr[:, t:t + 1], scale=b.rstd[:, t:t + 1]),
                deps=[ld] + stat_toks + run)
            r1 = p.c("vector", lambda em: em.reduce_max(
                b.wd_sc[:, 1:2], yh, axis=AX.X, apply_absolute_value=True), deps=[ct])
            r2 = p.c("vector", lambda em: em.tensor_max(mx, mx, b.wd_sc[:, 1:2]), deps=[r1])
            run = [r2]
            b.lnprev[0] = [ct, r1]
    return cross_partition_max(b, p, mx, b.s11[0:1, 2:3], run, "lnx")


def l1ffn_main(b, xsrc, xdst, amax8_in, w8_toks, st=None, amax_w=8, amax_deps=()):
    p = b.prog
    if st is None:
        st = ln_stats(b, xsrc)
    la = p.d("sync", b.xrow[0:1, 0:amax_w], amax8_in[0:1, 0:amax_w],
             deps=list(amax_deps) + list(b.get("xrow_prev", ())), sem="amax8")
    a1 = p.c("vector", lambda em, amax_w=amax_w: em.reduce_max(
        b.s11[0:1, 2:3], b.xrow[0:1, 0:amax_w], axis=AX.X), deps=[la])
    b.xrow_prev = [a1]
    a2 = p.c("vector", lambda em: em.tensor_scalar_max(b.s11[0:1, 2:3], b.s11[0:1, 2:3], 1e-12), deps=[a1])
    a3 = p.c("vector", lambda em: em.reciprocal(b.s11[0:1, 3:4], b.s11[0:1, 2:3]), deps=[a2])
    a4 = p.c("vector", lambda em: em.tensor_scalar_mul(b.s11[0:1, 3:4], b.s11[0:1, 3:4], 448.0), deps=[a3])
    bcx = broadcast_scalar_sb(b, b.s11[0:1, 3:4], b.s448x[:, 0:1], 7, [a4])
    s1 = p.c("vector", lambda em: em.tensor_scalar_mul(b.rs2[:, 0:TT], b.rstd[:, 0:TT], b.s448x[:, 0:1]),
             deps=st + bcx)
    s2 = p.c("vector", lambda em: em.tensor_scalar_mul(b.nb2[:, 0:TT], b.nmr[:, 0:TT], b.s448x[:, 0:1]),
             deps=st + bcx)
    H = D // 2
    bufs = {"i0": b.wd_i32[:, 0:H], "f0": b.wd_lo.bitcast(F32)[:, 0:H],
            "f1": b.wchunk_bf.bitcast(F32)[:, 0:H]}
    yh = b.lnchunk.bitcast(F32)[:, 0:H]
    wr = []
    for t in range(TT):
        for h in range(2):
            xc = b.xchunk
            prev = b.lnprev.get(0, ())
            ld = p.d("sync", xc[:, 0:H], xsrc[128 * t:128 * (t + 1), H * h:H * (h + 1)],
                     deps=prev, sem="x_ld")
            y1 = p.c("scalar", lambda em, t=t: em.activation(
                yh, xc[:, 0:H], AF.Identity, bias=b.nb2[:, t:t + 1], scale=b.rs2[:, t:t + 1]),
                deps=[ld, s1, s2])
            qt = e4m3_emulate(b, p, yh, b.scrapH, bufs, [y1])
            stq = p.d("gpsimd", b.ln_dram[128 * t:128 * (t + 1), H * h:H * (h + 1)],
                      b.scrapH, deps=qt, sem="ln_st")
            b.lnprev[0] = [y1] + qt + [stq]
            wr.append(stq)
    fm = []
    for c in range(CH):
        fm.append(p.d("scalar", b.fmA[:, TOK * c:TOK * (c + 1)],
                      b.ln_dram[:, 128 * c:128 * (c + 1)], deps=wr, transpose=True, sem="fm_tr"))
    cw1 = p.c("vector", lambda em: em.tensor_mul(b.s11[0:1, 4:5], b.s11[0:1, 3:4], b.s11[0:1, 1:2]),
              deps=[a4] + w8_toks)
    sv1 = p.c("vector", lambda em: em.reciprocal(b.s11[0:1, 5:6], b.s11[0:1, 4:5]), deps=[cw1])
    sv2 = p.c("vector", lambda em: em.tensor_scalar_mul(b.s11[0:1, 5:6], b.s11[0:1, 5:6], 0.5), deps=[sv1])
    hs = broadcast_scalar_sb(b, b.s11[0:1, 5:6], b.halfS[:, 0:1], 6, [sv2])
    bb = load_bias_row_scaled(b, b.ffn1_bias, [cw1])
    def mv_w8(og, c, buf, prev):
        i = (og * CH + c) % 2
        return p.d("sync", buf, b.w8_bf[512 * og:512 * (og + 1), 128 * c:128 * (c + 1)],
                   deps=prev + w8_toks, transpose=True, sem=f"mv{i}")
    ep_state = {}
    def ep(t, og, ps, mm_tok):
        i = t % 2
        prev = ep_state.get(i, ())
        ld = p.d("gpsimd", b.epi_x[i][:, 0:512],
                 xsrc[128 * t:128 * (t + 1), 512 * og:512 * (og + 1)], deps=prev, sem=f"epix{i}")
        a1_ = p.c("vector", lambda em, i=i, ps=ps: em.scalar_tensor_tensor(
            b.epi_y[:, 0:512], ps, b.halfS[:, 0:1], b.epi_x[i][:, 0:512], ALU.mult, ALU.add),
            deps=[ld, mm_tok] + hs + list(ep_state.get("y", ())))
        stq = p.d("gpsimd", xdst[128 * t:128 * (t + 1), 512 * og:512 * (og + 1)],
                  b.epi_y[:, 0:512], deps=[a1_], sem="epi_st")
        ep_state[i] = [ld, a1_, stq]
        ep_state["y"] = [stq]
        return [a1_, stq]
    t2_matmul(b, b.fmA, fm, mv_w8, ep, bias_row=(b.browbf, bb))
    return


def load_bias_row_scaled(b, bias_row_dram, scale_tok):
    p = b.prog
    scale_tok = list(scale_tok) + list(b.get("zero11_tok", ()))
    on = b.get("onesbf_tok")
    if on is None:
        on = [p.c("vector", lambda em: em.memset(b.onesbf[0:1, 0:128], 1.0), deps=())]
        b.onesbf_tok = on
    toks = list(on)
    prev = ()
    for g in range(D // 512):
        ld = p.d("sync", b.browf[0:1, 0:512], bias_row_dram[None, 512 * g:512 * (g + 1)],
                 deps=list(prev), sem="bias_ld")
        ct = p.c("scalar", lambda em, g=g: em.activation(
            b.browbf[0:1, 512 * g:512 * (g + 1)], b.browf[0:1, 0:512], AF.Identity,
            bias=b.zero11[0:1, 0:1], scale=b.s11[0:1, 4:5]), deps=[ld] + list(scale_tok))
        prev = [ct]
        toks.append(ct)
    return toks


def alloc_l1_buffers(b, nc, es):
    sb = lambda name, shape, dt: es.enter_context(nc.sbuf_tensor(name, shape, dt)).ap()
    if "onesf1" not in b:
        b.onesf1 = sb("onesf1", [1, 128], F32)
    b.s11 = sb("s11", [1, 8], F32)
    b.xrow = sb("xrow", [1, 128], F32)
    b.s448w = sb("s448w", [128, 1], F32)
    b.s448x = sb("s448x", [128, 1], F32)
    b.halfS = sb("halfS", [128, 1], F32)
    b.rs2 = sb("rs2", [128, TT], F32)
    b.nb2 = sb("nb2", [128, TT], F32)
    b.zero11 = sb("zero11", [1, 1], F32)
    b.scrapH = b.tmB[:, 0:D // 2]


def build_launchD1():
    nc = bass.Bass("TRN2", target_bir_lowering=False, debug=False)
    es = ExitStack()
    b = new_builder(nc)
    b.xin = nc.dram_tensor("xin", [TOK, D], F32, kind="ExternalInput").ap()
    b.amax_out = nc.dram_tensor("amax_out", [1, 1], F32, kind="ExternalOutput").ap()
    alloc_buffers(b, nc, es)
    alloc_l1_buffers(b, nc, es)
    st = ln_stats(b, b.xin)
    am = ln_absmax(b, b.xin, st)
    p = b.prog
    p.d("gpsimd", b.amax_out[0:1, 0:1], b.s11[0:1, 2:3], deps=am, sem="amx_st")
    p.wait("gpsimd", p.barrier_token())
    build_block(nc, b.prog, es)
    es.close()
    return nc


def build_launchD2():
    nc = bass.Bass("TRN2", target_bir_lowering=False, debug=False)
    es = ExitStack()
    b = new_builder(nc)
    b.xin = nc.dram_tensor("xin", [TOK, D], F32, kind="ExternalInput").ap()
    b.wffn = nc.dram_tensor("wffn", [D, D], F32, kind="ExternalInput").ap()
    b.ffn1_bias = nc.dram_tensor("fbias1", [D], F32, kind="ExternalInput").ap()
    b.amax8 = nc.dram_tensor("amax8", [1, 8], F32, kind="ExternalInput").ap()
    b.xout = nc.dram_tensor("xout", [TOK, D], F32, kind="ExternalOutput").ap()
    b.w8_bf = nc.dram_tensor("w8_bf", [D, D], BF16, kind="Internal").ap()
    b.ln_dram = nc.dram_tensor("ln_dram", [TOK, D], BF16, kind="Internal").ap()
    alloc_buffers(b, nc, es)
    alloc_l1_buffers(b, nc, es)
    p = b.prog
    z = p.c("vector", lambda em: em.memset(b.zero11[0:1, 0:1], 0.0), deps=())
    b.zero11_tok = [z]
    wt = prep_w8(b, b.wffn, b.w8_bf)
    l1ffn_main(b, b.xin, b.xout, b.amax8, wt + [z])
    p.wait("gpsimd", p.barrier_token())
    build_block(nc, b.prog, es)
    es.close()
    return nc


def run_l1ffn(x_slices, weight, bias):
    n = len(x_slices)
    nc1 = build_launchD1()
    res1 = run_bass_kernel_spmd(nc1, [{"xin": x_slices[i]} for i in range(n)], core_ids=list(range(n)))
    amax8 = np.zeros((1, 8), np.float32)
    for i in range(n):
        amax8[0, i] = res1.results[i]["amax_out"][0, 0]
    nc2 = build_launchD2()
    im = [{"xin": x_slices[i], "wffn": weight, "fbias1": bias, "amax8": amax8} for i in range(n)]
    res2 = run_bass_kernel_spmd(nc2, im, core_ids=list(range(n)))
    return [r["xout"] for r in res2.results]





LN127 = 4.844187086458591

def prep_w4(b, wsrc_f32, dst_bf, tag):
    toks = []
    for _ in prep_w4_gen(b, wsrc_f32, dst_bf, tag, toks):
        pass
    return toks


def prep_w4_gen(b, wsrc_f32, dst_bf, tag, out_toks):
    p = b.prog
    H = D // 2
    stg = b.get("pstage_w4", b.scrapH)
    toks = out_toks
    for k in range(CH):
        prev = list(b.wprev.get(("wcast",), ()))
        ld = p.d("sync", b.xchunk[:, 0:D], wsrc_f32[128 * k:128 * (k + 1), :],
                 deps=prev + list(b.wprev.get(("w4x",), ()))
                 + list(b.lnprev.get(0, ()))[-2:], sem="wp_ld")
        r1 = p.c("vector", lambda em: em.reduce_max(
            b.wd_sc[:, 2:3], b.xchunk[:, 0:D], axis=AX.X, apply_absolute_value=True), deps=[ld])
        s1 = p.c("vector", lambda em: em.tensor_scalar(
            b.wd_sc[:, 3:4], b.wd_sc[:, 2:3], 1.0 / 7.0, 1e-8, ALU.mult, ALU.max), deps=[r1])
        s2 = p.c("vector", lambda em: em.reciprocal(b.wd_sc[:, 4:5], b.wd_sc[:, 3:4]), deps=[s1])
        s3 = p.c("vector", lambda em: em.tensor_scalar_mul(b.wd_sc[:, 5:6], b.wd_sc[:, 3:4], -16.0), deps=[s1])
        hts = []
        for h in range(2):
            u = b.lnchunk.bitcast(F32)[:, 0:H]
            I = b.wd_i32[:, 0:H]
            g = b.wd_lo.bitcast(F32)[:, 0:H]
            u1 = p.c("vector", lambda em, h=h: em.tensor_scalar(
                u, b.xchunk[:, H * h:H * (h + 1)], b.wd_sc[:, 4:5], 16.5, ALU.mult, ALU.add),
                deps=[s2] + list(b.wprev.get(("w4q",), ())))
            i1 = p.c("vector", lambda em: em.tensor_copy(I, u), deps=[u1])
            g1 = p.c("vector", lambda em: em.tensor_tensor(g, I, u, ALU.is_gt), deps=[i1])
            f1 = p.c("vector", lambda em: em.tensor_sub(u, I, g), deps=[g1])
            o1 = p.c("vector", lambda em, h=h, stg=stg: em.tensor_scalar(
                stg, u, b.wd_sc[:, 3:4], b.wd_sc[:, 5:6], ALU.mult, ALU.add), deps=[f1])
            st = p.d("gpsimd", dst_bf[128 * k:128 * (k + 1), H * h:H * (h + 1)],
                     stg, deps=[o1], sem="wp_st")
            b.wprev[("w4q",)] = [o1, st]
            hts.append(st)
        b.wprev[("w4x",)] = [u1]
        b.lnprev[0] = list(b.lnprev.get(0, ()))[-2:] + [u1, r1]
        b.lnprev["sq"] = [o1, st]
        toks += hts
        yield


def ln_stats_minmax(b, xsrc):
    p = b.prog
    done = []
    for t in range(TT):
        xc = b.xchunk
        prev = b.lnprev.get(0, ())
        ld = p.d("sync", xc[:], xsrc[128 * t:128 * (t + 1), :], deps=prev, sem="x_ld")
        s1 = p.c("vector", lambda em, t=t, xc=xc: em.reduce_sum(
            b.ssum[:, t:t + 1], xc[:], axis=AX.X), deps=[ld])
        s2 = p.c("scalar", lambda em, t=t, xc=xc: em.activation(
            b.lnchunk[:, 0:D], xc[:], AF.Square, bias=0.0, scale=1.0,
            accum_out=b.ssq[:, t:t + 1]), deps=[ld] + list(b.lnprev.get("sq", ())))
        s3 = p.c("vector", lambda em, t=t, xc=xc: em.reduce_max(
            b.rmax[:, t:t + 1], xc[:], axis=AX.X), deps=[ld])
        s4 = p.c("vector", lambda em, t=t, xc=xc: em.tensor_reduce(
            b.rmin[:, t:t + 1], xc[:], axis=AX.X, op=ALU.min), deps=[ld])
        b.lnprev[0] = [s1, s2, s3, s4]
        b.lnprev["sq"] = [s2]
        done += [s1, s2, s3, s4]
    t0 = p.c("vector", lambda em: em.tensor_scalar_mul(b.mean[:], b.ssum[:, 0:TT], 1.0 / D), deps=done)
    t1 = p.c("vector", lambda em: em.tensor_scalar_mul(b.var[:], b.ssq[:, 0:TT], 1.0 / D), deps=done)
    t2 = p.c("vector", lambda em: em.tensor_tensor(b.msq[:], b.mean[:], b.mean[:], ALU.mult), deps=[t0])
    t3 = p.c("vector", lambda em: em.tensor_sub(b.var[:], b.var[:], b.msq[:]), deps=[t1, t2])
    t3b = p.c("vector", lambda em: em.tensor_scalar_add(b.var[:], b.var[:], 1e-5), deps=[t3])
    t4 = p.c("scalar", lambda em: em.activation(b.std[:], b.var[:], AF.Sqrt, bias=0.0, scale=1.0), deps=[t3b])
    t5 = p.c("vector", lambda em: em.reciprocal(b.rstd[:], b.std[:]), deps=[t4])
    t6 = p.c("vector", lambda em: em.tensor_tensor(b.nmr[:], b.mean[:], b.rstd[:], ALU.mult), deps=[t5])
    t7 = p.c("vector", lambda em: em.tensor_scalar_mul(b.nmr[:], b.nmr[:], -1.0), deps=[t6])
    m1 = p.c("vector", lambda em: em.tensor_sub(b.rmax[:, 0:TT], b.rmax[:, 0:TT], b.mean[:]), deps=[t0] + done)
    m2 = p.c("vector", lambda em: em.tensor_sub(b.rmin[:, 0:TT], b.mean[:], b.rmin[:, 0:TT]), deps=[m1])
    m3 = p.c("vector", lambda em: em.tensor_max(b.rmax[:, 0:TT], b.rmax[:, 0:TT], b.rmin[:, 0:TT]), deps=[m2])
    m4 = p.c("vector", lambda em: em.tensor_mul(b.sx[:, 0:TT], b.rmax[:, 0:TT], b.rstd[:]), deps=[m3, t5])
    m5 = p.c("vector", lambda em: em.tensor_scalar(b.sx[:, 0:TT], b.sx[:, 0:TT], 1.0 / 127.0, 1e-8,
                                                   ALU.mult, ALU.max), deps=[m4])
    m6 = p.c("vector", lambda em: em.reciprocal(b.rsx[:, 0:TT], b.sx[:, 0:TT]), deps=[m5])
    return [t5, t7, m5, m6]


def quantize_x_to_fm(b, xsrc, stat_toks):
    p = b.prog
    H = D // 2
    wr = []
    for t in range(TT):
        for h in range(2):
            xc = b.xchunk
            u = b.lnchunk.bitcast(F32)[:, 0:H]
            I = b.wd_i32[:, 0:H]
            g = b.wd_lo.bitcast(F32)[:, 0:H]
            prev = b.lnprev.get(0, ())
            ld = p.d("sync", xc[:, 0:H], xsrc[128 * t:128 * (t + 1), H * h:H * (h + 1)],
                     deps=prev, sem="x_ld")
            c1 = p.c("vector", lambda em, t=t: em.tensor_mul(
                b.wk1[:, 0:1], b.rstd[:, t:t + 1], b.rsx[:, t:t + 1]), deps=stat_toks + prev)
            c2 = p.c("vector", lambda em, t=t: em.tensor_scalar(
                b.wk1[:, 1:2], b.nmr[:, t:t + 1], b.rsx[:, t:t + 1], 256.5, ALU.mult, ALU.add), deps=[c1])
            u1 = p.c("scalar", lambda em: em.activation(
                u, xc[:, 0:H], AF.Identity, bias=b.wk1[:, 1:2], scale=b.wk1[:, 0:1]), deps=[ld, c2])
            i1 = p.c("vector", lambda em: em.tensor_copy(I, u), deps=[u1])
            g1 = p.c("vector", lambda em: em.tensor_tensor(g, I, u, ALU.is_gt), deps=[i1])
            n1 = p.c("vector", lambda em: em.scalar_tensor_tensor(
                b.scrapH, I, 256.0, g, ALU.subtract, ALU.subtract), deps=[g1])
            stq = p.d("gpsimd", b.ln_dram[128 * t:128 * (t + 1), H * h:H * (h + 1)],
                      b.scrapH, deps=[n1], sem="ln_st")
            b.lnprev[0] = [u1, n1, stq]
            wr.append(stq)
    fm = []
    for c in range(CH):
        fm.append(p.d("scalar", b.fmA[:, TOK * c:TOK * (c + 1)],
                      b.ln_dram[:, 128 * c:128 * (c + 1)], deps=wr, transpose=True, sem="fm_tr"))
    return fm


def _r3v(ap):
    return ap.rearrange("p (a c) -> p a c", c=128)


def widen_rope_tables(b, rt):
    p = b.prog
    cosb, sinb = carve(b, "cosb"), carve(b, "sinb")
    cos4, sin4 = carve(b, "cosb4"), carve(b, "sinb4")
    prev = list(rt)
    toks = []
    for t in range(TT):
        for (src, dst) in ((cosb, cos4), (sinb, sin4)):
            for r in range(4):
                tok = p.c("vector", lambda em, src=src, dst=dst, t=t, r=r: em.tensor_copy(
                    dst[:, 512 * t + 128 * r:512 * t + 128 * (r + 1)],
                    src[:, 128 * t:128 * (t + 1)]), deps=prev)
                prev = [tok]
                toks.append(tok)
    return toks[-1:]


def ep_rope_quant(b, dst_dram, fold_sk):
    p = b.prog
    cos4, sin4 = carve(b, "cosb4"), carve(b, "sinb4")
    rw = carve(b, "ropeW")
    qstg = carve(b, "qstg")
    if "rope_state" not in b:
        b.rope_state = {}
    state = b.rope_state

    def ep(t, og, ps, mm_tok):
        rt, tB = rw[:, 0:512], rw[:, 512:1024]
        I = b.wd_i32[:, 0:512]
        g = b.wd_lo.bitcast(F32)[:, 0:512]
        cos_t = cos4[:, 512 * t:512 * (t + 1)]
        sin_t = sin4[:, 512 * t:512 * (t + 1)]
        prev = list(state.get("dma", ())) + list(state.get("last", ()))
        sxd = list(b.get("sx_toks", ())) + b.bias256_tok
        c1 = p.c("vector", lambda em, ps=ps, rt=rt, cos_t=cos_t, t=t: em.scalar_tensor_tensor(
            rt, ps, b.sx[:, t:t + 1], cos_t, ALU.mult, ALU.mult),
            deps=[mm_tok] + prev + sxd)
        chalf = [c1]
        for hh in range(4):
            o = 128 * hh
            cA = p.c("vector", lambda em, ps=ps, tB=tB, sin_t=sin_t, t=t, o=o: em.scalar_tensor_tensor(
                tB[:, o:o + 64], ps[:, o + 64:o + 128], b.nsx[:, t:t + 1],
                sin_t[:, o:o + 64], ALU.mult, ALU.mult),
                deps=[mm_tok] + prev + chalf[-1:])
            cB = p.c("vector", lambda em, ps=ps, tB=tB, sin_t=sin_t, t=t, o=o: em.scalar_tensor_tensor(
                tB[:, o + 64:o + 128], ps[:, o:o + 64], b.sx[:, t:t + 1],
                sin_t[:, o + 64:o + 128], ALU.mult, ALU.mult), deps=[mm_tok, cA])
            chalf.append(cB)
        c4 = p.c("vector", lambda em, rt=rt, tB=tB: em.tensor_add(rt, rt, tB), deps=chalf[1:])
        r1 = p.c("vector", lambda em, rt=rt: em.reduce_max(
            b.wk1[:, 4:8], _r3v(rt), axis=AX.X, apply_absolute_value=True), deps=[c4])
        r2 = p.c("vector", lambda em: em.tensor_scalar(
            b.wk1[:, 8:12], b.wk1[:, 4:8], 1.0 / 127.0, 1e-8, ALU.mult, ALU.max), deps=[r1])
        r3o = p.c("vector", lambda em: em.reciprocal(b.wk1[:, 4:8], b.wk1[:, 8:12]), deps=[r2])
        u_toks = []
        lastu = [r3o]
        for hh in range(4):
            u1 = p.c("scalar", lambda em, rt=rt, tB=tB, hh=hh: em.activation(
                tB[:, 128 * hh:128 * (hh + 1)], rt[:, 128 * hh:128 * (hh + 1)],
                AF.Identity, bias=b.cm40[:, 0:1], scale=b.wk1[:, 4 + hh:5 + hh]), deps=lastu)
            lastu = [u1]
            u_toks.append(u1)
        i1 = p.c("scalar", lambda em, I=I, tB=tB: em.activation(
            I, tB, AF.Identity, bias=0.0, scale=1.0), deps=u_toks)
        g1 = p.c("vector", lambda em, I=I, tB=tB, g=g: em.tensor_tensor(g, I, tB, ALU.is_gt),
                 deps=[i1])
        if fold_sk:
            n1 = p.c("vector", lambda em, I=I, g=g, tB=tB: em.scalar_tensor_tensor(
                tB, I, 256.0, g, ALU.subtract, ALU.subtract), deps=[g1])
            toks = []
            for hh in range(4):
                toks.append(p.c("vector", lambda em, hh=hh, tB=tB: em.tensor_scalar_mul(
                    qstg[:, 128 * hh:128 * (hh + 1)], tB[:, 128 * hh:128 * (hh + 1)],
                    b.wk1[:, 8 + hh:9 + hh]), deps=[n1]))
        else:
            toks = [p.c("vector", lambda em, I=I, g=g: em.scalar_tensor_tensor(
                qstg[:, 0:512], I, 256.0, g, ALU.subtract, ALU.subtract), deps=[g1])]
        dm = p.d("gpsimd", dst_dram[128 * t:128 * (t + 1), 512 * og:512 * (og + 1)],
                 qstg[:, 0:512], deps=toks, sem="rope_st")
        state["dma"] = [dm]
        state["last"] = toks + u_toks
        return [toks[0], dm]
    return ep


def ep_vquant(b, dst_dram):
    p = b.prog
    rw = carve(b, "ropeW")
    qstg = carve(b, "qstg")
    state = b.rope_state

    def ep(t, og, ps, mm_tok):
        rt, tB = rw[:, 0:512], rw[:, 512:1024]
        I = b.wd_i32[:, 0:512]
        g = b.wd_lo.bitcast(F32)[:, 0:512]
        prev = list(state.get("dma", ())) + list(state.get("last", ()))
        sxd = list(b.get("sx_toks", ())) + b.bias256_tok
        c1 = p.c("vector", lambda em, ps=ps, rt=rt, t=t: em.tensor_scalar_mul(
            rt, ps, b.sx[:, t:t + 1]), deps=[mm_tok] + prev + sxd)
        r1 = p.c("vector", lambda em, rt=rt: em.reduce_max(
            b.wk1[:, 4:8], _r3v(rt), axis=AX.X, apply_absolute_value=True), deps=[c1])
        r2 = p.c("vector", lambda em: em.tensor_scalar(
            b.wk1[:, 8:12], b.wk1[:, 4:8], 1.0 / 127.0, 1e-8, ALU.mult, ALU.max), deps=[r1])
        r3o = p.c("vector", lambda em: em.reciprocal(b.wk1[:, 4:8], b.wk1[:, 8:12]), deps=[r2])
        u_toks = []
        lastu = [r3o]
        for hh in range(4):
            u1 = p.c("scalar", lambda em, rt=rt, tB=tB, hh=hh: em.activation(
                tB[:, 128 * hh:128 * (hh + 1)], rt[:, 128 * hh:128 * (hh + 1)],
                AF.Identity, bias=b.cm40[:, 0:1], scale=b.wk1[:, 4 + hh:5 + hh]), deps=lastu)
            lastu = [u1]
            u_toks.append(u1)
        i1 = p.c("scalar", lambda em, I=I, tB=tB: em.activation(
            I, tB, AF.Identity, bias=0.0, scale=1.0), deps=u_toks)
        g1 = p.c("vector", lambda em, I=I, tB=tB, g=g: em.tensor_tensor(g, I, tB, ALU.is_gt),
                 deps=[i1])
        n1 = p.c("vector", lambda em, I=I, g=g, tB=tB: em.scalar_tensor_tensor(
            tB, I, 256.0, g, ALU.subtract, ALU.subtract), deps=[g1])
        toks = []
        for hh in range(4):
            toks.append(p.c("vector", lambda em, hh=hh, tB=tB: em.tensor_scalar_mul(
                qstg[:, 128 * hh:128 * (hh + 1)], tB[:, 128 * hh:128 * (hh + 1)],
                b.wk1[:, 8 + hh:9 + hh]), deps=[n1]))
        dm = p.d("gpsimd", dst_dram[128 * t:128 * (t + 1), 512 * og:512 * (og + 1)],
                 qstg[:, 0:512], deps=toks, sem="rope_st")
        state["dma"] = [dm]
        state["last"] = toks + u_toks
        return [toks[0], dm]
    return ep


def attn1_launchE(b, xsrc, w4toks=None):
    p = b.prog
    if w4toks is None:
        wq4 = prep_w4(b, b.wq, b.wq4_bf, "wq4")
        wk4 = prep_w4(b, b.wk, b.wk4_bf, "wk4")
        wv4 = prep_w4(b, b.wv, b.wv4_bf, "wv4")
    else:
        wq4, wk4, wv4 = w4toks
    st = ln_stats_minmax(b, xsrc)
    ns = p.c("vector", lambda em: em.tensor_scalar_mul(b.nsx[:, 0:TT], b.sx[:, 0:TT], -1.0), deps=st)
    b.sx_toks = st + [ns]
    fm = quantize_x_to_fm(b, xsrc, st)
    rt = build_rope_tables(b, ())
    wide = widen_rope_tables(b, rt)
    b256 = p.c("vector", lambda em: em.memset(b.cm40[:, 0:1], 256.5), deps=())
    b.bias256_tok = [b256]
    p.wait("vector", wide)

    def mv(wbf, wtk):
        def f(og, c, buf, prev):
            i = (og * CH + c) % 2
            return p.d("sync", buf, wbf[512 * og:512 * (og + 1), 128 * c:128 * (c + 1)],
                       deps=prev + wtk, transpose=True, sem=f"mv{i}")
        return f

    t2_matmul(b, b.fmA, fm + [ns], mv(b.wq4_bf, wq4), ep_rope_quant(b, b.q_dram, True))
    t2_matmul(b, b.fmA, fm, mv(b.wk4_bf, wk4), ep_rope_quant(b, b.k_dram, True))
    t2_matmul(b, b.fmA, fm, mv(b.wv4_bf, wv4), ep_vquant(b, b.v_dram))
    return


def build_band2(b, deps):
    p = b.prog
    W2 = 2944
    band = carve(b, "btmp0")[:, 0:W2]
    tmp = carve(b, "btmp1")[:, 0:W2]
    onesw = carve(b, "bones")[:, 0:W2]
    aband = carve(b, "aband2")
    g0 = broadcast_scalar(b, b.g0in, b.gvec[:, 0:1], 6, deps)
    g1 = broadcast_scalar(b, b.g1in, b.gvec[:, 1:2], 6, deps)
    on = p.c("vector", lambda em: em.memset(onesw, 1.0), deps=())
    a0 = p.c("gpsimd", lambda em: em.affine_select(band, onesw, pattern=[[-1, W2]],
             compare_op=ALU.is_ge, fill=0.0, channel_multiplier=1, base=896), deps=[on])
    m0 = p.c("vector", lambda em: em.tensor_scalar_mul(band, band, b.gvec[:, 0:1]), deps=[a0] + g0)
    a1 = p.c("gpsimd", lambda em: em.affine_select(tmp, onesw, pattern=[[-1, W2]],
             compare_op=ALU.is_ge, fill=0.0, channel_multiplier=1, base=1920), deps=[on, m0])
    m1 = p.c("vector", lambda em: em.tensor_scalar_mul(tmp, tmp, b.gvec[:, 1:2]), deps=[a1] + g1)
    m2 = p.c("vector", lambda em: em.tensor_add(band, band, tmp), deps=[m1])
    m3 = p.c("vector", lambda em: em.tensor_scalar(aband, band, 1e30, -1e30, ALU.mult, ALU.add), deps=[m2])
    return [m3]


def band2_slice(b, t1b, t2c):
    uu0 = 896 - 128 * t1b + 512 * t2c
    return carve(b, "aband2")[:, uu0:uu0 + 512]


def attn1_launchF(b, xsrc, xdst, wo4toks=None):
    p = b.prog
    if wo4toks is None:
        wo4 = prep_w4(b, b.wo, b.wo4_bf, "wo4")
        b.lnprev["np"] = wo4[-2:]
    else:
        wo4 = wo4toks
    b05 = p.c("vector", lambda em: em.memset(b.cm40[:, 0:1], 0.5), deps=())
    b.bias05_tok = [b05]
    bd = build_band2(b, ())
    bufQ = [carve(b, "qstgf0"), carve(b, "qstgf1")]
    bufK = [carve(b, "kstg"), carve(b, "kstgf2")]
    prevq = [list(bd), list(bd)]
    prevk = [list(bd), list(bd)]
    qfm_wr, kfm_wr = [], []
    for c in range(CH):
        i = c % 2
        t0 = p.d("scalar", bufQ[i], b.q_dram[:, 128 * c:128 * (c + 1)],
                 deps=prevq[i], transpose=True, sem=f"qstg{i}")
        w0 = p.d("gpsimd", b.qfm_dram[128 * c:128 * (c + 1), :], bufQ[i],
                 deps=[t0], sem=f"qstg_wr{i}")
        prevq[i] = [w0]
        t1 = p.d("scalar", bufK[i][:, 0:1024], b.kt0_dram[:, 128 * c:128 * (c + 1)],
                 deps=prevk[i], transpose=True, sem=f"kstg{i}")
        t2 = p.d("scalar", bufK[i][:, 1024:2048], b.kt1_dram[:, 128 * c:128 * (c + 1)],
                 deps=prevk[i], transpose=True, sem=f"kstg{i}")
        w1 = p.d("gpsimd", b.kfm_dram[128 * c:128 * (c + 1), :], bufK[i][:, 0:2048],
                 deps=[t1, t2], sem=f"kstg_wr{i}")
        prevk[i] = [w1]
        qfm_wr.append(w0)
        kfm_wr.append(w1)
    qh = carve(b, "qh")
    kh = carve(b, "kh2")
    vh = carve(b, "vh2")
    pT = carve(b, "pT")
    e127 = b.lnchunk.bitcast(F32)[:, 0:D // 2]
    npstg = b.scrapH
    kh_guard, vh_guard, qh_guard = [], [], []
    pt_guard = [list(bd)]
    ao_state = {}
    for h in range(NH):
        kl = p.d("sync", kh[:, 0:2048], b.kfm_dram[128 * h:128 * (h + 1), :],
                 deps=kh_guard + kfm_wr, sem="khl")
        ql = p.d("sync", qh[:, 0:1024], b.qfm_dram[128 * h:128 * (h + 1), :],
                 deps=qh_guard + qfm_wr, sem="qhl")
        v0 = p.d("sync", vh.rearrange("p (a c) -> p a c", c=128)[:, 0:8, :],
                 b.vt0_dram.rearrange("(a p) c -> p a c", p=128)[:, :, 128 * h:128 * (h + 1)],
                 deps=vh_guard, sem="vhl")
        v1 = p.d("sync", vh.rearrange("p (a c) -> p a c", c=128)[:, 8:16, :],
                 b.vt1_dram.rearrange("(a p) c -> p a c", p=128)[:, :, 128 * h:128 * (h + 1)],
                 deps=vh_guard, sem="vhl")
        kh_guard, vh_guard, qh_guard = [], [], []
        np_wr = []
        for t1b in range(TT):
            mm_toks = []
            for t2c in range(4):
                bank = 4 + t2c
                mm = p.c("tensor", lambda em, t1b=t1b, t2c=t2c, bank=bank: em.matmul(
                    b.psum[bank][:, 0:512], qh[:, 128 * t1b:128 * (t1b + 1)],
                    kh[:, 512 * t2c:512 * (t2c + 1)], start=True, stop=True),
                    deps=[kl, ql] + b.bank_guard[bank])
                md = p.c("vector", lambda em, bank=bank, bs=band2_slice(b, t1b, t2c): em.tensor_add(
                    b.psum[bank][:, 0:512], b.psum[bank][:, 0:512], bs),
                    deps=[mm] + bd)
                mm_toks.append(md)
            r = []
            for t2c in range(4):
                r.append(p.c("vector", lambda em, t2c=t2c: em.reduce_max(
                    b.wk1[:, 4 + t2c:5 + t2c], b.psum[4 + t2c][:, 0:512], axis=AX.X),
                    deps=[mm_toks[t2c]] + list(b.lnprev.get("wk1h", ()))))
            x1 = p.c("vector", lambda em: em.tensor_max(b.wk1[:, 4:5], b.wk1[:, 4:5], b.wk1[:, 5:6]), deps=r)
            x2 = p.c("vector", lambda em: em.tensor_max(b.wk1[:, 6:7], b.wk1[:, 6:7], b.wk1[:, 7:8]), deps=r)
            x3 = p.c("vector", lambda em: em.tensor_max(b.wk1[:, 4:5], b.wk1[:, 4:5], b.wk1[:, 6:7]), deps=[x1, x2])
            bch = p.c("vector", lambda em: em.tensor_scalar(
                b.wk1[:, 5:6], b.wk1[:, 4:5], -INV_SQRT_DH, LN127, ALU.mult, ALU.add), deps=[x3])
            lac = []
            for t2c in range(4):
                exq = p.c("scalar", lambda em, t2c=t2c: em.activation(
                    e127[:, 512 * t2c:512 * (t2c + 1)], b.psum[4 + t2c][:, 0:512], AF.Exp,
                    bias=b.wk1[:, 5:6], scale=INV_SQRT_DH, accum_out=b.wk1[:, 8 + t2c:9 + t2c]),
                    deps=[bch] + list(b.lnprev.get("e", ())))
                b.bank_guard[4 + t2c] = [exq]
                lac.append(exq)
            l1 = p.c("vector", lambda em: em.tensor_add(b.wk1[:, 8:9], b.wk1[:, 8:9], b.wk1[:, 9:10]), deps=lac)
            l2 = p.c("vector", lambda em: em.tensor_add(b.wk1[:, 10:11], b.wk1[:, 10:11], b.wk1[:, 11:12]), deps=lac)
            l3 = p.c("vector", lambda em: em.tensor_add(b.wk1[:, 8:9], b.wk1[:, 8:9], b.wk1[:, 10:11]), deps=[l1, l2])
            lr0 = p.c("scalar", lambda em: em.activation(b.wk1[:, 3:4], b.wk1[:, 8:9], AF.Ln,
                                                         bias=0.0, scale=1.0), deps=[l3])
            lr1 = p.c("scalar", lambda em, t1b=t1b: em.activation(
                b.rcptab[:, t1b:t1b + 1], b.wk1[:, 3:4], AF.Exp, bias=0.0, scale=-1.0),
                deps=[lr0] + list(b.lnprev.get("rc", ())) + list(b.lnprev.get(("rct", t1b), ())))
            b.lnprev["rc"] = [lr1]
            b.lnprev["wk1h"] = lac + [lr0, lr1, l3]
            rt_ = []
            gprev = []
            for hlf in range(2):
                u = e127[:, 1024 * hlf:1024 * (hlf + 1)]
                w = b.wchunk_bf.bitcast(F32)[:, 0:1024]
                I = b.wd_i32[:, 0:1024]
                g = b.wd_lo.bitcast(F32)[:, 0:1024]
                u1 = p.c("scalar", lambda em, u=u, w=w: em.activation(
                    w, u, AF.Identity, bias=b.cm40[:, 0:1], scale=1.0),
                    deps=lac + list(b.lnprev.get("np", ())) + b.bias05_tok + gprev)
                i1 = p.c("scalar", lambda em, I=I, w=w: em.activation(
                    I, w, AF.Identity, bias=0.0, scale=1.0), deps=[u1])
                g1 = p.c("vector", lambda em, I=I, w=w, g=g: em.tensor_tensor(g, I, w, ALU.is_gt), deps=[i1])
                n1 = p.c("vector", lambda em, I=I, g=g, hlf=hlf: em.tensor_sub(
                    npstg[:, 1024 * hlf:1024 * (hlf + 1)], I, g), deps=[g1])
                gprev = [g1, n1]
                rt_.append(n1)
            b.lnprev["e"] = rt_
            dm = p.d("gpsimd", b.np_dram[128 * t1b:128 * (t1b + 1), :], npstg[:, 0:2048],
                     deps=rt_ + list(b.lnprev.get("npd", ())), sem="np_st")
            b.lnprev["npd"] = [dm]
            b.lnprev["np"] = rt_ + [dm]
            np_wr.append(dm)
        ptr = []
        for t2b in range(NKB):
            ptr.append(p.d("scalar", pT[:, 1024 * t2b:1024 * (t2b + 1)],
                           b.np_dram[:, 128 * t2b:128 * (t2b + 1)], deps=np_wr + pt_guard[0],
                           transpose=True, sem="np_tr"))
        av_mms = []
        for pas in range(2):
            for t1bq in range(4):
                t1b = 4 * pas + t1bq
                bank = t1bq
                mms = []
                for t2b in range(NKB):
                    deps = ptr + [v0, v1]
                    if t2b == 0:
                        deps = deps + b.bank_guard[bank]
                    mm = p.c("tensor", lambda em, t1b=t1b, t2b=t2b, bank=bank: em.matmul(
                        b.psum[bank][:, 0:128],
                        pT[:, 1024 * t2b + 128 * t1b: 1024 * t2b + 128 * (t1b + 1)],
                        vh[:, 128 * t2b:128 * (t2b + 1)],
                        start=(t2b == 0), stop=(t2b == NKB - 1)), deps=deps)
                    mms.append(mm)
                av_mms.append(mms[-1])
                i = len(av_mms) % 2
                prev = ao_state.get(i, ())
                sc = p.c("vector", lambda em, bank=bank, t1b=t1b, i=i: em.tensor_scalar_mul(
                    b.aostgF[i][:, 0:128], b.psum[bank][:, 0:128], b.rcptab[:, t1b:t1b + 1]),
                    deps=[mms[-1]] + list(prev))
                b.lnprev[("rct", t1b)] = [sc]
                dm = p.d("gpsimd", b.ao_dram_f[128 * t1b:128 * (t1b + 1), 128 * h:128 * (h + 1)],
                         b.aostgF[i][:, 0:128], deps=[sc], sem=f"ao_st{i}")
                b.bank_guard[bank] = [sc]
                ao_state[i] = [dm]
                kh_guard.append(mms[-1])
                vh_guard.append(mms[-1])
                qh_guard.append(mms[-1])
        pt_guard[0] = av_mms
    ao_done = p.barrier_token()
    st2 = []
    H = D // 2
    for t in range(TT):
        xc = b.xchunk
        prev = b.lnprev.get(1, ())
        ld = p.d("sync", xc[:, 0:D], b.ao_dram_f[128 * t:128 * (t + 1), :],
                 deps=list(prev) + (ao_done if t == 0 else []), sem="x_ld")
        r1 = p.c("vector", lambda em, xc=xc: em.reduce_max(
            b.wk1[:, 0:1], xc[:, 0:D], axis=AX.X, apply_absolute_value=True), deps=[ld])
        r2 = p.c("vector", lambda em, t=t: em.tensor_scalar(
            b.sa[:, t:t + 1], b.wk1[:, 0:1], 1.0 / 127.0, 1e-8, ALU.mult, ALU.max), deps=[r1])
        r3 = p.c("vector", lambda em, t=t: em.reciprocal(b.wk1[:, 1:2], b.sa[:, t:t + 1]), deps=[r2])
        for hlf in range(2):
            u = b.lnchunk.bitcast(F32)[:, 0:H]
            I = b.wd_i32[:, 0:H]
            g = b.wd_lo.bitcast(F32)[:, 0:H]
            u1 = p.c("vector", lambda em, hlf=hlf, xc=xc, u=u: em.tensor_scalar(
                u, xc[:, H * hlf:H * (hlf + 1)], b.wk1[:, 1:2], 256.5, ALU.mult, ALU.add),
                deps=[r3] + list(b.lnprev.get(1, ())))
            i1 = p.c("vector", lambda em, I=I, u=u: em.tensor_copy(I, u), deps=[u1])
            g1 = p.c("vector", lambda em, I=I, u=u, g=g: em.tensor_tensor(g, I, u, ALU.is_gt), deps=[i1])
            n1 = p.c("vector", lambda em, I=I, g=g: em.scalar_tensor_tensor(
                b.scrapH, I, 256.0, g, ALU.subtract, ALU.subtract), deps=[g1])
            stq = p.d("gpsimd", b.ln_dram[128 * t:128 * (t + 1), H * hlf:H * (hlf + 1)],
                      b.scrapH, deps=[n1] + list(b.lnprev.get("st2", ())), sem="ln_st")
            b.lnprev[1] = [u1, n1, stq, r1]
            b.lnprev["st2"] = [stq]
            st2.append(stq)
    nafm = []
    for c in range(CH):
        nafm.append(p.d("scalar", b.fmA[:, TOK * c:TOK * (c + 1)],
                        b.ln_dram[:, 128 * c:128 * (c + 1)], deps=st2, transpose=True, sem="fm_tr"))

    def mv_wo(og, c, buf, prev):
        i = (og * CH + c) % 2
        return p.d("sync", buf, b.wo4_bf[512 * og:512 * (og + 1), 128 * c:128 * (c + 1)],
                   deps=prev + wo4, transpose=True, sem=f"mv{i}")

    ep_state = {}

    def ep(t, og, ps, mm_tok):
        i = t % 2
        prev = ep_state.get(i, ())
        ld = p.d("gpsimd", b.epi_x[i][:, 0:512],
                 xsrc[128 * t:128 * (t + 1), 512 * og:512 * (og + 1)], deps=prev, sem=f"epix{i}")
        a1 = p.c("vector", lambda em, i=i, ps=ps, t=t: em.scalar_tensor_tensor(
            b.epi_y[:, 0:512], ps, b.sa[:, t:t + 1], b.epi_x[i][:, 0:512], ALU.mult, ALU.add),
            deps=[ld, mm_tok] + list(ep_state.get("y", ())))
        stq = p.d("gpsimd", xdst[128 * t:128 * (t + 1), 512 * og:512 * (og + 1)],
                  b.epi_y[:, 0:512], deps=[a1], sem="epi_st")
        ep_state[i] = [ld, a1, stq]
        ep_state["y"] = [stq]
        return [a1, stq]

    t2_matmul(b, b.fmA, nafm, mv_wo, ep)
    return


def alloc_attn1_buffers(b, nc, es):
    sb = lambda name, shape, dt: es.enter_context(nc.sbuf_tensor(name, shape, dt)).ap()
    b.sx = sb("sx", [128, TT], F32)
    b.nsx = sb("nsx", [128, TT], F32)
    b.rsx = sb("rsx", [128, TT], F32)
    b.rmax = sb("rmax", [128, TT], F32)
    b.rmin = sb("rmin", [128, TT], F32)
    b.wk1 = sb("wk1", [128, 12], F32)
    b.wk128i = sb("wk128i", [128, 128], I32)
    b.wk128f = sb("wk128f", [128, 128], F32)
    b.rcptab = sb("rcptab", [128, TT], F32)
    b.sa = sb("sa", [128, TT], F32)
    b.aostgF = [sb(f"aostgF{i}", [128, 128], F32) for i in range(2)]
    b.scrapH = b.tmB[:, 0:2048]


def setup_carves_f(b):
    b.carves = {
        "pT":    (8192, 24576, "bf"),
        "qh":    (24576, 25600, "bf"),
        "kh2":   (25600, 27648, "bf"),
        "vh2":   (27648, 29696, "bf"),
        "aband2": (29696, 32640, "bf"),
        "btmp0": (2048, 5504, "bf"),
        "btmp1": (5504, 8960, "bf"),
        "bones": (24576, 27520, "bf"),
        "kstg":  (2048, 4096, "bf"),
        "qstgf0": (0, 1024, "bf"),
        "qstgf1": (1024, 2048, "bf"),
        "kstgf2": (5504, 7552, "bf"),
    }


def build_launchE():
    nc = bass.Bass("TRN2", target_bir_lowering=False, debug=False)
    es = ExitStack()
    b = new_builder(nc)
    b.xin = nc.dram_tensor("xin", [TOK, D], F32, kind="ExternalInput").ap()
    b.wq = nc.dram_tensor("wq", [D, D], F32, kind="ExternalInput").ap()
    b.wk = nc.dram_tensor("wk", [D, D], F32, kind="ExternalInput").ap()
    b.wv = nc.dram_tensor("wv", [D, D], F32, kind="ExternalInput").ap()
    b.poff = nc.dram_tensor("poff", [1, 1], F32, kind="ExternalInput").ap()
    b.q_dram = nc.dram_tensor("q_out", [TOK, D], BF16, kind="ExternalOutput").ap()
    b.k_dram = nc.dram_tensor("k_out", [TOK, D], BF16, kind="ExternalOutput").ap()
    b.v_dram = nc.dram_tensor("v_out", [TOK, D], BF16, kind="ExternalOutput").ap()
    b.wq4_bf = nc.dram_tensor("wq4_bf", [D, D], BF16, kind="Internal").ap()
    b.wk4_bf = nc.dram_tensor("wk4_bf", [D, D], BF16, kind="Internal").ap()
    b.wv4_bf = nc.dram_tensor("wv4_bf", [D, D], BF16, kind="Internal").ap()
    b.ln_dram = nc.dram_tensor("ln_dram", [TOK, D], BF16, kind="Internal").ap()
    alloc_buffers(b, nc, es)
    alloc_attn_buffers(b, nc, es)
    alloc_attn1_buffers(b, nc, es)
    setup_carves(b)
    attn1_launchE(b, b.xin)
    b.prog.wait("gpsimd", b.prog.barrier_token())
    build_block(nc, b.prog, es)
    es.close()
    return nc


def build_launchF():
    nc = bass.Bass("TRN2", target_bir_lowering=False, debug=False)
    es = ExitStack()
    b = new_builder(nc)
    b.xin = nc.dram_tensor("xin", [TOK, D], F32, kind="ExternalInput").ap()
    b.wo = nc.dram_tensor("wo", [D, D], F32, kind="ExternalInput").ap()
    b.q_dram = nc.dram_tensor("q_in", [TOK, D], BF16, kind="ExternalInput").ap()
    b.kt0_dram = nc.dram_tensor("kt0", [TOK, D], BF16, kind="ExternalInput").ap()
    b.kt1_dram = nc.dram_tensor("kt1", [TOK, D], BF16, kind="ExternalInput").ap()
    b.vt0_dram = nc.dram_tensor("vt0", [TOK, D], BF16, kind="ExternalInput").ap()
    b.vt1_dram = nc.dram_tensor("vt1", [TOK, D], BF16, kind="ExternalInput").ap()
    b.g0in = nc.dram_tensor("g0in", [1, 1], F32, kind="ExternalInput").ap()
    b.g1in = nc.dram_tensor("g1in", [1, 1], F32, kind="ExternalInput").ap()
    b.xout = nc.dram_tensor("xout", [TOK, D], F32, kind="ExternalOutput").ap()
    b.wo4_bf = nc.dram_tensor("wo4_bf", [D, D], BF16, kind="Internal").ap()
    b.qfm_dram = nc.dram_tensor("qfm_dram", [D, TOK], BF16, kind="Internal").ap()
    b.kfm_dram = nc.dram_tensor("kfm_dram", [D, TK], BF16, kind="Internal").ap()
    b.np_dram = nc.dram_tensor("np_dram", [TOK, TK], BF16, kind="Internal").ap()
    b.ao_dram_f = nc.dram_tensor("ao_dram_f", [TOK, D], F32, kind="Internal").ap()
    b.ln_dram = nc.dram_tensor("ln_dram", [TOK, D], BF16, kind="Internal").ap()
    alloc_buffers(b, nc, es)
    alloc_attn_buffers(b, nc, es)
    alloc_attn1_buffers(b, nc, es)
    setup_carves_f(b)
    attn1_launchF(b, b.xin, b.xout)
    b.prog.wait("gpsimd", b.prog.barrier_token())
    build_block(nc, b.prog, es)
    es.close()
    return nc


def run_attn1(x_slices, wq, wk, wv, wo):
    n = len(x_slices)
    ncE = build_launchE()
    imE = []
    for i in range(n):
        poff = np.array([[1024.0 * (i % 2)]], dtype=np.float32)
        imE.append({"xin": x_slices[i], "wq": wq, "wk": wk, "wv": wv, "poff": poff})
    resE = run_bass_kernel_spmd(ncE, imE, core_ids=list(range(n)))
    ncF = build_launchF()
    imF = []
    for i in range(n):
        pair = i // 2
        e, o = 2 * pair, 2 * pair + 1
        g0 = np.array([[1.0 if i % 2 == 0 else 0.0]], dtype=np.float32)
        g1 = np.array([[0.0 if i % 2 == 0 else 1.0]], dtype=np.float32)
        imF.append({
            "xin": x_slices[i], "wo": wo,
            "q_in": resE.results[i]["q_out"],
            "kt0": resE.results[e]["k_out"], "kt1": resE.results[o]["k_out"],
            "vt0": resE.results[e]["v_out"], "vt1": resE.results[o]["v_out"],
            "g0in": g0, "g1in": g1})
    resF = run_bass_kernel_spmd(ncF, imF, core_ids=list(range(n)))
    return [r["xout"] for r in resF.results]



def build_fused(n_cores=NCORE):
    nc = bass.Bass("TRN2", target_bir_lowering=False, debug=False)
    es = ExitStack()
    b = new_builder(nc)
    p = b.prog
    dram = lambda name, shape, dt, kind="Internal": nc.dram_tensor(name, shape, dt, kind=kind).ap()

    b.xin = dram("xin", [TOK, D], F32, "ExternalInput")
    b.had = dram("had", [D, D], F32, "ExternalInput")
    b.wdata = dram("wdata", [D, 2048], I32, "ExternalInput")
    b.scales = dram("scales", [D, 32], F32, "ExternalInput")
    b.ffn0_bias = dram("fbias", [D], F32, "ExternalInput")
    wq0 = dram("wq0", [D, D], F32, "ExternalInput")
    wk0 = dram("wk0", [D, D], F32, "ExternalInput")
    wv0 = dram("wv0", [D, D], F32, "ExternalInput")
    wo0 = dram("wo0", [D, D], F32, "ExternalInput")
    wq1 = dram("wq1_in", [D, D], F32, "ExternalInput")
    wk1 = dram("wk1_in", [D, D], F32, "ExternalInput")
    wv1 = dram("wv1_in", [D, D], F32, "ExternalInput")
    wo1 = dram("wo1_in", [D, D], F32, "ExternalInput")
    b.wffn = dram("wffn", [D, D], F32, "ExternalInput")
    b.ffn1_bias = dram("fbias1", [D], F32, "ExternalInput")
    b.poff = dram("poff", [1, 1], F32, "ExternalInput")
    b.g0in = dram("g0in", [1, 1], F32, "ExternalInput")
    b.g1in = dram("g1in", [1, 1], F32, "ExternalInput")
    xout = dram("xout", [TOK, D], F32, "ExternalOutput")

    b.had_bf = dram("had_bf", [D, D], BF16)
    b.wdq_bf = dram("wdq_bf", [D, D], BF16)
    b.wq_bf = dram("wq_bf", [D, D], BF16)
    b.wk_bf = dram("wk_bf", [D, D], BF16)
    b.wv_bf = dram("wv_bf", [D, D], BF16)
    b.wo_bf = dram("wo_bf", [D, D], BF16)
    b.wq4_bf = dram("wq4_bf", [D, D], BF16)
    b.wk4_bf = dram("wk4_bf", [D, D], BF16)
    b.wv4_bf = dram("wv4_bf", [D, D], BF16)
    b.wo4_bf = dram("wo4_bf", [D, D], BF16)
    b.w8_bf = dram("w8_bf", [D, D], BF16)
    b.ln_dram = dram("ln_dram", [TOK, D], BF16)
    b.xr_dram = dram("xr_dram", [TOK, D], BF16)
    b.q_dram = dram("q01", [TOK, D], BF16)
    b.k_dram = dram("k01", [TOK, D], BF16)
    b.vaug_dram = dram("v0aug", [TOK, 4128], BF16)
    b.v_dram = dram("v1", [TOK, D], BF16)
    kg = dram("kg", [TK, D], BF16)
    vg0 = dram("vg0", [TK, 4128], BF16)
    vg1 = dram("vg1", [TK, D], BF16)
    b.kfm_dram = dram("kfm_dram", [D, TK], BF16)
    b.qfm_dram = dram("qfm_dram", [D, TOK], BF16)
    b.np_dram = dram("np_dram", [TOK, TK], BF16)
    b.ao_dram = dram("ao_dram", [TOK, D], BF16)
    b.ao_dram_f = dram("ao_dram_f", [TOK, D], F32)
    b.amax_loc = dram("amax_loc", [1, 1], F32)
    b.amax_red = dram("amax_red", [1, 1], F32)
    xa = dram("xa", [TOK, D], F32)
    xb = dram("xb", [TOK, D], F32)

    alloc_buffers(b, nc, es)
    alloc_attn_buffers(b, nc, es)
    alloc_l1_buffers(b, nc, es)
    alloc_attn1_buffers(b, nc, es)
    for flag in ("HEAD_BARRIER", "STAGE_BARRIER", "AV_ALLGATES", "SKIP_WO",
                 "PT_FULLGUARD", "KH_SERIAL", "GATE_POOL", "SKIP_PREP"):
        b[flag] = False

    RGp = [[2 * i, 2 * i + 1] for i in range(n_cores // 2)]
    RGall = [list(range(n_cores))]

    def allgather(src, dst, deps):
        return p.c("gpsimd", lambda em: em.collective_compute(
            "AllGather", ALU.bypass, replica_groups=RGp,
            ins=[src.opt()], outs=[dst.opt()]), deps=deps)

    def amax_allreduce(deps):
        return p.c("gpsimd", lambda em: em.collective_compute(
            "AllReduce", ALU.max, replica_groups=RGall,
            ins=[b.amax_loc.opt()], outs=[b.amax_red.opt()]), deps=deps)

    wtA = prep_cast_bf16(b, b.had, b.had_bf, "had") + prep_wdq(b)
    ffn0_sublayer(b, b.xin, xa, wtA)
    wtB = (prep_cast_bf16(b, wq0, b.wq_bf, "wq") + prep_cast_bf16(b, wk0, b.wk_bf, "wk")
           + prep_cast_bf16(b, wv0, b.wv_bf, "wv"))
    wtC = prep_cast_bf16(b, wo0, b.wo_bf, "wo")
    p.set_floor()

    setup_carves(b)
    b.fill = Fill()
    b.pstage_w8 = b.xchunk.bitcast(BF16)[:, 0:2048]
    b.pstage_w4 = b.wchunk_bf[:, 0:2048]
    wt8, wq4t, wk4t, wv4t, wo4t = [], [], [], [], []
    b.fill.add(prep_w8_gen(b, b.wffn, b.w8_bf, wt8))
    b.fill.add(prep_w4_gen(b, wq1, b.wq4_bf, "wq4", wq4t))
    b.fill.add(prep_w4_gen(b, wk1, b.wk4_bf, "wk4", wk4t))
    b.fill.add(prep_w4_gen(b, wv1, b.wv4_bf, "wv4", wv4t))
    b.fill.add(prep_w4_gen(b, wo1, b.wo4_bf, "wo4", wo4t))
    attn0_launchB(b, xa, b.wq_bf, b.wk_bf, b.wv_bf, wtB)
    p.set_floor()
    cc1 = allgather(b.k_dram, kg, ())
    cc2 = allgather(b.vaug_dram, vg0, [cc1])
    p.wait("gpsimd", [cc2])
    p.set_floor()

    b.kt0_dram, b.kt1_dram = kg[0:TOK, :], kg[TOK:2 * TOK, :]
    b.vaug0_dram, b.vaug1_dram = vg0[0:TOK, :], vg0[TOK:2 * TOK, :]
    attn0_launchC(b, xa, xb, b.wo_bf, wtC)
    b.fill.drain()
    b.fill = None
    b.pstage_w8 = b.pstage_w4 = None

    p.set_floor()

    ffn0_sublayer(b, xb, xa, [])
    p.set_floor()

    z = p.c("vector", lambda em: em.memset(b.zero11[0:1, 0:1], 0.0), deps=())
    b.zero11_tok = [z]
    st = ln_stats(b, xa)
    am = ln_absmax(b, xa, st)
    dA = p.d("gpsimd", b.amax_loc, b.s11[0:1, 2:3], deps=am, sem="amx_st")
    cca = amax_allreduce([dA])
    l1ffn_main(b, xa, xb, b.amax_red, wt8 + [z], st=st, amax_w=1, amax_deps=[cca])
    p.set_floor()

    setup_carves(b)
    b.wq, b.wk, b.wv, b.wo = wq1, wk1, wv1, wo1
    attn1_launchE(b, xb, w4toks=(wq4t, wk4t, wv4t))
    p.set_floor()
    cc3 = allgather(b.k_dram, kg, ())
    cc4 = allgather(b.v_dram, vg1, [cc3])
    p.wait("gpsimd", [cc4])
    p.set_floor()

    setup_carves_f(b)
    b.kt0_dram, b.kt1_dram = kg[0:TOK, :], kg[TOK:2 * TOK, :]
    b.vt0_dram, b.vt1_dram = vg1[0:TOK, :], vg1[TOK:2 * TOK, :]
    attn1_launchF(b, xb, xa, wo4toks=wo4t)
    p.set_floor()

    st2 = ln_stats(b, xa)
    am2 = ln_absmax(b, xa, st2)
    dA2 = p.d("gpsimd", b.amax_loc, b.s11[0:1, 2:3], deps=am2, sem="amx_st")
    cca2 = amax_allreduce([dA2])
    l1ffn_main(b, xa, xout, b.amax_red, [], st=st2, amax_w=1, amax_deps=[cca2])

    p.wait("gpsimd", p.barrier_token())
    build_block(nc, b.prog, es)
    es.close()
    return nc


_UNFUSED = False

_nc_cache = {}


def _get_nc(name, builder):
    if name not in _nc_cache:
        _nc_cache[name] = builder()
    return _nc_cache[name]


def kernel(**inputs):
    x = np.asarray(inputs["L0_attn_x"], np.float32)
    had = np.asarray(inputs["L0_ffn_hadamard"], np.float32)
    wsc = np.asarray(inputs["L0_ffn_weight_scales"], np.float32)
    fb0 = np.asarray(inputs["L0_ffn_bias"], np.float32)
    wd = np.asarray(inputs["L0_ffn_weight_data"], np.int32)
    wq0 = np.asarray(inputs["L0_attn_wq"], np.float32)
    wk0 = np.asarray(inputs["L0_attn_wk"], np.float32)
    wv0 = np.asarray(inputs["L0_attn_wv"], np.float32)
    wo0 = np.asarray(inputs["L0_attn_wo"], np.float32)
    wq1 = np.asarray(inputs["L1_attn_wq"], np.float32)
    wk1 = np.asarray(inputs["L1_attn_wk"], np.float32)
    wv1 = np.asarray(inputs["L1_attn_wv"], np.float32)
    wo1 = np.asarray(inputs["L1_attn_wo"], np.float32)
    w1 = np.asarray(inputs["L1_ffn_weight"], np.float32)
    fb1 = np.asarray(inputs["L1_ffn_bias"], np.float32)

    n = 8
    xs = [np.ascontiguousarray(x[i // 2, 1024 * (i % 2):1024 * (i % 2) + 1024, :])
          for i in range(n)]
    cores = list(range(n))
    poffs = [np.array([[1024.0 * (i % 2)]], np.float32) for i in range(n)]
    g0s = [np.array([[1.0 if i % 2 == 0 else 0.0]], np.float32) for i in range(n)]
    g1s = [np.array([[0.0 if i % 2 == 0 else 1.0]], np.float32) for i in range(n)]

    def ffn0(xs):
        nc = _get_nc("A", build_launchA)
        im = [{"xin": xs[i], "had": had, "wdata": wd, "scales": wsc, "fbias": fb0}
              for i in range(n)]
        res = run_bass_kernel_spmd(nc, im, core_ids=cores)
        return [r["xout"] for r in res.results]

    def attn0(xs):
        ncB = _get_nc("B", build_launchB)
        imB = [{"xin": xs[i], "wq": wq0, "wk": wk0, "wv": wv0, "poff": poffs[i]}
               for i in range(n)]
        resB = run_bass_kernel_spmd(ncB, imB, core_ids=cores)
        ncC = _get_nc("C", build_launchC)
        imC = []
        for i in range(n):
            e, o = 2 * (i // 2), 2 * (i // 2) + 1
            imC.append({
                "xin": xs[i], "wo": wo0,
                "q_in": resB.results[i]["q_out"],
                "kt0": resB.results[e]["k_out"], "kt1": resB.results[o]["k_out"],
                "vaug0": resB.results[e]["v_out"], "vaug1": resB.results[o]["v_out"],
                "g0in": g0s[i], "g1in": g1s[i]})
        resC = run_bass_kernel_spmd(ncC, imC, core_ids=cores)
        return [r["xout"] for r in resC.results]

    def ffn1(xs):
        nc1 = _get_nc("D1", build_launchD1)
        res1 = run_bass_kernel_spmd(nc1, [{"xin": xs[i]} for i in range(n)], core_ids=cores)
        amax8 = np.zeros((1, 8), np.float32)
        for i in range(n):
            amax8[0, i] = res1.results[i]["amax_out"][0, 0]
        nc2 = _get_nc("D2", build_launchD2)
        im = [{"xin": xs[i], "wffn": w1, "fbias1": fb1, "amax8": amax8} for i in range(n)]
        res2 = run_bass_kernel_spmd(nc2, im, core_ids=cores)
        return [r["xout"] for r in res2.results]

    def attn1(xs):
        ncE = _get_nc("E", build_launchE)
        imE = [{"xin": xs[i], "wq": wq1, "wk": wk1, "wv": wv1, "poff": poffs[i]}
               for i in range(n)]
        resE = run_bass_kernel_spmd(ncE, imE, core_ids=cores)
        ncF = _get_nc("F", build_launchF)
        imF = []
        for i in range(n):
            e, o = 2 * (i // 2), 2 * (i // 2) + 1
            imF.append({
                "xin": xs[i], "wo": wo1,
                "q_in": resE.results[i]["q_out"],
                "kt0": resE.results[e]["k_out"], "kt1": resE.results[o]["k_out"],
                "vt0": resE.results[e]["v_out"], "vt1": resE.results[o]["v_out"],
                "g0in": g0s[i], "g1in": g1s[i]})
        resF = run_bass_kernel_spmd(ncF, imF, core_ids=cores)
        return [r["xout"] for r in resF.results]

    if _UNFUSED:
        xs = ffn0(xs)
        xs = attn0(xs)
        xs = ffn0(xs)
        xs = ffn1(xs)
        xs = attn1(xs)
        xs = ffn1(xs)
    else:
        nc = _get_nc("fused", build_fused)
        im = [{"xin": xs[i], "had": had, "wdata": wd, "scales": wsc, "fbias": fb0,
               "wq0": wq0, "wk0": wk0, "wv0": wv0, "wo0": wo0,
               "wq1_in": wq1, "wk1_in": wk1, "wv1_in": wv1, "wo1_in": wo1,
               "wffn": w1, "fbias1": fb1,
               "poff": poffs[i], "g0in": g0s[i], "g1in": g1s[i]} for i in range(n)]
        res = run_bass_kernel_spmd(nc, im, core_ids=cores)
        xs = [r["xout"] for r in res.results]

    out = np.empty((4, 2048, 4096), np.float32)
    for i in range(n):
        out[i // 2, 1024 * (i % 2):1024 * (i % 2) + 1024, :] = xs[i]
    return out
